# version 04
# baseline (speedup 1.0000x reference)
import numpy as np
import ml_dtypes
from contextlib import ExitStack

import concourse.bass as bass
import concourse.mybir as mybir
from concourse.bass_utils import run_bass_kernel_spmd

F32 = mybir.dt.float32
BF16 = mybir.dt.bfloat16
AF = mybir.ActivationFunctionType
ALU = mybir.AluOpType
AX = mybir.AxisListType

D = 1024
TL = 4096
TC = 512
NCH = TL // TC
DT = D // 128
HID = 4096
HT = HID // 128
NCORES = 8
LN_EPS = 1e-5

BF = ml_dtypes.bfloat16


class KB:
    ENGS = ("sync", "tensor", "scalar", "vector", "gpsimd")

    def __init__(self, nc, ctx):
        self.nc = nc
        self.ctx = ctx
        self.streams = {e: [] for e in self.ENGS}
        self.sems = {}
        self.cnt = {}
        self.guar = {e: {} for e in self.ENGS}
        self.pbtok = {}

    def sem(self, name):
        if name not in self.sems:
            self.sems[name] = self.ctx.enter_context(self.nc.semaphore(name))
            self.cnt[name] = 0
        return self.sems[name]

    def op(self, eng, fn, waits=(), inc=None, inc_n=1):
        need = []
        g = self.guar[eng]
        for w in waits:
            if w is None:
                continue
            s, v = w
            self.sem(s)
            if g.get(s, 0) < v:
                need.append((s, v))
                g[s] = v
        tok = None
        if inc is not None:
            self.sem(inc)
            self.cnt[inc] += inc_n
            tok = (inc, self.cnt[inc])
        sems = self.sems

        def closure(E, need=tuple(need), fn=fn, inc=inc, inc_n=inc_n):
            for (s, v) in need:
                E.wait_ge(sems[s], v)
            inst = fn(E)
            if inc is not None:
                inst.then_inc(sems[inc], inc_n)

        self.streams[eng].append(closure)
        return tok

    def tok(self, name):
        self.sem(name)
        return (name, self.cnt[name])

    def load(self, out, in_, waits=(), inc="ld"):
        return self.op("sync", lambda E: E.dma_start(out=out, in_=in_),
                       waits=waits, inc=inc, inc_n=16)

    def store(self, out, in_, waits=(), inc="st"):
        return self.op("gpsimd", lambda E: E.dma_start(out=out, in_=in_),
                       waits=waits, inc=inc, inc_n=16)

    def finalize(self, block):
        for e in self.ENGS:
            fns = self.streams[e]
            if not fns:
                continue
            def body(E, fns=fns):
                for f in fns:
                    f(E)
            getattr(block, e)(body)


_PREP_SEQ = [0]

def emit_prep(kb, ctx, nc, pairs, tok_start=None):
    maxc = max(src.shape[1] for src, _ in pairs)
    _PREP_SEQ[0] += 1
    q = _PREP_SEQ[0]
    with ExitStack() as s:
        b32 = [s.enter_context(nc.sbuf_tensor(f"prep32_{q}_{i}", [128, maxc], F32)) for i in range(2)]
        bbf = [s.enter_context(nc.sbuf_tensor(f"prepbf_{q}_{i}", [128, maxc], BF16)) for i in range(2)]
        slot_free = [tok_start, tok_start]
        slot_out = [tok_start, tok_start]
        it = 0
        for src, dst in pairs:
            R, C = src.shape[0], src.shape[1]
            for r0 in range(0, R, 128):
                sl = it % 2
                t_ld = kb.load(b32[sl][:, 0:C], src[r0:r0 + 128, :], waits=[slot_free[sl]])
                t_act = kb.op("scalar",
                              lambda E, sl=sl, C=C: E.copy(out=bbf[sl][:, 0:C], in_=b32[sl][:, 0:C]),
                              waits=[t_ld, slot_out[sl]], inc="act")
                slot_free[sl] = t_act
                t_st = kb.store(dst[r0:r0 + 128, :], bbf[sl][:, 0:C], waits=[t_act])
                slot_out[sl] = t_st
                it += 1
        return kb.tok("st")


def emit_ln(kb, B, rs, tok_resid, n_dt=DT, tcw=TC):
    xn = B["xn_ap"]
    st = B["stats"]
    pb = B["pb"]
    sq = lambda p: B["sq"][:, p * tcw:(p + 1) * tcw]
    t32 = lambda p: B["t32"][:, p * tcw:(p + 1) * tcw]

    w01 = [tok_resid, kb.pbtok.get(0), kb.pbtok.get(1)]
    for i in range(n_dt):
        kb.op("tensor", lambda E, i=i: E.matmul(pb[0][0:1, 0:tcw], B["ones_col_f"][:, 0:1], rs(i),
                                                start=(i == 0), stop=(i == n_dt - 1)), waits=w01)
    sq_free = [None, None]
    t_sums = None
    for i in range(n_dt):
        p = i % 2
        ta = kb.op("scalar", lambda E, i=i, p=p: E.activation(out=sq(p), in_=rs(i), func=AF.Square),
                   waits=[tok_resid, sq_free[p]], inc="act")
        t_sums = kb.op("tensor", lambda E, i=i, p=p: E.matmul(pb[1][0:1, 0:tcw],
                                                              B["ones_col_f"][:, 0:1], sq(p),
                                                              start=(i == 0), stop=(i == n_dt - 1)),
                       waits=[ta] + (w01 if i == 0 else []), inc="pe")
        sq_free[p] = t_sums
    t_pe = kb.tok("pe")

    negm = st[0:1, 0:tcw]
    varq = st[0:1, tcw:2 * tcw]
    var = st[0:1, 2 * tcw:3 * tcw]
    rstd = st[0:1, 3 * tcw:4 * tcw]
    t1 = kb.op("vector", lambda E: E.tensor_scalar(out=negm, in0=pb[0][0:1, 0:tcw],
                                                   scalar1=-1.0 / D, scalar2=None, op0=ALU.mult),
               waits=[t_pe, B.get("stats_free")], inc="dve")
    t2 = kb.op("vector", lambda E: E.tensor_scalar(out=varq, in0=pb[1][0:1, 0:tcw],
                                                   scalar1=1.0 / D, scalar2=None, op0=ALU.mult),
               waits=[t_pe], inc="dve")
    t3 = kb.op("vector", lambda E: E.tensor_tensor(out=var, in0=negm, in1=negm, op=ALU.mult),
               waits=[t1], inc="dve")
    t4 = kb.op("vector", lambda E: E.tensor_tensor(out=var, in0=varq, in1=var, op=ALU.subtract),
               waits=[t2, t3], inc="dve")
    t5 = kb.op("scalar", lambda E: E.activation(out=varq, in_=var, func=AF.Ln, bias=LN_EPS),
               waits=[t4], inc="act")
    t6 = kb.op("scalar", lambda E: E.activation(out=rstd, in_=varq, func=AF.Exp, scale=-0.5),
               waits=[t5], inc="act")
    kb.pbtok[0] = t1
    kb.pbtok[1] = t2

    w23 = [t6, kb.pbtok.get(2), kb.pbtok.get(3)]
    kb.op("tensor", lambda E: E.matmul(pb[2][:, 0:tcw], B["ones_row_f"][0:1, :], negm,
                                       start=True, stop=True), waits=w23)
    t_bc = kb.op("tensor", lambda E: E.matmul(pb[3][:, 0:tcw], B["ones_row_f"][0:1, :], rstd,
                                              start=True, stop=True), waits=w23, inc="pe")
    B["stats_free"] = t_bc

    t32_free = [None, None]
    t_xmm = [None] * n_dt
    for i in range(n_dt):
        p = i % 2
        tA = kb.op("vector", lambda E, i=i, p=p: E.tensor_tensor(out=t32(p), in0=rs(i),
                                                                 in1=pb[2][:, 0:tcw], op=ALU.add),
                   waits=[t_bc, t32_free[p]], inc="dve")
        tB = kb.op("vector", lambda E, i=i, p=p: E.tensor_tensor(out=xn(i), in0=t32(p),
                                                                 in1=pb[3][:, 0:tcw], op=ALU.mult),
                   waits=[tA, B.get("xn_free")], inc="dve")
        t32_free[p] = tB
        if B.get("cb_xmm") is not None:
            t32_free[p] = B["cb_xmm"](i, t32(p), [tA])
        t_xmm[i] = t32_free[p]
    t_xn = kb.tok("dve")
    kb.pbtok[2] = t_xn
    kb.pbtok[3] = t_xn
    return t_xn


def emit_mm_stage(kb, B, wdram, Kt, Mt, rhs_fn, evac_fn, tok_rhs, mgs=8, N=TC):
    pb = B["pb"]
    wbuf = B["wbuf"]
    WSLOTS = B["wslots"]
    WCOLS = B["wcols"]
    slot_tok = B["wslot_tok"]

    for mg in range((Mt + mgs - 1) // mgs):
        m0 = mg * mgs
        nm = min(mgs, Mt - m0)
        bank_waits = [kb.pbtok.get(j) for j in range(nm)]
        for k in range(Kt):
            sl = B["wslot_i"] % WSLOTS
            B["wslot_i"] += 1
            t_ld = kb.load(wbuf[:, sl * WCOLS: sl * WCOLS + nm * 128],
                           wdram[k * 128:(k + 1) * 128, m0 * 128:(m0 + nm) * 128],
                           waits=[slot_tok[sl]])
            mmw = [t_ld, tok_rhs] + (bank_waits if k == 0 else [])
            for j in range(nm):
                last = j == nm - 1
                kb.op("tensor",
                      lambda E, sl=sl, j=j, k=k: E.matmul(
                          pb[j][:, 0:N],
                          wbuf[:, sl * WCOLS + j * 128: sl * WCOLS + (j + 1) * 128],
                          rhs_fn(k), start=(k == 0), stop=(k == Kt - 1)),
                      waits=mmw, inc="pe" if last else None)
            slot_tok[sl] = kb.tok("pe")
        t_mm = kb.tok("pe")
        for j in range(nm):
            t_ev = evac_fn(m0 + j, pb[j], [t_mm])
            kb.pbtok[j] = t_ev


def emit_ffn0_chunk(kb, B, c, scr, resid_in, resid_out, tok_prev_store):
    rs = lambda i: B["resid"][:, i * TC:(i + 1) * TC]
    tok_resid = None
    for i in range(DT):
        tok_resid = kb.load(rs(i), resid_in[i * 128:(i + 1) * 128, c * TC:(c + 1) * TC],
                            waits=[tok_prev_store])
    t_xn = emit_ln(kb, B, rs, tok_resid)
    xn = B["xn_ap"]
    hA = lambda m: B["hA"][:, m * TC:(m + 1) * TC]
    hB = lambda m: B["hB"][:, m * TC:(m + 1) * TC]
    sil = lambda j: B["sil"][:, (j % 2) * TC:((j % 2) + 1) * TC]
    pb = B["pb"]
    wbuf = B["wbuf"]
    WSLOTS = B["wslots"]
    WCOLS = B["wcols"]
    slot_tok = B["wslot_tok"]

    for mg in range(HT // 4):
        m0 = mg * 4
        bank_waits = [kb.pbtok.get(j) for j in range(8)]
        for k in range(DT):
            sl = B["wslot_i"] % WSLOTS
            B["wslot_i"] += 1
            t_ld1 = kb.load(wbuf[:, sl * WCOLS: sl * WCOLS + 512],
                            scr["w1t"][k * 128:(k + 1) * 128, m0 * 128:(m0 + 4) * 128],
                            waits=[slot_tok[sl]])
            t_ld2 = kb.load(wbuf[:, sl * WCOLS + 512: sl * WCOLS + 1024],
                            scr["w2t"][k * 128:(k + 1) * 128, m0 * 128:(m0 + 4) * 128])
            mmw = [t_ld1, t_ld2, t_xn] + (bank_waits if k == 0 else [])
            for j in range(8):
                off = sl * WCOLS + (j % 4) * 128 + (512 if j >= 4 else 0)
                last = j == 7
                kb.op("tensor",
                      lambda E, off=off, k=k, j=j: E.matmul(
                          pb[j][:, 0:TC], wbuf[:, off:off + 128], xn(k),
                          start=(k == 0), stop=(k == DT - 1)),
                      waits=mmw, inc="pe" if last else None)
            slot_tok[sl] = kb.tok("pe")
        t_mm = kb.tok("pe")
        for j in range(4):
            m = m0 + j
            ta = kb.op("scalar",
                       lambda E, j=j, m=m: E.activation(out=sil(j), in_=pb[j][:, 0:TC],
                                                        func=AF.Silu, scale=B["s1"][:, m:m + 1]),
                       waits=[t_mm, B["sil_free"][j % 2]], inc="act")
            tv = kb.op("vector",
                       lambda E, j=j, m=m: E.scalar_tensor_tensor(
                           out=hA(m), in0=pb[4 + j][:, 0:TC], scalar=B["s2"][:, m:m + 1],
                           in1=sil(j), op0=ALU.mult, op1=ALU.mult),
                       waits=[t_mm, ta], inc="dve")
            B["sil_free"][j % 2] = tv
            kb.pbtok[j] = tv
            kb.pbtok[4 + j] = tv
    t_hA = kb.tok("dve")

    def ev_h1(m, bank, w):
        return kb.op("scalar",
                     lambda E, m=m, bank=bank: E.activation(out=hB(m), in_=bank[:, 0:TC],
                                                            func=AF.Gelu, scale=B["sh1"][:, m:m + 1]),
                     waits=w, inc="act")
    emit_mm_stage(kb, B, scr["h1t"], HT, HT, hA, ev_h1, t_hA)
    t_hB = kb.tok("act")

    def ev_h2(m, bank, w):
        return kb.op("scalar",
                     lambda E, m=m, bank=bank: E.activation(out=hA(m), in_=bank[:, 0:TC],
                                                            func=AF.Gelu, scale=B["sh2"][:, m:m + 1]),
                     waits=w, inc="act")
    emit_mm_stage(kb, B, scr["h2t"], HT, HT, hB, ev_h2, t_hB)
    t_hA2 = kb.tok("act")

    def ev_w3(m, bank, w):
        return kb.op("vector",
                     lambda E, m=m, bank=bank: E.scalar_tensor_tensor(
                         out=rs(m), in0=bank[:, 0:TC], scalar=B["s3h"][:, m:m + 1],
                         in1=rs(m), op0=ALU.mult, op1=ALU.add),
                     waits=w + [tok_resid], inc="dve")
    emit_mm_stage(kb, B, scr["w3t"], HT, DT, hA, ev_w3, t_hA2)
    t_done = kb.tok("dve")
    t_st = None
    for i in range(DT):
        t_st = kb.store(resid_out[i * 128:(i + 1) * 128, c * TC:(c + 1) * TC], rs(i),
                        waits=[t_done])
    return t_st


FCONSTS = [LN_EPS, -0.5, 1.0, -1.0, 0.5, 2.0, 256.0, -80.0, 0.0]


def alloc_common(kb, ctx, nc, B):
    sb = lambda n, s, d=F32: ctx.enter_context(nc.sbuf_tensor(n, s, d))
    B["resid"] = sb("resid_sb", [128, DT * TC])
    B["sq"] = sb("sq", [128, 2 * TC])
    B["t32"] = sb("t32", [128, 2 * TC])
    B["stats"] = sb("stats", [128, 4 * TC])
    B["ones_col_f"] = sb("ones_col_f", [128, 1])
    B["ones_row_f"] = sb("ones_row_f", [1, 128])
    B["pb"] = [ctx.enter_context(nc.psum_tensor(f"pb{i}", [128, 512], F32)) for i in range(B.get("npb", 8))]
    B["fconst"] = sb("fconst", [128, len(FCONSTS)])
    for j, v in enumerate(FCONSTS):
        kb.op("vector", lambda E, j=j, v=v: E.memset(B["fconst"][:, j:j + 1], v))
        nc.const_aps.register(F32, v, B["fconst"][:, j:j + 1])
    kb.op("vector", lambda E: E.memset(B["ones_col_f"][:], 1.0))
    kb.op("vector", lambda E: E.memset(B["ones_row_f"][:], 1.0), inc="dve")
    B["tok_const"] = kb.tok("dve")
    for e in ("sync", "tensor", "scalar", "gpsimd"):
        kb.op(e, lambda E: E.nop(), waits=[B["tok_const"]])
    B["wslot_i"] = 0
    B["sil_free"] = [None] * 2
    B["cb_xmm"] = None
    B["xn_free"] = None


def declare_ffn0_ins(nc):
    return {
        "w1t": nc.dram_tensor("w1t", [D, HID], F32, kind="ExternalInput").ap(),
        "w2t": nc.dram_tensor("w2t", [D, HID], F32, kind="ExternalInput").ap(),
        "h1t": nc.dram_tensor("h1t", [HID, HID], F32, kind="ExternalInput").ap(),
        "h2t": nc.dram_tensor("h2t", [HID, HID], F32, kind="ExternalInput").ap(),
        "w3t": nc.dram_tensor("w3t", [HID, D], F32, kind="ExternalInput").ap(),
        "s1": nc.dram_tensor("s1", [128, HT], F32, kind="ExternalInput").ap(),
        "s2": nc.dram_tensor("s2", [128, HT], F32, kind="ExternalInput").ap(),
        "sh1": nc.dram_tensor("sh1", [128, HT], F32, kind="ExternalInput").ap(),
        "sh2": nc.dram_tensor("sh2", [128, HT], F32, kind="ExternalInput").ap(),
        "s3": nc.dram_tensor("s3", [128, DT], F32, kind="ExternalInput").ap(),
    }


def declare_ffn0_scr(nc):
    return {
        "w1t": nc.dram_tensor("w1t_bf", [D, HID], BF16).ap(),
        "w2t": nc.dram_tensor("w2t_bf", [D, HID], BF16).ap(),
        "h1t": nc.dram_tensor("h1t_bf", [HID, HID], BF16).ap(),
        "h2t": nc.dram_tensor("h2t_bf", [HID, HID], BF16).ap(),
        "w3t": nc.dram_tensor("w3t_bf", [HID, D], BF16).ap(),
    }


def emit_ffn0_all(kb, ctx, nc, B0, A, scr, resid_in, resid_out, do_prep, tag, tok_start):
    with ExitStack() as s:
        al = lambda n, sh, d=F32: s.enter_context(nc.sbuf_tensor(n + tag, sh, d))
        pb7 = s.enter_context(nc.psum_tensor("pb7" + tag, [128, 512], F32))
        B = dict(B0)
        B["pb"] = list(B0["pb"]) + [pb7]
        B["s1"] = al("s1t", [128, HT]); B["s2"] = al("s2t", [128, HT])
        B["sh1"] = al("sh1t", [128, HT]); B["sh2"] = al("sh2t", [128, HT])
        B["s3h"] = al("s3h", [128, DT])
        s3t = al("s3t", [128, DT])

        kb.load(B["s1"][:], A["s1"], waits=[tok_start])
        kb.load(B["s2"][:], A["s2"], waits=[tok_start])
        kb.load(B["sh1"][:], A["sh1"], waits=[tok_start])
        kb.load(B["sh2"][:], A["sh2"], waits=[tok_start])
        t_s3 = kb.load(s3t[:], A["s3"], waits=[tok_start])
        kb.op("vector", lambda E, s3t=s3t, B=B: E.tensor_scalar(
            out=B["s3h"][:], in0=s3t[:], scalar1=0.5, scalar2=None, op0=ALU.mult),
            waits=[t_s3], inc="dve")

        if do_prep:
            emit_prep(kb, ctx, nc, [(A["w1t"], scr["w1t"]), (A["w2t"], scr["w2t"]),
                                    (A["h1t"], scr["h1t"]), (A["h2t"], scr["h2t"]),
                                    (A["w3t"], scr["w3t"])], tok_start=tok_start)
        t_prep = kb.tok("st")

        B["hA"] = al("hA", [128, HT * TC], BF16)
        B["hB"] = al("hB", [128, HT * TC], BF16)
        B["sil"] = al("sil", [128, 2 * TC], BF16)
        B["wbuf"] = al("wbuf", [128, 6 * 1024], BF16)
        B["wslots"] = 6
        B["wcols"] = 1024
        B["wslot_tok"] = [t_prep] * 6
        B["wslot_i"] = 0
        B["sil_free"] = [None] * 2
        B["xn_free"] = None
        B["cb_xmm"] = None
        B["xn_ap"] = lambda i, B=B: B["hB"][:, i * TC:(i + 1) * TC]

        t_st = tok_start
        for c in range(NCH):
            t_st = emit_ffn0_chunk(kb, B, c, scr, resid_in, resid_out, t_st)
        B0["stats_free"] = B.get("stats_free")
        return t_st


def build_ffn0():
    nc = bass.Bass("TRN2", target_bir_lowering=False)
    resid_in = nc.dram_tensor("resid", [D, TL], F32, kind="ExternalInput").ap()
    A = declare_ffn0_ins(nc)
    resid_out = nc.dram_tensor("resid_out", [D, TL], F32, kind="ExternalOutput").ap()
    scr = declare_ffn0_scr(nc)
    with ExitStack() as ctx:
        kb = KB(nc, ctx)
        B0 = {"nc": nc, "npb": 7}
        alloc_common(kb, ctx, nc, B0)
        t_st = emit_ffn0_all(kb, ctx, nc, B0, A, scr, resid_in, resid_out, True, "_f0", None)
        kb.op("gpsimd", lambda E: E.engine_nop(), waits=[t_st])
        block = ctx.enter_context(nc.Block())
        kb.finalize(block)
    return nc


NEXP = 8
import os
def NCH_MOE():
    return int(os.environ.get("NCH_MOE", NCH))

def emit_moe_chunk(kb, B, c, scr, resid_in, resid_out, tok_prev_store):
    rs = lambda i: B["resid"][:, i * TC:(i + 1) * TC]
    tok_resid = None
    for i in range(DT):
        tok_resid = kb.load(rs(i), resid_in[i * 128:(i + 1) * 128, c * TC:(c + 1) * TC],
                            waits=[tok_prev_store])
    xn32 = lambda i: B["xn32"][:, i * TC:(i + 1) * TC]
    xn = B["xn_ap"]

    def cb(i, t32ap, w):
        tA = kb.op("vector", lambda E, i=i, t32ap=t32ap: E.tensor_tensor(
            out=xn32(i), in0=t32ap, in1=B["pb"][3][:, 0:TC], op=ALU.mult),
            waits=w + [B.get("xn32_free")], inc="dve")
        kb.op("scalar", lambda E, i=i: E.copy(out=xn(i), in_=xn32(i)), waits=[tA], inc="act")
        return tA
    B["cb_xmm"] = cb
    emit_ln(kb, B, rs, tok_resid)
    B["cb_xmm"] = None
    t_xn32 = kb.tok("dve")
    t_xn = kb.tok("act")

    pb = B["pb"]
    lgs = lambda s: B["lg"][:, s * 8:(s + 1) * 8]
    t_lg = []
    for s in range(4):
        w = [t_xn32, kb.pbtok.get(6)]
        for k in range(DT):
            kb.op("tensor", lambda E, s=s, k=k: E.matmul(
                pb[6][:, 0:8], xn32(k)[:, s * 128:(s + 1) * 128],
                B["rwk"][:, k * NEXP:(k + 1) * NEXP],
                start=(k == 0), stop=(k == DT - 1)),
                waits=w, inc="pe" if k == DT - 1 else None)
        t_pe = kb.tok("pe")
        tl = kb.op("vector", lambda E, s=s: E.tensor_tensor(out=lgs(s), in0=pb[6][:, 0:8],
                                                            in1=B["rb_bc"][:], op=ALU.add),
                   waits=[t_pe], inc="dve")
        kb.pbtok[6] = tl
        t_lg.append(tl)
    B["xn32_free"] = kb.tok("pe")

    gt = lambda s: B["gt"][:, s * 8:(s + 1) * 8]
    m1 = B["m1"]; m2 = B["m2"]; msk1 = B["msk1"]; msk2 = B["msk2"]; g2c = B["g2c"]
    t_gates = []
    for s in range(4):
        t1 = kb.op("vector", lambda E, s=s: E.reduce_max(out=m1[:], in_=lgs(s), axis=AX.X),
                   waits=[t_lg[s], B.get("topk_free")], inc="dve")
        t2 = kb.op("vector", lambda E, s=s: E.tensor_scalar(out=msk1[:], in0=lgs(s), scalar1=m1[:, 0:1],
                                                            scalar2=None, op0=ALU.is_equal),
                   waits=[t1], inc="dve")
        t3 = kb.op("vector", lambda E, s=s: E.tensor_scalar(out=msk2[:], in0=msk1[:], scalar1=-1e30,
                                                            scalar2=None, op0=ALU.mult),
                   waits=[t2], inc="dve")
        t4 = kb.op("vector", lambda E, s=s: E.tensor_tensor(out=msk2[:], in0=msk2[:], in1=lgs(s),
                                                            op=ALU.add),
                   waits=[t3], inc="dve")
        t5 = kb.op("vector", lambda E, s=s: E.reduce_max(out=m2[:], in_=msk2[:], axis=AX.X),
                   waits=[t4], inc="dve")
        t6 = kb.op("vector", lambda E, s=s: E.tensor_scalar(out=msk2[:], in0=msk2[:], scalar1=m2[:, 0:1],
                                                            scalar2=None, op0=ALU.is_equal),
                   waits=[t5], inc="dve")
        t7 = kb.op("vector", lambda E: E.tensor_tensor(out=m2[:], in0=m2[:], in1=m1[:], op=ALU.subtract),
                   waits=[t6], inc="dve")
        t8 = kb.op("scalar", lambda E: E.activation(out=g2c[:], in_=m2[:], func=AF.Sigmoid),
                   waits=[t7], inc="act")
        t9 = kb.op("vector", lambda E: E.tensor_scalar(out=m1[:], in0=g2c[:], scalar1=-1.0,
                                                       scalar2=1.0, op0=ALU.mult, op1=ALU.add),
                   waits=[t8], inc="dve")
        tA = kb.op("vector", lambda E, s=s: E.tensor_scalar(out=gt(s), in0=msk1[:], scalar1=m1[:, 0:1],
                                                            scalar2=None, op0=ALU.mult),
                   waits=[t9], inc="dve")
        tB = kb.op("vector", lambda E, s=s: E.scalar_tensor_tensor(out=gt(s), in0=msk2[:],
                                                                   scalar=g2c[:, 0:1], in1=gt(s),
                                                                   op0=ALU.mult, op1=ALU.add),
                   waits=[tA], inc="dve")
        B["topk_free"] = tB
        t_gates.append(tB)

    if c == 0 and B.get("dbg_lg") is not None:
        kb.store(B["dbg_lg"], B["lg"][:], waits=t_gates)
    w = [kb.pbtok.get("gt")] + t_gates
    for s in range(4):
        kb.op("tensor", lambda E, s=s: E.transpose(B["pbT"][0:8, s * 128:(s + 1) * 128],
                                                   gt(s), B["ident128"][:]),
              waits=w, inc="pe" if s == 3 else None)
    t_tp = kb.tok("pe")
    t_gT = kb.op("vector", lambda E: E.tensor_copy(out=B["gatesT"][:], in_=B["pbT"][0:8, 0:512]),
                 waits=[t_tp], inc="dve")
    kb.pbtok["gt"] = t_gT
    gbc = lambda e: B["gbc"][:, e * TC:(e + 1) * TC]
    for e in range(NEXP):
        bank = e % 4
        tm = kb.op("tensor", lambda E, e=e, bank=bank: E.matmul(
            pb[bank][:, 0:TC], B["onehot8"][0:8, e * 128:(e + 1) * 128], B["gatesT"][:],
            start=True, stop=True),
            waits=[t_gT, kb.pbtok.get(bank)], inc="pe")
        te = kb.op("vector", lambda E, e=e, bank=bank: E.tensor_scalar(
            out=gbc(e), in0=pb[bank][:, 0:TC], scalar1=0.5, scalar2=None, op0=ALU.mult),
            waits=[tm], inc="dve")
        kb.pbtok[bank] = te

    h = lambda m: B["hA"][:, m * TC:(m + 1) * TC]
    t_res_upd = None
    for e in range(NEXP):
        def ev_w1(m, bank, w, e=e):
            return kb.op("scalar", lambda E, m=m, bank=bank: E.activation(
                out=h(m), in_=bank[:, 0:TC], func=AF.Gelu, bias=B["b1c"][:, e * HT + m: e * HT + m + 1]),
                waits=w + [B["h_free"][m]], inc="act")
        emit_mm_stage(kb, B, scr["w1"][e * D:(e + 1) * D, :], DT, HT, xn, ev_w1, t_xn, mgs=4)
        t_h = kb.tok("act")
        def ev_w2(m, bank, w, e=e):
            p = m % 2
            tA = kb.op("vector", lambda E, m=m, bank=bank, e=e, p=p: E.scalar_tensor_tensor(
                out=B["t32"][:, p * TC:(p + 1) * TC], in0=bank[:, 0:TC],
                scalar=B["b2c"][:, e * DT + m: e * DT + m + 1], in1=gbc(e),
                op0=ALU.add, op1=ALU.mult),
                waits=w + [B["t32m_free"][p]], inc="dve")
            tB = kb.op("vector", lambda E, m=m, p=p: E.tensor_tensor(
                out=rs(m), in0=B["t32"][:, p * TC:(p + 1) * TC], in1=rs(m), op=ALU.add),
                waits=[tA, tok_resid], inc="dve")
            B["t32m_free"][p] = tB
            return tB
        emit_mm_stage(kb, B, scr["w2"][e * HID:(e + 1) * HID, :], HT, DT, h, ev_w2, t_h, mgs=4)
        t_w2mm = kb.tok("pe")
        for m in range(HT):
            B["h_free"][m] = t_w2mm
        t_res_upd = kb.tok("dve")

    t_st = None
    for i in range(DT):
        t_st = kb.store(resid_out[i * 128:(i + 1) * 128, c * TC:(c + 1) * TC], rs(i),
                        waits=[t_res_upd])
    return t_st


def declare_moe_ins(nc):
    return {
        "w1": nc.dram_tensor("w1", [NEXP * D, HID], F32, kind="ExternalInput").ap(),
        "w2": nc.dram_tensor("w2", [NEXP * HID, D], F32, kind="ExternalInput").ap(),
        "rw": nc.dram_tensor("rw", [D, NEXP], F32, kind="ExternalInput").ap(),
        "rb_bc": nc.dram_tensor("rb_bc", [128, NEXP], F32, kind="ExternalInput").ap(),
        "b1c": nc.dram_tensor("b1c", [128, NEXP * HT], F32, kind="ExternalInput").ap(),
        "b2c": nc.dram_tensor("b2c", [128, NEXP * DT], F32, kind="ExternalInput").ap(),
        "ident128": nc.dram_tensor("ident128", [128, 128], F32, kind="ExternalInput").ap(),
        "onehot8": nc.dram_tensor("onehot8", [8, NEXP * 128], F32, kind="ExternalInput").ap(),
    }


def declare_moe_scr(nc):
    return {
        "w1": nc.dram_tensor("w1_bf", [NEXP * D, HID], BF16).ap(),
        "w2": nc.dram_tensor("w2_bf", [NEXP * HID, D], BF16).ap(),
    }


def emit_moe_all(kb, ctx, nc, B0, A, scr, resid_in, resid_out, do_prep, tag, tok_start):
    with ExitStack() as s:
        al = lambda n, sh, d=F32: s.enter_context(nc.sbuf_tensor(n + tag, sh, d))
        B = dict(B0)
        B["pbT"] = s.enter_context(nc.psum_tensor("pbT" + tag, [8, 512], F32))
        B["rb_bc"] = al("rbbc", [128, NEXP])
        B["b1c"] = al("b1ct", [128, NEXP * HT])
        B["b2c"] = al("b2ct", [128, NEXP * DT])
        B["ident128"] = al("ident128t", [128, 128])
        B["onehot8"] = al("onehot8t", [8, NEXP * 128])
        B["rwk"] = al("rwk", [128, DT * NEXP])

        kb.load(B["rb_bc"][:], A["rb_bc"], waits=[tok_start])
        kb.load(B["b1c"][:], A["b1c"], waits=[tok_start])
        kb.load(B["b2c"][:], A["b2c"], waits=[tok_start])
        kb.load(B["ident128"][:], A["ident128"], waits=[tok_start])
        kb.load(B["onehot8"][:], A["onehot8"], waits=[tok_start])
        for k in range(DT):
            kb.load(B["rwk"][:, k * NEXP:(k + 1) * NEXP], A["rw"][k * 128:(k + 1) * 128, :],
                    waits=[tok_start])

        if do_prep:
            emit_prep(kb, ctx, nc, [(A["w1"], scr["w1"]), (A["w2"], scr["w2"])], tok_start=tok_start)
        t_prep = kb.tok("st")

        B["hA"] = al("hA", [128, HT * TC], BF16)
        B["xn"] = al("xn", [128, DT * TC], BF16)
        B["xn32"] = al("xn32", [128, DT * TC])
        B["wbuf"] = al("wbuf", [128, 6 * 512], BF16)
        B["wslots"] = 6
        B["wcols"] = 512
        B["wslot_tok"] = [t_prep] * 6
        B["wslot_i"] = 0
        B["xn_ap"] = lambda i, B=B: B["xn"][:, i * TC:(i + 1) * TC]
        B["lg"] = al("lg", [128, 4 * 8])
        B["gt"] = al("gt", [128, 4 * 8])
        B["m1"] = al("m1", [128, 1]); B["m2"] = al("m2", [128, 1])
        B["msk1"] = al("msk1", [128, 8]); B["msk2"] = al("msk2", [128, 8])
        B["g2c"] = al("g2c", [128, 1])
        B["gatesT"] = al("gatesT", [8, 512])
        B["gbc"] = al("gbc", [128, NEXP * TC])
        B["h_free"] = [None] * HT
        B["t32m_free"] = [None, None]
        B["xn32_free"] = None
        B["topk_free"] = None
        B["xn_free"] = None

        t_st = tok_start
        for c in range(NCH_MOE()):
            t_st = emit_moe_chunk(kb, B, c, scr, resid_in, resid_out, t_st)
        B0["stats_free"] = B.get("stats_free")
        return t_st


def build_moe():
    nc = bass.Bass("TRN2", target_bir_lowering=False)
    resid_in = nc.dram_tensor("resid", [D, TL], F32, kind="ExternalInput").ap()
    A = declare_moe_ins(nc)
    resid_out = nc.dram_tensor("resid_out", [D, TL], F32, kind="ExternalOutput").ap()
    scr = declare_moe_scr(nc)
    with ExitStack() as ctx:
        kb = KB(nc, ctx)
        B0 = {"nc": nc, "npb": 7}
        alloc_common(kb, ctx, nc, B0)
        t_st = emit_moe_all(kb, ctx, nc, B0, A, scr, resid_in, resid_out, True, "_m0", None)
        kb.op("gpsimd", lambda E: E.engine_nop(), waits=[t_st])
        block = ctx.enter_context(nc.Block())
        kb.finalize(block)
    return nc


LA = 256
NCA = TL // LA

def emit_attn0_prelude(kb, B):
    pb = B["pb"]
    t0 = kb.op("scalar", lambda E: E.activation(out=B["dsc"][0:1, 0:1], in_=B["dlg"][0:1, 0:1],
                                                func=AF.Sigmoid), waits=[B["tok_in"]], inc="act")
    t1 = kb.op("scalar", lambda E: E.activation(out=B["dsc"][0:1, 1:2], in_=B["dsc"][0:1, 0:1],
                                                func=AF.Ln), waits=[t0], inc="act")
    t2 = kb.op("vector", lambda E: E.tensor_scalar(out=B["dsc"][0:1, 2:3], in0=B["dsc"][0:1, 1:2],
                                                   scalar1=-80.0, scalar2=None, op0=ALU.max),
               waits=[t1], inc="dve")
    lnd = B["dsc"][0:1, 2:3]
    t3 = kb.op("tensor", lambda E: E.matmul(pb[0][:, 0:1], B["ones_row_f"][0:1, :], lnd,
                                            start=True, stop=True), waits=[t2, kb.pbtok.get(0)])
    t3 = kb.op("tensor", lambda E: E.matmul(pb[0][:, 1:2], B["ones_row_f"][0:1, :],
                                            B["osc"][0:1, 0:1], start=True, stop=True),
               waits=[B["tok_in"]], inc="pe")
    t4 = kb.op("vector", lambda E: E.tensor_copy(out=B["lnd_b"][:], in_=pb[0][:, 0:1]),
               waits=[t3], inc="dve")
    t5 = kb.op("vector", lambda E: E.tensor_copy(out=B["os_b"][:], in_=pb[0][:, 1:2]),
               waits=[t3], inc="dve")
    kb.pbtok[0] = t5

    t6 = kb.op("scalar", lambda E: E.activation(out=B["dL_b"][:], in_=B["lnd_b"][:],
                                                func=AF.Exp, scale=256.0), waits=[t4], inc="act")
    t7 = kb.op("scalar", lambda E: E.activation(out=B["dLpows_b"][:], in_=B["jE"][:],
                                                func=AF.Exp, scale=B["lnd_b"][:, 0:1]),
               waits=[t4, B["tok_in"]], inc="act")
    t8 = kb.op("scalar", lambda E: E.activation(out=B["dpows_b"][:], in_=B["colE"][:],
                                                func=AF.Exp, scale=B["lnd_b"][:, 0:1]),
               waits=[t4, B["tok_in"]], inc="act")
    tm = []
    for j in range(2):
        ta = kb.op("scalar", lambda E, j=j: E.activation(
            out=B["maskT"][:, j * LA:(j + 1) * LA], in_=B["ET"][:, j * LA:(j + 1) * LA],
            func=AF.Exp, scale=B["lnd_b"][:, 0:1]), waits=[t4, B["tok_in"]], inc="act")
        tb = kb.op("vector", lambda E, j=j: E.tensor_tensor(
            out=B["maskT"][:, j * LA:(j + 1) * LA], in0=B["maskT"][:, j * LA:(j + 1) * LA],
            in1=B["C01"][:, j * LA:(j + 1) * LA], op=ALU.mult), waits=[ta], inc="dve")
        tm.append(tb)
    t9 = kb.op("scalar", lambda E: E.activation(out=B["dqrow"][0:1, :], in_=B["rowq"][0:1, :],
                                                func=AF.Exp, scale=lnd), waits=[t2], inc="act")
    ta = kb.op("tensor", lambda E: E.matmul(pb[1][:, 0:LA], B["ones_row_f"][0:1, :],
                                            B["dqrow"][0:1, :], start=True, stop=True),
               waits=[t9, kb.pbtok.get(1)], inc="pe")
    tb = kb.op("vector", lambda E: E.tensor_copy(out=B["dqrow_b"][:], in_=pb[1][:, 0:LA]),
               waits=[ta], inc="dve")
    kb.pbtok[1] = tb

    for name, coefT in (("qw", "qcT"), ("kw", "kcT"), ("vw", "vcT")):
        for m in range(DT):
            bank = m % 2
            tmm = kb.op("tensor", lambda E, m=m, coefT=coefT, bank=bank: E.matmul(
                pb[2 + bank][:, 0:256], B["basisT"][:, m * 128:(m + 1) * 128], B[coefT][:],
                start=True, stop=True), waits=[B["tok_in"], kb.pbtok.get(2 + bank)], inc="pe")
            tev = kb.op("vector", lambda E, m=m, name=name, bank=bank: E.tensor_copy(
                out=B[name][:, m * 256:(m + 1) * 256], in_=pb[2 + bank][:, 0:256]),
                waits=[tmm], inc="dve")
            kb.pbtok[2 + bank] = tev
    for m2 in range(2):
        for h in range(2):
            bank = 4 + (m2 * 2 + h) % 2
            tmm = kb.op("tensor", lambda E, m2=m2, h=h, bank=bank: E.matmul(
                pb[bank][:, 0:512], B["ocT"][:, m2 * 128:(m2 + 1) * 128],
                B["basisT"][:, h * 512:(h + 1) * 512],
                start=True, stop=True), waits=[B["tok_in"], t5, kb.pbtok.get(bank)], inc="pe")
            tev = kb.op("vector", lambda E, m2=m2, h=h, bank=bank: E.tensor_scalar(
                out=B["owT"][:, m2 * 1024 + h * 512: m2 * 1024 + (h + 1) * 512],
                in0=pb[bank][:, 0:512], scalar1=B["os_b"][:, 0:1], scalar2=None, op0=ALU.mult),
                waits=[tmm], inc="dve")
            kb.pbtok[bank] = tev
    kb.op("vector", lambda E: E.memset(B["S_after"][:], 0.0), inc="dve")
    B["tok_pre"] = kb.tok("dve")


def emit_attn0_chunk(kb, B, c, resid_in, resid_out, tok_prev):
    pb = B["pb"]
    rs = lambda i: B["resid"][:, i * LA:(i + 1) * LA]
    tok_resid = None
    for i in range(DT):
        tok_resid = kb.load(rs(i), resid_in[i * 128:(i + 1) * 128, c * LA:(c + 1) * LA],
                            waits=[tok_prev, B["tok_pre"]])
    t_xn = emit_ln(kb, B, rs, tok_resid, tcw=LA)
    xn = B["xn_ap"]

    tSbf = kb.op("scalar", lambda E: E.copy(out=B["S_bf"][:], in_=B["S_after"][:]),
                 waits=[B["tok_S"]], inc="act")

    for m2 in range(2):
        w = [t_xn, kb.pbtok.get(m2)]
        for k in range(DT):
            kb.op("tensor", lambda E, m2=m2, k=k: E.matmul(
                pb[m2][:, 0:LA], B["qw"][:, k * 256 + m2 * 128: k * 256 + (m2 + 1) * 128], xn(k),
                start=(k == 0), stop=(k == DT - 1)),
                waits=w, inc="pe" if k == DT - 1 else None)
        t_mm = kb.tok("pe")
        t1 = kb.op("vector", lambda E, m2=m2: E.tensor_copy(
            out=B["q_bf"][:, m2 * LA:(m2 + 1) * LA], in_=pb[m2][:, 0:LA]), waits=[t_mm], inc="dve")
        t2 = kb.op("vector", lambda E, m2=m2: E.tensor_tensor(
            out=B["qt_bf"][:, m2 * LA:(m2 + 1) * LA], in0=pb[m2][:, 0:LA], in1=B["dqrow_b"][:],
            op=ALU.mult), waits=[t_mm], inc="dve")
        t3 = kb.op("vector", lambda E, m2=m2, c=c: E.scalar_tensor_tensor(
            out=B["qd"][:, m2 * TL + c * LA: m2 * TL + (c + 1) * LA], in0=pb[m2][:, 0:LA],
            scalar=B["dLpows_b"][:, (NCA - 1 - c):(NCA - c)], in1=B["dqrow_b"][:],
            op0=ALU.mult, op1=ALU.mult), waits=[t_mm], inc="dve")
        kb.pbtok[m2] = t3
    for m2 in range(2):
        w = [t_xn, kb.pbtok.get(2 + m2)]
        for k in range(DT):
            kb.op("tensor", lambda E, m2=m2, k=k: E.matmul(
                pb[2 + m2][:, 0:LA], B["kw"][:, k * 256 + m2 * 128: k * 256 + (m2 + 1) * 128], xn(k),
                start=(k == 0), stop=(k == DT - 1)),
                waits=w, inc="pe" if k == DT - 1 else None)
        t_mm = kb.tok("pe")
        t1 = kb.op("vector", lambda E, m2=m2: E.tensor_copy(
            out=B["k_bf"][:, m2 * LA:(m2 + 1) * LA], in_=pb[2 + m2][:, 0:LA]), waits=[t_mm], inc="dve")
        kb.pbtok[2 + m2] = t1

    for s2 in range(2):
        w = [t_xn, kb.pbtok.get(4 + s2)]
        for k in range(DT):
            kb.op("tensor", lambda E, s2=s2, k=k: E.matmul(
                pb[4 + s2][:, 0:256], xn(k)[:, s2 * 128:(s2 + 1) * 128], B["vw"][:, k * 256:(k + 1) * 256],
                start=(k == 0), stop=(k == DT - 1)),
                waits=w, inc="pe" if k == DT - 1 else None)
        t_mm = kb.tok("pe")
        t1 = kb.op("vector", lambda E, s2=s2: E.tensor_copy(
            out=B["vT_bf"][:, s2 * 256:(s2 + 1) * 256], in_=pb[4 + s2][:, 0:256]),
            waits=[t_mm], inc="dve")
        kb.pbtok[4 + s2] = t1
    for s2 in range(2):
        w = [t_xn, kb.pbtok.get(6 + s2)]
        for k in range(DT):
            kb.op("tensor", lambda E, s2=s2, k=k: E.matmul(
                pb[6 + s2][:, 0:256], xn(k)[:, s2 * 128:(s2 + 1) * 128], B["kw"][:, k * 256:(k + 1) * 256],
                start=(k == 0), stop=(k == DT - 1)),
                waits=w, inc="pe" if k == DT - 1 else None)
        t_mm = kb.tok("pe")
        t1 = kb.op("vector", lambda E, s2=s2: E.tensor_scalar(
            out=B["kT_bf"][:, s2 * 256:(s2 + 1) * 256], in0=pb[6 + s2][:, 0:256],
            scalar1=B["dpows_b"][:, s2:s2 + 1], scalar2=None, op0=ALU.mult),
            waits=[t_mm], inc="dve")
        kb.pbtok[6 + s2] = t1
    t_kv = kb.tok("dve")

    for s2 in range(2):
        w = [t_kv, kb.pbtok.get(s2)]
        for k2 in range(2):
            kb.op("tensor", lambda E, s2=s2, k2=k2: E.matmul(
                pb[s2][:, 0:LA], B["k_bf"][:, k2 * LA + s2 * 128: k2 * LA + (s2 + 1) * 128],
                B["q_bf"][:, k2 * LA:(k2 + 1) * LA],
                start=(k2 == 0), stop=(k2 == 1)),
                waits=w, inc="pe" if k2 == 1 else None)
        t_mm = kb.tok("pe")
        t1 = kb.op("vector", lambda E, s2=s2: E.tensor_tensor(
            out=B["sc_bf"][:, s2 * LA:(s2 + 1) * LA], in0=pb[s2][:, 0:LA],
            in1=B["maskT"][:, s2 * LA:(s2 + 1) * LA], op=ALU.mult), waits=[t_mm], inc="dve")
        kb.pbtok[s2] = t1
    t_sc = kb.tok("dve")

    for m2 in range(2):
        w = [t_sc, tSbf, kb.pbtok.get(2 + m2)]
        for s2 in range(2):
            kb.op("tensor", lambda E, m2=m2, s2=s2: E.matmul(
                pb[2 + m2][:, 0:LA], B["vT_bf"][:, s2 * 256 + m2 * 128: s2 * 256 + (m2 + 1) * 128],
                B["sc_bf"][:, s2 * LA:(s2 + 1) * LA],
                start=(s2 == 0), stop=False), waits=w)
        for k2 in range(2):
            kb.op("tensor", lambda E, m2=m2, k2=k2: E.matmul(
                pb[2 + m2][:, 0:LA], B["S_bf"][:, k2 * 256 + m2 * 128: k2 * 256 + (m2 + 1) * 128],
                B["qt_bf"][:, k2 * LA:(k2 + 1) * LA],
                start=False, stop=(k2 == 1)),
                waits=w, inc="pe" if k2 == 1 else None)
        t_mm = kb.tok("pe")
        t1 = kb.op("vector", lambda E, m2=m2: E.tensor_copy(
            out=B["y_bf"][:, m2 * LA:(m2 + 1) * LA], in_=pb[2 + m2][:, 0:LA]), waits=[t_mm], inc="dve")
        kb.pbtok[2 + m2] = t1
    t_y = kb.tok("dve")

    for m2 in range(2):
        w = [t_kv, kb.pbtok.get(4 + m2), tSbf]
        for s2 in range(2):
            kb.op("tensor", lambda E, m2=m2, s2=s2: E.matmul(
                pb[4 + m2][:, 0:256], B["kT_bf"][:, s2 * 256 + m2 * 128: s2 * 256 + (m2 + 1) * 128],
                B["vT_bf"][:, s2 * 256:(s2 + 1) * 256],
                start=(s2 == 0), stop=(s2 == 1)),
                waits=w, inc="pe" if s2 == 1 else None)
        t_mm = kb.tok("pe")
        tu = kb.op("vector", lambda E, m2=m2: E.scalar_tensor_tensor(
            out=B["S_after"][:, m2 * 256:(m2 + 1) * 256], in0=B["S_after"][:, m2 * 256:(m2 + 1) * 256],
            scalar=B["dL_b"][:, 0:1], in1=pb[4 + m2][:, 0:256], op0=ALU.mult, op1=ALU.add),
            waits=[t_mm, tSbf], inc="dve")
        kb.pbtok[4 + m2] = tu
    B["tok_S"] = kb.tok("dve")

    for m in range(DT):
        bank = 4 + m % 4
        half = (m // 4) * 256
        w = [t_y, kb.pbtok.get(bank)]
        for k2 in range(2):
            kb.op("tensor", lambda E, m=m, k2=k2, bank=bank, half=half: E.matmul(
                pb[bank][:, half:half + LA],
                B["owT"][:, k2 * 1024 + m * 128: k2 * 1024 + (m + 1) * 128],
                B["y_bf"][:, k2 * LA:(k2 + 1) * LA],
                start=(k2 == 0), stop=(k2 == 1)),
                waits=w, inc="pe" if k2 == 1 else None)
        t_mm = kb.tok("pe")
        tr = kb.op("vector", lambda E, m=m, bank=bank, half=half: E.tensor_tensor(
            out=rs(m), in0=pb[bank][:, half:half + LA], in1=rs(m), op=ALU.add),
            waits=[t_mm, tok_resid], inc="dve")
        kb.pbtok[bank] = tr
    t_done = kb.tok("dve")
    t_st = None
    for i in range(DT):
        t_st = kb.store(resid_out[i * 128:(i + 1) * 128, c * LA:(c + 1) * LA], rs(i),
                        waits=[t_done])
    return t_st


def emit_attn0_corr(kb, B, c, resid_io, tok_prev, t_geff):
    pb = B["pb"]
    rs = lambda i: B["resid"][:, i * LA:(i + 1) * LA]
    tok_resid = None
    for i in range(DT):
        tok_resid = kb.load(rs(i), resid_io[i * 128:(i + 1) * 128, c * LA:(c + 1) * LA],
                            waits=[tok_prev, t_geff])
    for m2 in range(2):
        w = [t_geff, kb.pbtok.get(m2)]
        for k2 in range(2):
            kb.op("tensor", lambda E, m2=m2, k2=k2, c=c: E.matmul(
                pb[m2][:, 0:LA], B["geff"][:, k2 * 256 + m2 * 128: k2 * 256 + (m2 + 1) * 128],
                B["qd"][:, k2 * TL + c * LA: k2 * TL + (c + 1) * LA],
                start=(k2 == 0), stop=(k2 == 1)),
                waits=w, inc="pe" if k2 == 1 else None)
        t_mm = kb.tok("pe")
        t1 = kb.op("vector", lambda E, m2=m2: E.tensor_copy(
            out=B["y_bf"][:, m2 * LA:(m2 + 1) * LA], in_=pb[m2][:, 0:LA]), waits=[t_mm], inc="dve")
        kb.pbtok[m2] = t1
    t_y = kb.tok("dve")
    for m in range(DT):
        bank = 4 + m % 4
        half = (m // 4) * 256
        w = [t_y, kb.pbtok.get(bank)]
        for k2 in range(2):
            kb.op("tensor", lambda E, m=m, k2=k2, bank=bank, half=half: E.matmul(
                pb[bank][:, half:half + LA],
                B["owT"][:, k2 * 1024 + m * 128: k2 * 1024 + (m + 1) * 128],
                B["y_bf"][:, k2 * LA:(k2 + 1) * LA],
                start=(k2 == 0), stop=(k2 == 1)),
                waits=w, inc="pe" if k2 == 1 else None)
        t_mm = kb.tok("pe")
        tr = kb.op("vector", lambda E, m=m, bank=bank, half=half: E.tensor_tensor(
            out=rs(m), in0=pb[bank][:, half:half + LA], in1=rs(m), op=ALU.add),
            waits=[t_mm, tok_resid], inc="dve")
        kb.pbtok[bank] = tr
    t_done = kb.tok("dve")
    t_st = None
    for i in range(DT):
        t_st = kb.store(resid_io[i * 128:(i + 1) * 128, c * LA:(c + 1) * LA], rs(i),
                        waits=[t_done])
    return t_st


def declare_attn0_ins(nc):
    return {
        "basisT": nc.dram_tensor("basisT", [128, D], F32, kind="ExternalInput").ap(),
        "qcT": nc.dram_tensor("qcT", [128, 256], F32, kind="ExternalInput").ap(),
        "kcT": nc.dram_tensor("kcT", [128, 256], F32, kind="ExternalInput").ap(),
        "vcT": nc.dram_tensor("vcT", [128, 256], F32, kind="ExternalInput").ap(),
        "ocT": nc.dram_tensor("ocT", [128, 256], F32, kind="ExternalInput").ap(),
        "dlg": nc.dram_tensor("dlg", [1, 1], F32, kind="ExternalInput").ap(),
        "osc": nc.dram_tensor("osc", [1, 1], F32, kind="ExternalInput").ap(),
        "ET": nc.dram_tensor("ET", [256, 256], F32, kind="ExternalInput").ap(),
        "C01": nc.dram_tensor("C01", [256, 256], F32, kind="ExternalInput").ap(),
        "rowq": nc.dram_tensor("rowq", [1, 256], F32, kind="ExternalInput").ap(),
        "colE": nc.dram_tensor("colE", [128, 2], F32, kind="ExternalInput").ap(),
        "jE": nc.dram_tensor("jE", [128, 16], F32, kind="ExternalInput").ap(),
        "isfirst": nc.dram_tensor("isfirst", [128, 1], F32, kind="ExternalInput").ap(),
    }


def emit_attn0_all(kb, ctx, nc, B0, A, resid_in, resid_out, tok_start, tag="_a0"):
    g_in = nc.dram_tensor("g_in" + tag, [256, 256], F32)
    g_out = nc.dram_tensor("g_out" + tag, [256, 256], F32)
    with ExitStack() as s:
        al = lambda n, sh, d=F32: s.enter_context(nc.sbuf_tensor(n + tag, sh, d))
        pb7 = s.enter_context(nc.psum_tensor("pb7" + tag, [128, 512], F32))
        B = dict(B0)
        B["pb"] = list(B0["pb"]) + [pb7]
        B["basisT"] = al("basisT_t", [128, D])
        B["qcT"] = al("qcT_t", [128, 256]); B["kcT"] = al("kcT_t", [128, 256])
        B["vcT"] = al("vcT_t", [128, 256]); B["ocT"] = al("ocT_t", [128, 256])
        B["dlg"] = al("dlg_t", [1, 1]); B["osc"] = al("osc_t", [1, 1])
        B["ET"] = al("ET_t", [128, 2 * LA]); B["C01"] = al("C01_t", [128, 2 * LA])
        B["rowq"] = al("rowq_t", [1, LA]); B["colE"] = al("colE_t", [128, 2])
        B["jE"] = al("jE_t", [128, NCA]); B["isf"] = al("isf_t", [128, 1])
        B["dsc"] = al("dsc_t", [1, 4])
        B["lnd_b"] = al("lnd_b", [128, 1]); B["os_b"] = al("os_b", [128, 1])
        B["dL_b"] = al("dL_b", [128, 1]); B["dLpows_b"] = al("dLpows_b", [128, NCA])
        B["dpows_b"] = al("dpows_b", [128, 2]); B["dqrow"] = al("dqrow", [1, LA])
        B["dqrow_b"] = al("dqrow_b", [128, LA]); B["maskT"] = al("maskT", [128, 2 * LA])
        B["qw"] = al("qw", [128, DT * 256], BF16); B["kw"] = al("kw", [128, DT * 256], BF16)
        B["vw"] = al("vw", [128, DT * 256], BF16); B["owT"] = al("owT", [128, 2 * 1024], BF16)
        B["S_after"] = al("S_after", [128, 2 * 256])
        B["S_bf"] = al("S_bf", [128, 2 * 256], BF16)
        B["geff"] = al("geff", [128, 2 * 256], BF16)
        B["q_bf"] = al("q_bf", [128, 2 * LA], BF16); B["qt_bf"] = al("qt_bf", [128, 2 * LA], BF16)
        B["k_bf"] = al("k_bf", [128, 2 * LA], BF16)
        B["kT_bf"] = al("kT_bf", [128, 2 * 256], BF16); B["vT_bf"] = al("vT_bf", [128, 2 * 256], BF16)
        B["sc_bf"] = al("sc_bf", [128, 2 * LA], BF16); B["y_bf"] = al("y_bf", [128, 2 * LA], BF16)
        B["qd"] = al("qd", [128, 2 * TL], BF16)
        B["xn"] = al("xn_attn", [128, DT * LA], BF16)
        B["xn_ap"] = lambda i, B=B: B["xn"][:, i * LA:(i + 1) * LA]
        B["xn_free"] = None
        B["cb_xmm"] = None

        for name, ap in (("basisT", A["basisT"]), ("qcT", A["qcT"]), ("kcT", A["kcT"]),
                         ("vcT", A["vcT"]), ("ocT", A["ocT"]), ("dlg", A["dlg"]),
                         ("osc", A["osc"]), ("rowq", A["rowq"]), ("colE", A["colE"]),
                         ("jE", A["jE"]), ("isf", A["isfirst"])):
            kb.load(B[name][0:ap.shape[0], 0:ap.shape[1]], ap, waits=[tok_start])
        for j in range(2):
            kb.load(B["ET"][:, j * LA:(j + 1) * LA], A["ET"][j * 128:(j + 1) * 128, :],
                    waits=[tok_start])
            kb.load(B["C01"][:, j * LA:(j + 1) * LA], A["C01"][j * 128:(j + 1) * 128, :],
                    waits=[tok_start])
        B["tok_in"] = kb.tok("ld")

        emit_attn0_prelude(kb, B)
        B["tok_S"] = B["tok_pre"]

        t_st = tok_start
        for c in range(NCA - 1, -1, -1):
            t_st = emit_attn0_chunk(kb, B, c, resid_in, resid_out, t_st)

        tS = B["tok_S"]
        t_g1 = kb.store(g_in[0:128, :], B["S_after"][:, 0:256], waits=[tS])
        t_g2 = kb.store(g_in[128:256, :], B["S_after"][:, 256:512], waits=[tS])
        t_cc = kb.op("gpsimd", lambda E, g_in=g_in, g_out=g_out: E.collective_compute(
            "AllReduce", ALU.add, replica_groups=[[0, 1], [2, 3], [4, 5], [6, 7]],
            ins=[g_in.ap()], outs=[g_out.ap()]), waits=[t_g1, t_g2], inc="cc", inc_n=1)
        t_l1 = kb.load(B["ET"][:, 0:256], g_out[0:128, :], waits=[t_cc])
        t_l2 = kb.load(B["ET"][:, 256:512], g_out[128:256, :], waits=[t_cc])
        for m2 in range(2):
            ta = kb.op("vector", lambda E, m2=m2, B=B: E.tensor_tensor(
                out=B["ET"][:, m2 * 256:(m2 + 1) * 256], in0=B["ET"][:, m2 * 256:(m2 + 1) * 256],
                in1=B["S_after"][:, m2 * 256:(m2 + 1) * 256], op=ALU.subtract),
                waits=[t_l1, t_l2], inc="dve")
            tb = kb.op("vector", lambda E, m2=m2, B=B: E.tensor_scalar(
                out=B["geff"][:, m2 * 256:(m2 + 1) * 256], in0=B["ET"][:, m2 * 256:(m2 + 1) * 256],
                scalar1=B["isf"][:, 0:1], scalar2=None, op0=ALU.mult),
                waits=[ta], inc="dve")
        t_geff = kb.tok("dve")

        for c in range(NCA):
            t_st = emit_attn0_corr(kb, B, c, resid_out, t_st, t_geff)
        B0["stats_free"] = B.get("stats_free")
        return t_st


def build_attn0():
    nc = bass.Bass("TRN2", target_bir_lowering=False)
    resid_in = nc.dram_tensor("resid", [D, TL], F32, kind="ExternalInput").ap()
    A = declare_attn0_ins(nc)
    resid_out = nc.dram_tensor("resid_out", [D, TL], F32, kind="ExternalOutput").ap()
    with ExitStack() as ctx:
        kb = KB(nc, ctx)
        B0 = {"nc": nc, "npb": 7}
        alloc_common(kb, ctx, nc, B0)
        t_st = emit_attn0_all(kb, ctx, nc, B0, A, resid_in, resid_out, None)
        kb.op("gpsimd", lambda E: E.engine_nop(), waits=[t_st])
        block = ctx.enter_context(nc.Block())
        kb.finalize(block)
    return nc


def run_attn0(resids, w, is_first_flags=None):
    L = 256
    sidx = np.arange(L)[:, None]
    tidx = np.arange(L)[None, :]
    ET = np.maximum(sidx - tidx - 1, 0).astype(np.float32)
    C01 = (sidx > tidx).astype(np.float32)
    rowq = (L - 1 - np.arange(L, dtype=np.float32))[None, :]
    colE = (np.arange(128, dtype=np.float32)[:, None] + 128.0 * np.arange(2, dtype=np.float32)[None, :])
    jE = 256.0 * np.broadcast_to(np.arange(NCA, dtype=np.float32)[None, :], (128, NCA))
    common = {
        "basisT": np.ascontiguousarray(np.asarray(w["basis"]).T),
        "qcT": np.ascontiguousarray(np.asarray(w["q_coeffs"]).T),
        "kcT": np.ascontiguousarray(np.asarray(w["k_coeffs"]).T),
        "vcT": np.ascontiguousarray(np.asarray(w["v_coeffs"]).T),
        "ocT": np.ascontiguousarray(np.asarray(w["o_coeffs"]).T),
        "dlg": np.asarray(w["decay_logit"], np.float32).reshape(1, 1),
        "osc": np.asarray(w["out_scale"], np.float32).reshape(1, 1),
        "ET": ET, "C01": C01, "rowq": np.ascontiguousarray(rowq),
        "colE": np.ascontiguousarray(colE), "jE": np.ascontiguousarray(jE),
    }
    nc = _get_nc("attn0", build_attn0)
    in_maps = []
    for i, r in enumerate(resids):
        isf = np.full((128, 1), 1.0 if (i % 2 == 0) else 0.0, np.float32)
        in_maps.append(dict(common, resid=np.ascontiguousarray(r), isfirst=isf))
    res = run_bass_kernel_spmd(nc, in_maps, core_ids=list(range(NCORES)))
    return [res.results[i]["resid_out"] for i in range(NCORES)]


DI = 1024
DIT = DI // 128
TFULL = 2 * TL
NCT = TFULL // TC
NST = 16

def emit_mamba(kb, ctx, nc, B, scr, resid_in, resid_out, tok_start=None):
    pb = B["pb"]
    PAIRS = [[0, 1], [2, 3], [4, 5], [6, 7]]
    sb = lambda n, s, d=F32: None

    with ExitStack() as s1:
        al = lambda n, s, d=F32: s1.enter_context(nc.sbuf_tensor(n, s, d))
        B["xn"] = al("xn_m", [128, DT * TC], BF16)
        B["xn_ap"] = lambda i: B["xn"][:, i * TC:(i + 1) * TC]
        B["xnf"] = al("xnf", [128, DT * TC], BF16)
        B["xpre"] = al("xpre", [128, 2 * DIT * 515])
        B["xpre_free"] = [None] * (2 * DIT)
        B["xpre_halo"] = [None] * DIT
        B["resb"] = al("resb", [128, DIT * TC])
        B["xcb"] = al("xcb", [128, DIT * TC])
        B["xcb_bf"] = al("xcb_bf", [128, DIT * TC], BF16)
        B["ct1"] = al("ct1", [128, TC]); B["ct2"] = al("ct2", [128, TC]); B["ct3"] = al("ct3", [128, TC])
        B["xdb"] = al("xdb", [128, TC])
        B["xpw"] = al("xpwt", [128, DIT * 96], BF16)
        B["wbuf"] = al("wbuf_m", [128, 6 * 1024], BF16)
        B["wslots"] = 6
        B["wcols"] = 1024
        B["wslot_tok"] = [B["tok_prep"]] * 6

        with ExitStack() as s2:
            tmp = s2.enter_context(nc.sbuf_tensor("preptmp2", [128, 96], F32))
            tok = B.get("tok_start")
            for k in range(DIT):
                tl = kb.load(tmp[:, 0:96], B["xpw_dram"][k * 128:(k + 1) * 128, :], waits=[tok])
                tok = kb.op("scalar", lambda E, k=k, tmp=tmp: E.copy(out=B["xpw"][:, k * 96:(k + 1) * 96],
                                                            in_=tmp[:, 0:96]), waits=[tl], inc="act")
            B["tok_xpw"] = tok

        rs = lambda i: B["resid"][:, i * TC:(i + 1) * TC]
        xn = B["xn_ap"]
        t_st = tok_start
        for c in range(NCH):
            tok_resid = None
            for i in range(DT):
                tok_resid = kb.load(rs(i), resid_in[i * 128:(i + 1) * 128, c * TC:(c + 1) * TC],
                                    waits=[t_st])
            t_xn = emit_ln(kb, B, rs, tok_resid)
            for i in range(DT):
                t_st = kb.store(scr["xn_stage"][i * 128:(i + 1) * 128, c * TC:(c + 1) * TC], xn(i),
                                waits=[t_xn])
            B["xn_free"] = t_st
        t_ag = kb.op("gpsimd", lambda E: E.collective_compute(
            "AllGather", ALU.bypass, replica_groups=PAIRS,
            ins=[scr["xn_stage"]], outs=[scr["ag_out"]]), waits=[t_st], inc="cc", inc_n=1)

        xpre = lambda p, i: B["xpre"][:, (p * DIT + i) * 515: (p * DIT + i) * 515 + 515]
        for i in range(DIT):
            kb.op("vector", lambda E, i=i: E.memset(B["xpre"][:, i * 515: i * 515 + 3], 0.0),
                  inc="dve" if i == DIT - 1 else None)
        t_xc_st = None
        t_res_st = None
        for ct in range(NCT):
            p = ct % 2
            blk = 0 if ct < NCH else 1024
            cc0 = (ct % NCH) * TC
            t_ld = None
            for i in range(DT):
                t_ld = kb.load(B["xnf"][:, i * TC:(i + 1) * TC],
                               scr["ag_out"][blk + i * 128: blk + (i + 1) * 128, cc0:cc0 + TC],
                               waits=[t_ag, B.get("xnf_free")])
            def ev_ip(m, bank, w, p=p):
                if m < DIT:
                    t = kb.op("vector", lambda E, m=m, bank=bank, p=p: E.tensor_copy(
                        out=xpre(p, m)[:, 3:515], in_=bank[:, 0:TC]),
                        waits=w + [B["xpre_free"][p * DIT + m]], inc="dve")
                else:
                    t = kb.op("vector", lambda E, m=m, bank=bank: E.tensor_copy(
                        out=B["resb"][:, (m - DIT) * TC:(m - DIT + 1) * TC], in_=bank[:, 0:TC]),
                        waits=w + [B.get("resb_free")], inc="dve")
                return t
            emit_mm_stage(kb, B, scr["ipw"], DT, 2 * DIT, lambda k: B["xnf"][:, k * TC:(k + 1) * TC],
                          ev_ip, t_ld, mgs=8)
            B["xnf_free"] = kb.tok("pe")
            t_ip = kb.tok("dve")
            for i in range(DIT):
                t_res_st = kb.store(scr["resg"][i * 128:(i + 1) * 128, ct * TC:(ct + 1) * TC],
                                    B["resb"][:, i * TC:(i + 1) * TC], waits=[t_ip])
            B["resb_free"] = t_res_st
            t_xca = None
            for i in range(DIT):
                cw = lambda tap, i=i: B["cwc"][:, tap * DIT + i: tap * DIT + i + 1]
                t1 = kb.op("vector", lambda E, i=i, p=p, cw=cw: E.tensor_scalar(
                    out=B["ct1"][:, 0:TC], in0=xpre(p, i)[:, 0:TC], scalar1=cw(0), scalar2=None,
                    op0=ALU.mult), waits=[t_ip, B.get("ct_free")], inc="dve")
                t2 = kb.op("vector", lambda E, i=i, p=p, cw=cw: E.tensor_scalar(
                    out=B["ct2"][:, 0:TC], in0=xpre(p, i)[:, 1:1 + TC], scalar1=cw(1), scalar2=None,
                    op0=ALU.mult), waits=[t_ip], inc="dve")
                t3 = kb.op("vector", lambda E, i=i, p=p, cw=cw: E.tensor_scalar(
                    out=B["ct3"][:, 0:TC], in0=xpre(p, i)[:, 2:2 + TC], scalar1=cw(2), scalar2=None,
                    op0=ALU.mult), waits=[t_ip], inc="dve")
                t4 = kb.op("vector", lambda E, i=i, p=p, cw=cw: E.scalar_tensor_tensor(
                    out=B["ct1"][:, 0:TC], in0=xpre(p, i)[:, 3:3 + TC], scalar=cw(3),
                    in1=B["ct1"][:, 0:TC], op0=ALU.mult, op1=ALU.add), waits=[t1], inc="dve")
                t5 = kb.op("vector", lambda E: E.tensor_tensor(
                    out=B["ct2"][:, 0:TC], in0=B["ct2"][:, 0:TC], in1=B["ct3"][:, 0:TC], op=ALU.add),
                    waits=[t2, t3], inc="dve")
                t6 = kb.op("vector", lambda E: E.tensor_tensor(
                    out=B["ct1"][:, 0:TC], in0=B["ct1"][:, 0:TC], in1=B["ct2"][:, 0:TC], op=ALU.add),
                    waits=[t4, t5], inc="dve")
                t7 = kb.op("scalar", lambda E, i=i: E.activation(
                    out=B["xcb"][:, i * TC:(i + 1) * TC], in_=B["ct1"][:, 0:TC], func=AF.Silu,
                    bias=B["cbc"][:, i:i + 1]), waits=[t6, B.get("xcb_free")], inc="act")
                t8 = kb.op("scalar", lambda E, i=i: E.copy(
                    out=B["xcb_bf"][:, i * TC:(i + 1) * TC], in_=B["xcb"][:, i * TC:(i + 1) * TC]),
                    waits=[t7, B.get("xcb_bf_free")], inc="act")
                B["ct_free"] = t7
                t_xca = t8
                th = kb.op("vector", lambda E, i=i, p=p: E.tensor_copy(
                    out=xpre(1 - p, i)[:, 0:3], in_=xpre(p, i)[:, 512:515]),
                    waits=[t_ip, B["xpre_free"][(1 - p) * DIT + i]], inc="dve")
                B["xpre_halo"][i] = th
            for i in range(DIT):
                t_xc_st = kb.store(scr["xconv"][i * 128:(i + 1) * 128, ct * TC:(ct + 1) * TC],
                                   B["xcb"][:, i * TC:(i + 1) * TC], waits=[t_xca])
            B["xcb_free"] = t_xc_st
            w96 = [t_xca, kb.pbtok.get(0), B["tok_xpw"]]
            for k in range(DIT):
                kb.op("tensor", lambda E, k=k: E.matmul(
                    pb[0][0:96, 0:TC], B["xpw"][:, k * 96:(k + 1) * 96],
                    B["xcb_bf"][:, k * TC:(k + 1) * TC],
                    start=(k == 0), stop=(k == DIT - 1)), waits=w96, inc="pe" if k == DIT - 1 else None)
            t_xp = kb.tok("pe")
            B["xcb_bf_free"] = t_xp
            t_ev = kb.op("vector", lambda E: E.tensor_copy(out=B["xdb"][0:96, 0:TC],
                                                           in_=pb[0][0:96, 0:TC]),
                         waits=[t_xp, B.get("xdb_free")], inc="dve")
            kb.pbtok[0] = t_ev
            B["xdb_free"] = kb.store(scr["xdbl_part"][0:96, ct * TC:(ct + 1) * TC],
                                     B["xdb"][0:96, 0:TC], waits=[t_ev])
            for i in range(DIT):
                B["xpre_free"][p * DIT + i] = B["xpre_halo"][i]
        t_xd_st = kb.tok("st")
        t_xc_st_last = kb.tok("st")

    t_ar1 = kb.op("gpsimd", lambda E: E.collective_compute(
        "AllReduce", ALU.add, replica_groups=PAIRS,
        ins=[scr["xdbl_part"]], outs=[scr["xdbl_sum"]]), waits=[t_xd_st], inc="cc", inc_n=1)

    with ExitStack() as s3:
        al = lambda n, s, d=F32: s3.enter_context(nc.sbuf_tensor(n, s, d))
        B["dpw"] = al("dpwt", [64, DI], BF16)
        B["opw"] = al("opwt", [128, DIT * 1024], BF16)
        B["dtb"] = al("dtbt", [64, TC], BF16)
        B["dt32"] = al("dt32t", [64, TC])
        B["BCc"] = al("BCct", [32, TC])
        B["delta"] = al("deltat", [128, DIT * TC])
        B["du"] = al("dut", [128, DIT * TC])
        B["u32"] = al("u32t", [128, DIT * TC])
        B["res32"] = al("res32t", [128, DIT * TC])
        B["yb"] = al("ybt", [128, DIT * TC])
        B["ygb"] = al("ygbt", [128, DIT * TC], BF16)
        B["pob"] = al("pobt", [128, DT * TC])
        B["Bb"] = al("Bbt", [128, TC]); B["Cb"] = al("Cbt", [128, TC])
        B["dA"] = al("dAt", [128, TC]); B["dBu"] = al("dBut", [128, TC])
        B["hseq"] = al("hseqt", [128, TC])
        B["ct1"] = al("ct1b", [128, TC]); B["ct2"] = al("ct2b", [128, TC]); B["ct3"] = al("ct3b", [128, TC])

        with ExitStack() as s4:
            tmp = s4.enter_context(nc.sbuf_tensor("preptmp3", [128, 1024], F32))
            tl = kb.load(tmp[0:64, 0:DI], B["dpw_dram"], waits=[t_xc_st_last])
            tok = kb.op("scalar", lambda E, tmp=tmp: E.copy(out=B["dpw"][:], in_=tmp[0:64, 0:DI]),
                        waits=[tl], inc="act")
            for k in range(DIT):
                tl = kb.load(tmp[:, 0:1024], B["opw_dram"][k * 128:(k + 1) * 128, :], waits=[tok])
                tok = kb.op("scalar", lambda E, k=k, tmp=tmp: E.copy(
                    out=B["opw"][:, k * 1024:(k + 1) * 1024], in_=tmp[:, 0:1024]),
                    waits=[tl], inc="act")
            B["tok_w6"] = tok

        ta = kb.op("scalar", lambda E: E.activation(out=B["aneg"][:], in_=B["alogc"][:], func=AF.Exp),
                   waits=[B["tok_in"]], inc="act")
        t_aneg = kb.op("vector", lambda E: E.tensor_scalar(out=B["aneg"][:], in0=B["aneg"][:],
                                                           scalar1=-1.0, scalar2=None, op0=ALU.mult),
                       waits=[ta], inc="dve")
        kb.op("vector", lambda E: E.memset(B["carries"][:], 0.0), inc="dve")
        t_car = kb.tok("dve")

        t_po_st = None
        for ct in range(NCT):
            tl = kb.load(B["dt32"][0:64, 0:TC], scr["xdbl_sum"][0:64, ct * TC:(ct + 1) * TC],
                         waits=[t_ar1, B.get("dt32_free")])
            t_dt = kb.op("scalar", lambda E: E.copy(out=B["dtb"][0:64, 0:TC], in_=B["dt32"][0:64, 0:TC]),
                         waits=[tl, B.get("dtb_free")], inc="act")
            B["dt32_free"] = t_dt
            t_bcl = kb.load(B["BCc"][0:32, 0:TC], scr["xdbl_sum"][64:96, ct * TC:(ct + 1) * TC],
                            waits=[t_ar1, B.get("BCc_free")])
            bank_waits = [kb.pbtok.get(j) for j in range(8)]
            for m in range(DIT):
                kb.op("tensor", lambda E, m=m: E.matmul(
                    pb[m][:, 0:TC], B["dpw"][0:64, m * 128:(m + 1) * 128], B["dtb"][0:64, 0:TC],
                    start=True, stop=True),
                    waits=[t_dt, B["tok_w6"]] + bank_waits, inc="pe" if m == DIT - 1 else None)
            t_mm = kb.tok("pe")
            B["dtb_free"] = t_mm
            for m in range(DIT):
                dsl = B["delta"][:, m * TC:(m + 1) * TC]
                q0 = kb.op("vector", lambda E, m=m, dsl=dsl: E.tensor_scalar(
                    out=dsl, in0=pb[m][:, 0:TC], scalar1=B["dpbc"][:, m:m + 1], scalar2=None,
                    op0=ALU.add), waits=[t_mm, B.get("delta_free"), B["tok_in"]], inc="dve")
                q1 = kb.op("scalar", lambda E, dsl=dsl: E.activation(
                    out=B["ct1"][:, 0:TC], in_=dsl, func=AF.Abs),
                    waits=[q0, B.get("ct_free6")], inc="act")
                q2 = kb.op("scalar", lambda E: E.activation(out=B["ct2"][:, 0:TC], in_=B["ct1"][:, 0:TC],
                                                            func=AF.Exp, scale=-1.0), waits=[q1], inc="act")
                q3 = kb.op("scalar", lambda E: E.activation(out=B["ct1"][:, 0:TC], in_=B["ct2"][:, 0:TC],
                                                            func=AF.Ln, bias=1.0), waits=[q2], inc="act")
                q4 = kb.op("vector", lambda E, dsl=dsl: E.tensor_scalar(
                    out=B["ct2"][:, 0:TC], in0=dsl, scalar1=0.0, scalar2=None, op0=ALU.max),
                    waits=[q3], inc="dve")
                q5 = kb.op("vector", lambda E, dsl=dsl: E.tensor_tensor(
                    out=dsl, in0=B["ct1"][:, 0:TC], in1=B["ct2"][:, 0:TC], op=ALU.add),
                    waits=[q4], inc="dve")
                B["ct_free6"] = q5
                kb.pbtok[m] = q5
            t_delta = kb.tok("dve")
            t_u = None
            for i in range(DIT):
                t_u = kb.load(B["u32"][:, i * TC:(i + 1) * TC],
                              scr["xconv"][i * 128:(i + 1) * 128, ct * TC:(ct + 1) * TC],
                              waits=[t_xc_st_last, B.get("u32_free")])
            t_r = None
            for i in range(DIT):
                t_r = kb.load(B["res32"][:, i * TC:(i + 1) * TC],
                              scr["resg"][i * 128:(i + 1) * 128, ct * TC:(ct + 1) * TC],
                              waits=[t_xc_st_last, B.get("res32_free")])
            t_du = None
            for i in range(DIT):
                t_du = kb.op("vector", lambda E, i=i: E.tensor_tensor(
                    out=B["du"][:, i * TC:(i + 1) * TC], in0=B["delta"][:, i * TC:(i + 1) * TC],
                    in1=B["u32"][:, i * TC:(i + 1) * TC], op=ALU.mult),
                    waits=[t_delta, t_u, B.get("du_free")], inc="dve")
            for n in range(NST):
                bb = n % 2
                tb1 = kb.op("tensor", lambda E, n=n, bb=bb: E.matmul(
                    pb[bb][:, 0:TC], B["oh32"][0:32, n * 128:(n + 1) * 128], B["BCc"][0:32, 0:TC],
                    start=True, stop=True),
                    waits=[t_bcl, B["tok_in"], kb.pbtok.get(bb)], inc="pe")
                tb2 = kb.op("vector", lambda E, bb=bb: E.tensor_copy(out=B["Bb"][:, 0:TC],
                                                                     in_=pb[bb][:, 0:TC]),
                            waits=[tb1, B.get("Bb_free")], inc="dve")
                tb3 = kb.op("tensor", lambda E, n=n, bb=bb: E.matmul(
                    pb[bb][:, 0:TC], B["oh32"][0:32, (16 + n) * 128:(16 + n + 1) * 128],
                    B["BCc"][0:32, 0:TC], start=True, stop=True),
                    waits=[tb2], inc="pe")
                tb4 = kb.op("vector", lambda E, bb=bb: E.tensor_copy(out=B["Cb"][:, 0:TC],
                                                                     in_=pb[bb][:, 0:TC]),
                            waits=[tb3, B.get("Cb_free")], inc="dve")
                kb.pbtok[bb] = tb4
                for i in range(DIT):
                    idx = i * NST + n
                    tda = kb.op("scalar", lambda E, i=i, idx=idx: E.activation(
                        out=B["dA"][:, 0:TC], in_=B["delta"][:, i * TC:(i + 1) * TC], func=AF.Exp,
                        scale=B["aneg"][:, idx:idx + 1]),
                        waits=[t_delta, t_aneg, B.get("dA_free")], inc="act")
                    tdb = kb.op("vector", lambda E, i=i: E.tensor_tensor(
                        out=B["dBu"][:, 0:TC], in0=B["du"][:, i * TC:(i + 1) * TC],
                        in1=B["Bb"][:, 0:TC], op=ALU.mult),
                        waits=[t_du, tb2, B.get("dBu_free")], inc="dve")
                    tsc = kb.op("vector", lambda E, idx=idx: E.tensor_tensor_scan(
                        out=B["hseq"][:, 0:TC], data0=B["dA"][:, 0:TC], data1=B["dBu"][:, 0:TC],
                        initial=B["carries"][:, idx:idx + 1], op0=ALU.mult, op1=ALU.add),
                        waits=[tda, tdb, t_car, B.get("hseq_free")], inc="dve")
                    tcar = kb.op("vector", lambda E, idx=idx: E.tensor_copy(
                        out=B["carries"][:, idx:idx + 1], in_=B["hseq"][:, TC - 1:TC]),
                        waits=[tsc], inc="dve")
                    B["dA_free"] = tsc
                    B["dBu_free"] = tsc
                    if n == 0:
                        ty = kb.op("vector", lambda E, i=i: E.tensor_tensor(
                            out=B["yb"][:, i * TC:(i + 1) * TC], in0=B["hseq"][:, 0:TC],
                            in1=B["Cb"][:, 0:TC], op=ALU.mult),
                            waits=[tsc, tb4, B.get("yb_free")], inc="dve")
                    else:
                        tt = kb.op("vector", lambda E: E.tensor_tensor(
                            out=B["ct3"][:, 0:TC], in0=B["hseq"][:, 0:TC], in1=B["Cb"][:, 0:TC],
                            op=ALU.mult), waits=[tsc, tb4], inc="dve")
                        ty = kb.op("vector", lambda E, i=i: E.tensor_tensor(
                            out=B["yb"][:, i * TC:(i + 1) * TC], in0=B["yb"][:, i * TC:(i + 1) * TC],
                            in1=B["ct3"][:, 0:TC], op=ALU.add), waits=[tt], inc="dve")
                    B["hseq_free"] = ty
                B["Bb_free"] = kb.tok("dve")
                B["Cb_free"] = B["Bb_free"]
            t_car = kb.tok("dve")
            B["delta_free"] = t_car
            B["du_free"] = t_car
            B["BCc_free"] = t_car
            t_yg = None
            for i in range(DIT):
                ta2 = kb.op("vector", lambda E, i=i: E.scalar_tensor_tensor(
                    out=B["yb"][:, i * TC:(i + 1) * TC], in0=B["u32"][:, i * TC:(i + 1) * TC],
                    scalar=B["dskc"][:, i:i + 1], in1=B["yb"][:, i * TC:(i + 1) * TC],
                    op0=ALU.mult, op1=ALU.add), waits=[t_car, B["tok_in"]], inc="dve")
                tsr = kb.op("scalar", lambda E, i=i: E.activation(
                    out=B["ct1"][:, 0:TC] if i % 2 == 0 else B["ct2"][:, 0:TC],
                    in_=B["res32"][:, i * TC:(i + 1) * TC], func=AF.Silu),
                    waits=[t_r, B.get("srp_free_%d" % (i % 2))], inc="act")
                t_yg = kb.op("vector", lambda E, i=i: E.tensor_tensor(
                    out=B["ygb"][:, i * TC:(i + 1) * TC], in0=B["yb"][:, i * TC:(i + 1) * TC],
                    in1=B["ct1"][:, 0:TC] if i % 2 == 0 else B["ct2"][:, 0:TC], op=ALU.mult),
                    waits=[ta2, tsr, B.get("ygb_free")], inc="dve")
                B["srp_free_%d" % (i % 2)] = t_yg
            B["u32_free"] = t_yg
            B["res32_free"] = t_yg
            B["yb_free"] = t_yg
            B["ct_free6"] = t_yg
            bank_waits = [kb.pbtok.get(j) for j in range(8)]
            for m in range(DT):
                for k in range(DIT):
                    kb.op("tensor", lambda E, m=m, k=k: E.matmul(
                        pb[m][:, 0:TC], B["opw"][:, k * 1024 + m * 128: k * 1024 + (m + 1) * 128],
                        B["ygb"][:, k * TC:(k + 1) * TC],
                        start=(k == 0), stop=(k == DIT - 1)),
                        waits=[t_yg, B["tok_w6"]] + (bank_waits if k == 0 else []),
                        inc="pe" if k == DIT - 1 else None)
            t_pmm = kb.tok("pe")
            t_poev = None
            for m in range(DT):
                t_poev = kb.op("vector", lambda E, m=m: E.tensor_copy(
                    out=B["pob"][:, m * TC:(m + 1) * TC], in_=pb[m][:, 0:TC]),
                    waits=[t_pmm, B.get("pob_free")], inc="dve")
                kb.pbtok[m] = t_poev
            B["ygb_free"] = kb.tok("pe")
            t_po = None
            for m in range(DT):
                t_po = kb.store(scr["pout"][m * 128:(m + 1) * 128, ct * TC:(ct + 1) * TC],
                                B["pob"][:, m * TC:(m + 1) * TC], waits=[t_poev])
            B["pob_free"] = t_po
            t_po_st = t_po

    t_ar2 = kb.op("gpsimd", lambda E: E.collective_compute(
        "AllReduce", ALU.add, replica_groups=PAIRS,
        ins=[scr["pout"]], outs=[scr["pout_sum"]]), waits=[t_po_st], inc="cc", inc_n=1)

    with ExitStack() as s5:
        al = lambda n, s, d=F32: s5.enter_context(nc.sbuf_tensor(n, s, d))
        pa = al("pa", [128, DT * TC])
        pb2 = al("pob2", [128, DT * TC])
        rs = lambda i: B["resid"][:, i * TC:(i + 1) * TC]
        t_st2 = None
        pa_free = None
        for c in range(NCH):
            tok_resid = None
            for i in range(DT):
                tok_resid = kb.load(rs(i), resid_in[i * 128:(i + 1) * 128, c * TC:(c + 1) * TC],
                                    waits=[t_st2, t_ar2])
            tpa = None
            for i in range(DT):
                tpa = kb.load(pa[:, i * TC:(i + 1) * TC],
                              scr["pout_sum"][i * 128:(i + 1) * 128, c * TC:(c + 1) * TC],
                              waits=[t_ar2, pa_free])
            tpb = None
            for i in range(DT):
                tpb = kb.load(pb2[:, i * TC:(i + 1) * TC],
                              scr["pout_sum"][i * 128:(i + 1) * 128, TL + c * TC: TL + (c + 1) * TC],
                              waits=[t_ar2, pa_free])
            t_up = None
            for i in range(DT):
                t1 = kb.op("vector", lambda E, i=i: E.scalar_tensor_tensor(
                    out=rs(i), in0=pa[:, i * TC:(i + 1) * TC], scalar=B["isf"][:, 0:1],
                    in1=rs(i), op0=ALU.mult, op1=ALU.add),
                    waits=[tok_resid, tpa, B["tok_in"]], inc="dve")
                t_up = kb.op("vector", lambda E, i=i: E.scalar_tensor_tensor(
                    out=rs(i), in0=pb2[:, i * TC:(i + 1) * TC], scalar=B["iss"][:, 0:1],
                    in1=rs(i), op0=ALU.mult, op1=ALU.add), waits=[t1, tpb], inc="dve")
            pa_free = t_up
            for i in range(DT):
                t_st2 = kb.store(resid_out[i * 128:(i + 1) * 128, c * TC:(c + 1) * TC], rs(i),
                                 waits=[t_up])
    return t_st2


def declare_mamba_ins(nc):
    return {
        "ipw": nc.dram_tensor("ipw", [D, 2 * DI], F32, kind="ExternalInput").ap(),
        "cwc": nc.dram_tensor("cwc", [128, 4 * DIT], F32, kind="ExternalInput").ap(),
        "cbc": nc.dram_tensor("cbc", [128, DIT], F32, kind="ExternalInput").ap(),
        "xpw": nc.dram_tensor("xpw", [DI, 96], F32, kind="ExternalInput").ap(),
        "dpw": nc.dram_tensor("dpw", [64, DI], F32, kind="ExternalInput").ap(),
        "dpbc": nc.dram_tensor("dpbc", [128, DIT], F32, kind="ExternalInput").ap(),
        "alogc": nc.dram_tensor("alogc", [128, DIT * NST], F32, kind="ExternalInput").ap(),
        "dskc": nc.dram_tensor("dskc", [128, DIT], F32, kind="ExternalInput").ap(),
        "opw": nc.dram_tensor("opw", [DI, D], F32, kind="ExternalInput").ap(),
        "oh32": nc.dram_tensor("oh32", [32, 32 * 128], F32, kind="ExternalInput").ap(),
        "issecond": nc.dram_tensor("issecond", [128, 1], F32, kind="ExternalInput").ap(),
    }


def declare_mamba_scr(nc):
    return {
        "xn_stage": nc.dram_tensor("xn_stage", [D, TL], BF16).ap(),
        "ag_out": nc.dram_tensor("ag_out", [2 * D, TL], BF16).ap(),
        "xdbl_part": nc.dram_tensor("xdbl_part", [96, TFULL], F32).ap(),
        "xdbl_sum": nc.dram_tensor("xdbl_sum", [96, TFULL], F32).ap(),
        "xconv": nc.dram_tensor("xconv", [DI, TFULL], F32).ap(),
        "resg": nc.dram_tensor("resg", [DI, TFULL], F32).ap(),
        "pout": nc.dram_tensor("pout", [D, TFULL], F32).ap(),
        "pout_sum": nc.dram_tensor("pout_sum", [D, TFULL], F32).ap(),
        "ipw": nc.dram_tensor("ipw_bf", [D, 2 * DI], BF16).ap(),
    }


def emit_mamba_all(kb, ctx, nc, B0, A, A_isfirst, scr, resid_in, resid_out, tok_start, tag="_mb"):
    with ExitStack() as s:
        al = lambda n, sh, d=F32: s.enter_context(nc.sbuf_tensor(n + tag, sh, d))
        pb7 = s.enter_context(nc.psum_tensor("pb7" + tag, [128, 512], F32))
        B = dict(B0)
        B["pb"] = list(B0["pb"]) + [pb7]
        B["cwc"] = al("cwct", [128, 4 * DIT]); B["cbc"] = al("cbct", [128, DIT])
        B["dpbc"] = al("dpbct", [128, DIT]); B["alogc"] = al("alogct", [128, DIT * NST])
        B["dskc"] = al("dskct", [128, DIT]); B["isf"] = al("isft", [128, 1])
        B["iss"] = al("isst", [128, 1])
        B["oh32"] = al("oh32t", [32, 32 * 128])
        B["aneg"] = al("anegt", [128, DIT * NST])
        B["carries"] = al("carriest", [128, DIT * NST])
        B["xpw_dram"] = A["xpw"]
        B["dpw_dram"] = A["dpw"]
        B["opw_dram"] = A["opw"]
        B["xn_free"] = None
        B["cb_xmm"] = None

        B["tok_start"] = tok_start
        for name, ap in (("cwc", A["cwc"]), ("cbc", A["cbc"]), ("dpbc", A["dpbc"]),
                         ("alogc", A["alogc"]), ("dskc", A["dskc"]), ("isf", A_isfirst),
                         ("iss", A["issecond"]), ("oh32", A["oh32"])):
            kb.load(B[name][0:ap.shape[0], 0:ap.shape[1]], ap, waits=[tok_start])
        B["tok_in"] = kb.tok("ld")

        emit_prep(kb, ctx, nc, [(A["ipw"], scr["ipw"])], tok_start=tok_start)
        B["tok_prep"] = kb.tok("st")

        t_end = emit_mamba(kb, ctx, nc, B, scr, resid_in, resid_out, tok_start)
        B0["stats_free"] = B.get("stats_free")
        return t_end


def build_mamba():
    nc = bass.Bass("TRN2", target_bir_lowering=False)
    resid_in = nc.dram_tensor("resid", [D, TL], F32, kind="ExternalInput").ap()
    A = declare_mamba_ins(nc)
    isfirst = nc.dram_tensor("isfirst", [128, 1], F32, kind="ExternalInput").ap()
    resid_out = nc.dram_tensor("resid_out", [D, TL], F32, kind="ExternalOutput").ap()
    scr = declare_mamba_scr(nc)
    with ExitStack() as ctx:
        kb = KB(nc, ctx)
        B0 = {"nc": nc, "npb": 7}
        alloc_common(kb, ctx, nc, B0)
        t_end = emit_mamba_all(kb, ctx, nc, B0, A, isfirst, scr, resid_in, resid_out, None)
        kb.op("gpsimd", lambda E: E.engine_nop(), waits=[t_end])
        block = ctx.enter_context(nc.Block())
        kb.finalize(block)
    return nc


def run_mamba(resids, w):
    w = {k: np.asarray(v) for k, v in w.items()}
    nc = _get_nc("mamba", build_mamba)
    oh32 = np.zeros((32, 32 * 128), np.float32)
    for r in range(32):
        oh32[r, r * 128:(r + 1) * 128] = 1.0
    in_maps = []
    for core in range(NCORES):
        dh = core % 2
        sl = slice(dh * DI, (dh + 1) * DI)
        ipw_x = w["in_proj_w"][:, sl]
        ipw_r = w["in_proj_w"][:, 2048 + dh * DI: 2048 + (dh + 1) * DI]
        cw = w["conv_w"][sl, 0, :]
        cwc = np.concatenate([_col128(np.ascontiguousarray(cw[:, tap])) for tap in range(4)], axis=1)
        alog = w["A_log"][sl]
        alogc = np.ascontiguousarray(alog.reshape(DIT, 128, NST).transpose(1, 0, 2).reshape(128, DIT * NST))
        m = {
            "resid": np.ascontiguousarray(resids[core]),
            "ipw": np.ascontiguousarray(np.concatenate([ipw_x, ipw_r], axis=1)),
            "cwc": cwc,
            "cbc": _col128(w["conv_b"][sl]),
            "xpw": np.ascontiguousarray(w["x_proj_w"][sl]),
            "dpw": np.ascontiguousarray(w["dt_proj_w"][:, sl]),
            "dpbc": _col128(w["dt_proj_b"][sl]),
            "alogc": alogc,
            "dskc": _col128(w["D_skip"][sl]),
            "opw": np.ascontiguousarray(w["out_proj_w"][sl]),
            "oh32": oh32,
            "isfirst": np.full((128, 1), 1.0 if dh == 0 else 0.0, np.float32),
            "issecond": np.full((128, 1), 1.0 if dh == 1 else 0.0, np.float32),
        }
        in_maps.append(m)
    res = run_bass_kernel_spmd(nc, in_maps, core_ids=list(range(NCORES)))
    return [res.results[i]["resid_out"] for i in range(NCORES)]


def build_fused():
    nph = int(os.environ.get("N_PHASES", "6"))
    nc = bass.Bass("TRN2", target_bir_lowering=False)
    resid0 = nc.dram_tensor("resid", [D, TL], F32, kind="ExternalInput").ap()
    A_f = declare_ffn0_ins(nc)
    A_a = declare_attn0_ins(nc)
    A_m = declare_moe_ins(nc)
    A_b = declare_mamba_ins(nc)
    out = nc.dram_tensor("resid_out", [D, TL], F32, kind="ExternalOutput").ap()
    r = [nc.dram_tensor(f"rscr{i}", [D, TL], F32).ap() for i in range(5)]
    scr_f = declare_ffn0_scr(nc)
    scr_m = declare_moe_scr(nc)
    scr_b = declare_mamba_scr(nc)

    with ExitStack() as ctx:
        kb = KB(nc, ctx)
        B0 = {"nc": nc, "npb": 7}
        alloc_common(kb, ctx, nc, B0)
        dsts = [r[0], r[1], r[2], r[3], r[4], out]
        dsts[nph - 1] = out
        t = emit_ffn0_all(kb, ctx, nc, B0, A_f, scr_f, resid0, dsts[0], True, "_f0", None)
        if nph >= 2:
            t = emit_attn0_all(kb, ctx, nc, B0, A_a, dsts[0], dsts[1], t)
        if nph >= 3:
            t = emit_ffn0_all(kb, ctx, nc, B0, A_f, scr_f, dsts[1], dsts[2], False, "_f1", t)
        if nph >= 4:
            t = emit_moe_all(kb, ctx, nc, B0, A_m, scr_m, dsts[2], dsts[3], True, "_m0", t)
        if nph >= 5:
            t = emit_mamba_all(kb, ctx, nc, B0, A_b, A_a["isfirst"], scr_b, dsts[3], dsts[4], t)
        if nph >= 6:
            t = emit_moe_all(kb, ctx, nc, B0, A_m, scr_m, dsts[4], out, False, "_m1", t)
        kb.op("gpsimd", lambda E: E.engine_nop(), waits=[t])
        block = ctx.enter_context(nc.Block())
        kb.finalize(block)
    return nc


def _ffn0_maps(w):
    return {
        "w1t": np.ascontiguousarray(np.asarray(w["w1_w"]).T),
        "w2t": np.ascontiguousarray(np.asarray(w["w2_w"]).T),
        "h1t": np.ascontiguousarray(np.asarray(w["h1_w"]).T),
        "h2t": np.ascontiguousarray(np.asarray(w["h2_w"]).T),
        "w3t": np.ascontiguousarray(np.asarray(w["w3_w"]).T),
        "s1": _col128(w["w1_s"]), "s2": _col128(w["w2_s"]),
        "sh1": _col128(w["h1_s"]), "sh2": _col128(w["h2_s"]),
        "s3": _col128(w["w3_s"]),
    }


def _attn0_maps(w):
    L = 256
    sidx = np.arange(L)[:, None]
    tidx = np.arange(L)[None, :]
    return {
        "basisT": np.ascontiguousarray(np.asarray(w["basis"]).T),
        "qcT": np.ascontiguousarray(np.asarray(w["q_coeffs"]).T),
        "kcT": np.ascontiguousarray(np.asarray(w["k_coeffs"]).T),
        "vcT": np.ascontiguousarray(np.asarray(w["v_coeffs"]).T),
        "ocT": np.ascontiguousarray(np.asarray(w["o_coeffs"]).T),
        "dlg": np.asarray(w["decay_logit"], np.float32).reshape(1, 1),
        "osc": np.asarray(w["out_scale"], np.float32).reshape(1, 1),
        "ET": np.maximum(sidx - tidx - 1, 0).astype(np.float32),
        "C01": (sidx > tidx).astype(np.float32),
        "rowq": np.ascontiguousarray((L - 1 - np.arange(L, dtype=np.float32))[None, :]),
        "colE": np.ascontiguousarray(np.arange(128, dtype=np.float32)[:, None]
                                     + 128.0 * np.arange(2, dtype=np.float32)[None, :]),
        "jE": np.ascontiguousarray(256.0 * np.broadcast_to(
            np.arange(NCA, dtype=np.float32)[None, :], (128, NCA))),
    }


def _moe_maps(w):
    b1 = np.asarray(w["b1"]); b2 = np.asarray(w["b2"])
    return {
        "w1": np.ascontiguousarray(np.asarray(w["w1"]).reshape(NEXP * D, HID)),
        "w2": np.ascontiguousarray(np.asarray(w["w2"]).reshape(NEXP * HID, D)),
        "rw": np.ascontiguousarray(np.asarray(w["router_w"])),
        "rb_bc": np.ascontiguousarray(np.broadcast_to(np.asarray(w["router_b"])[None, :], (128, NEXP))),
        "b1c": np.concatenate([_col128(b1[e]) for e in range(NEXP)], axis=1),
        "b2c": np.concatenate([_col128(b2[e]) for e in range(NEXP)], axis=1),
        "ident128": np.eye(128, dtype=np.float32),
        "onehot8": np.concatenate([np.eye(8, dtype=np.float32)[:, e:e + 1]
                                   * np.ones((1, 128), np.float32) for e in range(NEXP)], axis=1),
    }


def _mamba_maps(w, dh):
    w = {k: np.asarray(v) for k, v in w.items()}
    oh32 = np.zeros((32, 32 * 128), np.float32)
    for rr in range(32):
        oh32[rr, rr * 128:(rr + 1) * 128] = 1.0
    sl = slice(dh * DI, (dh + 1) * DI)
    ipw_x = w["in_proj_w"][:, sl]
    ipw_r = w["in_proj_w"][:, 2048 + dh * DI: 2048 + (dh + 1) * DI]
    cw = w["conv_w"][sl, 0, :]
    alog = w["A_log"][sl]
    return {
        "ipw": np.ascontiguousarray(np.concatenate([ipw_x, ipw_r], axis=1)),
        "cwc": np.concatenate([_col128(np.ascontiguousarray(cw[:, tap])) for tap in range(4)], axis=1),
        "cbc": _col128(w["conv_b"][sl]),
        "xpw": np.ascontiguousarray(w["x_proj_w"][sl]),
        "dpw": np.ascontiguousarray(w["dt_proj_w"][:, sl]),
        "dpbc": _col128(w["dt_proj_b"][sl]),
        "alogc": np.ascontiguousarray(alog.reshape(DIT, 128, NST).transpose(1, 0, 2).reshape(128, DIT * NST)),
        "dskc": _col128(w["D_skip"][sl]),
        "opw": np.ascontiguousarray(w["out_proj_w"][sl]),
        "oh32": oh32,
        "issecond": np.full((128, 1), 1.0 if dh == 1 else 0.0, np.float32),
    }


def _col128(v):
    return np.ascontiguousarray(np.asarray(v).reshape(-1, 128).T)


def _shard_x(x):
    shards = []
    for b in range(4):
        for h in range(2):
            xs = x[b, h * TL:(h + 1) * TL, :]
            shards.append(np.ascontiguousarray(xs.T))
    return shards


_NC_CACHE = {}

def _get_nc(name, builder):
    if name not in _NC_CACHE:
        _NC_CACHE[name] = builder()
    return _NC_CACHE[name]


def run_ffn0(resids, w):
    common = {
        "w1t": np.ascontiguousarray(np.asarray(w["w1_w"]).T),
        "w2t": np.ascontiguousarray(np.asarray(w["w2_w"]).T),
        "h1t": np.ascontiguousarray(np.asarray(w["h1_w"]).T),
        "h2t": np.ascontiguousarray(np.asarray(w["h2_w"]).T),
        "w3t": np.ascontiguousarray(np.asarray(w["w3_w"]).T),
        "s1": _col128(w["w1_s"]), "s2": _col128(w["w2_s"]),
        "sh1": _col128(w["h1_s"]), "sh2": _col128(w["h2_s"]),
        "s3": _col128(w["w3_s"]),
    }
    nc = _get_nc("ffn0", build_ffn0)
    in_maps = [dict(common, resid=np.ascontiguousarray(r)) for r in resids]
    res = run_bass_kernel_spmd(nc, in_maps, core_ids=list(range(NCORES)))
    return [res.results[i]["resid_out"] for i in range(NCORES)]


def run_moe(resids, w):
    w1 = np.ascontiguousarray(np.asarray(w["w1"]).reshape(NEXP * D, HID))
    w2 = np.ascontiguousarray(np.asarray(w["w2"]).reshape(NEXP * HID, D))
    b1 = np.asarray(w["b1"])
    b2 = np.asarray(w["b2"])
    common = {
        "w1": w1, "w2": w2,
        "rw": np.ascontiguousarray(np.asarray(w["router_w"])),
        "rb_bc": np.ascontiguousarray(np.broadcast_to(np.asarray(w["router_b"])[None, :], (128, NEXP))),
        "b1c": np.concatenate([_col128(b1[e]) for e in range(NEXP)], axis=1),
        "b2c": np.concatenate([_col128(b2[e]) for e in range(NEXP)], axis=1),
        "ident128": np.eye(128, dtype=np.float32),
        "onehot8": np.concatenate([np.eye(8, dtype=np.float32)[:, e:e + 1] * np.ones((1, 128), np.float32)
                                   for e in range(NEXP)], axis=1),
    }
    nc = _get_nc("moe", build_moe)
    in_maps = [dict(common, resid=np.ascontiguousarray(r)) for r in resids]
    res = run_bass_kernel_spmd(nc, in_maps, core_ids=list(range(NCORES)))
    return [res.results[i]["resid_out"] for i in range(NCORES)]


def kernel(**inputs):
    x = np.asarray(inputs["L0_attn_x"])
    resids = _shard_x(x)
    w_ffn0 = {k[len("L0_ffn_"):]: np.asarray(v) for k, v in inputs.items() if k.startswith("L0_ffn_")}
    w_attn0 = {k[len("L0_attn_"):]: np.asarray(v) for k, v in inputs.items()
               if k.startswith("L0_attn_") and k != "L0_attn_x"}
    w_moe = {k[len("L1_ffn_"):]: np.asarray(v) for k, v in inputs.items() if k.startswith("L1_ffn_")}
    w_mamba = {k[len("L1_attn_"):]: np.asarray(v) for k, v in inputs.items() if k.startswith("L1_attn_")}

    common = {}
    common.update(_ffn0_maps(w_ffn0))
    common.update(_attn0_maps(w_attn0))
    common.update(_moe_maps(w_moe))
    mam = [_mamba_maps(w_mamba, dh) for dh in range(2)]

    nc = _get_nc("fused", build_fused)
    in_maps = []
    for core in range(NCORES):
        dh = core % 2
        m = dict(common)
        m.update(mam[dh])
        m["resid"] = np.ascontiguousarray(resids[core])
        m["isfirst"] = np.full((128, 1), 1.0 if dh == 0 else 0.0, np.float32)
        in_maps.append(m)
    res = run_bass_kernel_spmd(nc, in_maps, core_ids=list(range(NCORES)))
    outs = [res.results[i]["resid_out"] for i in range(NCORES)]

    out = np.empty((4, 8192, D), np.float32)
    for b in range(4):
        for h in range(2):
            out[b, h * TL:(h + 1) * TL, :] = outs[2 * b + h].T
    return out


# version 05
# speedup vs baseline: 1.0317x; 1.0317x over previous
import numpy as np
import ml_dtypes
from contextlib import ExitStack

import concourse.bass as bass
import concourse.mybir as mybir
from concourse.bass_utils import run_bass_kernel_spmd

F32 = mybir.dt.float32
BF16 = mybir.dt.bfloat16
AF = mybir.ActivationFunctionType
ALU = mybir.AluOpType
AX = mybir.AxisListType

D = 1024
TL = 4096
TC = 512
NCH = TL // TC
DT = D // 128
HID = 4096
HT = HID // 128
NCORES = 8
LN_EPS = 1e-5

BF = ml_dtypes.bfloat16


class KB:
    ENGS = ("sync", "tensor", "scalar", "vector", "gpsimd")

    def __init__(self, nc, ctx):
        self.nc = nc
        self.ctx = ctx
        self.streams = {e: [] for e in self.ENGS}
        self.sems = {}
        self.cnt = {}
        self.guar = {e: {} for e in self.ENGS}
        self.pbtok = {}

    def sem(self, name):
        if name not in self.sems:
            self.sems[name] = self.ctx.enter_context(self.nc.semaphore(name))
            self.cnt[name] = 0
        return self.sems[name]

    def op(self, eng, fn, waits=(), inc=None, inc_n=1):
        need = []
        g = self.guar[eng]
        for w in waits:
            if w is None:
                continue
            s, v = w
            self.sem(s)
            if g.get(s, 0) < v:
                need.append((s, v))
                g[s] = v
        tok = None
        if inc is not None:
            self.sem(inc)
            self.cnt[inc] += inc_n
            tok = (inc, self.cnt[inc])
        sems = self.sems

        def closure(E, need=tuple(need), fn=fn, inc=inc, inc_n=inc_n):
            for (s, v) in need:
                E.wait_ge(sems[s], v)
            inst = fn(E)
            if inc is not None:
                inst.then_inc(sems[inc], inc_n)

        self.streams[eng].append(closure)
        return tok

    def tok(self, name):
        self.sem(name)
        return (name, self.cnt[name])

    def load(self, out, in_, waits=(), inc="ld", q="sync"):
        if q == "scalar":
            inc = "ls"
        return self.op(q, lambda E: E.dma_start(out=out, in_=in_),
                       waits=waits, inc=inc, inc_n=16)

    def store(self, out, in_, waits=(), inc="st"):
        return self.op("gpsimd", lambda E: E.dma_start(out=out, in_=in_),
                       waits=waits, inc=inc, inc_n=16)

    def finalize(self, block):
        for e in self.ENGS:
            fns = self.streams[e]
            if not fns:
                continue
            def body(E, fns=fns):
                for f in fns:
                    f(E)
            getattr(block, e)(body)


_PREP_SEQ = [0]

def emit_prep(kb, ctx, nc, pairs, tok_start=None):
    maxc = max(src.shape[1] for src, _ in pairs)
    _PREP_SEQ[0] += 1
    q = _PREP_SEQ[0]
    with ExitStack() as s:
        b32 = [s.enter_context(nc.sbuf_tensor(f"prep32_{q}_{i}", [128, maxc], F32)) for i in range(2)]
        bbf = [s.enter_context(nc.sbuf_tensor(f"prepbf_{q}_{i}", [128, maxc], BF16)) for i in range(2)]
        slot_free = [tok_start, tok_start]
        slot_out = [tok_start, tok_start]
        it = 0
        for src, dst in pairs:
            R, C = src.shape[0], src.shape[1]
            for r0 in range(0, R, 128):
                sl = it % 2
                t_ld = kb.load(b32[sl][:, 0:C], src[r0:r0 + 128, :], waits=[slot_free[sl]])
                t_act = kb.op("scalar",
                              lambda E, sl=sl, C=C: E.copy(out=bbf[sl][:, 0:C], in_=b32[sl][:, 0:C]),
                              waits=[t_ld, slot_out[sl]], inc="act")
                slot_free[sl] = t_act
                t_st = kb.store(dst[r0:r0 + 128, :], bbf[sl][:, 0:C], waits=[t_act])
                slot_out[sl] = t_st
                it += 1
        return kb.tok("st")


def emit_ln(kb, B, rs, tok_resid, n_dt=DT, tcw=TC):
    xn = B["xn_ap"]
    st = B["stats"]
    pb = B["pb"]
    sq = lambda p: B["sq"][:, p * tcw:(p + 1) * tcw]
    t32 = lambda p: B["t32"][:, p * tcw:(p + 1) * tcw]

    w01 = [tok_resid, kb.pbtok.get(0), kb.pbtok.get(1)]
    for i in range(n_dt):
        kb.op("tensor", lambda E, i=i: E.matmul(pb[0][0:1, 0:tcw], B["ones_col_f"][:, 0:1], rs(i),
                                                start=(i == 0), stop=(i == n_dt - 1)), waits=w01)
    sq_free = [None, None]
    t_sums = None
    for i in range(n_dt):
        p = i % 2
        ta = kb.op("scalar", lambda E, i=i, p=p: E.activation(out=sq(p), in_=rs(i), func=AF.Square),
                   waits=[tok_resid, sq_free[p]], inc="act")
        t_sums = kb.op("tensor", lambda E, i=i, p=p: E.matmul(pb[1][0:1, 0:tcw],
                                                              B["ones_col_f"][:, 0:1], sq(p),
                                                              start=(i == 0), stop=(i == n_dt - 1)),
                       waits=[ta] + (w01 if i == 0 else []), inc="pe")
        sq_free[p] = t_sums
    t_pe = kb.tok("pe")

    negm = st[0:1, 0:tcw]
    varq = st[0:1, tcw:2 * tcw]
    var = st[0:1, 2 * tcw:3 * tcw]
    rstd = st[0:1, 3 * tcw:4 * tcw]
    t1 = kb.op("vector", lambda E: E.tensor_scalar(out=negm, in0=pb[0][0:1, 0:tcw],
                                                   scalar1=-1.0 / D, scalar2=None, op0=ALU.mult),
               waits=[t_pe, B.get("stats_free")], inc="dve")
    t2 = kb.op("vector", lambda E: E.tensor_scalar(out=varq, in0=pb[1][0:1, 0:tcw],
                                                   scalar1=1.0 / D, scalar2=None, op0=ALU.mult),
               waits=[t_pe], inc="dve")
    t3 = kb.op("vector", lambda E: E.tensor_tensor(out=var, in0=negm, in1=negm, op=ALU.mult),
               waits=[t1], inc="dve")
    t4 = kb.op("vector", lambda E: E.tensor_tensor(out=var, in0=varq, in1=var, op=ALU.subtract),
               waits=[t2, t3], inc="dve")
    t5 = kb.op("scalar", lambda E: E.activation(out=varq, in_=var, func=AF.Ln, bias=LN_EPS),
               waits=[t4], inc="act")
    t6 = kb.op("scalar", lambda E: E.activation(out=rstd, in_=varq, func=AF.Exp, scale=-0.5),
               waits=[t5], inc="act")
    kb.pbtok[0] = t1
    kb.pbtok[1] = t2

    w23 = [t6, kb.pbtok.get(2), kb.pbtok.get(3)]
    kb.op("tensor", lambda E: E.matmul(pb[2][:, 0:tcw], B["ones_row_f"][0:1, :], negm,
                                       start=True, stop=True), waits=w23)
    t_bc = kb.op("tensor", lambda E: E.matmul(pb[3][:, 0:tcw], B["ones_row_f"][0:1, :], rstd,
                                              start=True, stop=True), waits=w23, inc="pe")
    B["stats_free"] = t_bc

    t32_free = [None, None]
    t_xmm = [None] * n_dt
    for i in range(n_dt):
        p = i % 2
        tA = kb.op("vector", lambda E, i=i, p=p: E.tensor_tensor(out=t32(p), in0=rs(i),
                                                                 in1=pb[2][:, 0:tcw], op=ALU.add),
                   waits=[t_bc, t32_free[p]], inc="dve")
        tB = kb.op("vector", lambda E, i=i, p=p: E.tensor_tensor(out=xn(i), in0=t32(p),
                                                                 in1=pb[3][:, 0:tcw], op=ALU.mult),
                   waits=[tA, B.get("xn_free")], inc="dve")
        t32_free[p] = tB
        if B.get("cb_xmm") is not None:
            t32_free[p] = B["cb_xmm"](i, t32(p), [tA])
        t_xmm[i] = t32_free[p]
    t_xn = kb.tok("dve")
    kb.pbtok[2] = t_xn
    kb.pbtok[3] = t_xn
    return t_xn


def emit_mm_stage(kb, B, wdram, Kt, Mt, rhs_fn, evac_fn, tok_rhs, mgs=8, N=TC, kg=4, q="sync"):
    pb = B["pb"]
    wbuf = B["wbuf"]
    WSLOTS = B["wslots"]
    WCOLS = B["wcols"]
    slot_tok = B["wslot_tok"]

    for mg in range((Mt + mgs - 1) // mgs):
        m0 = mg * mgs
        nm = min(mgs, Mt - m0)
        bank_waits = [kb.pbtok.get(j) for j in range(nm)]
        for kg0 in range(0, Kt, kg):
            ng = min(kg, Kt - kg0)
            sl = B["wslot_i"] % WSLOTS
            B["wslot_i"] += 1
            src = wdram[kg0 * 128:(kg0 + ng) * 128, m0 * 128:(m0 + nm) * 128]
            src = src.rearrange("(g p) c -> p g c", p=128)
            dst = wbuf[:, sl * WCOLS: sl * WCOLS + ng * nm * 128]
            dst = dst.rearrange("p (g c) -> p g c", g=ng)
            t_ld = kb.load(dst, src, waits=[slot_tok[sl]], q=q)
            for gi in range(ng):
                k = kg0 + gi
                mmw = [t_ld, tok_rhs] + (bank_waits if k == 0 else [])
                for j in range(nm):
                    last = j == nm - 1
                    off = sl * WCOLS + gi * nm * 128 + j * 128
                    kb.op("tensor",
                          lambda E, off=off, j=j, k=k: E.matmul(
                              pb[j][:, 0:N], wbuf[:, off:off + 128],
                              rhs_fn(k), start=(k == 0), stop=(k == Kt - 1)),
                          waits=mmw, inc="pe" if last else None)
            slot_tok[sl] = kb.tok("pe")
        t_mm = kb.tok("pe")
        for j in range(nm):
            t_ev = evac_fn(m0 + j, pb[j], [t_mm])
            kb.pbtok[j] = t_ev


def emit_ffn0_chunk(kb, B, c, scr, resid_in, resid_out, tok_prev_store):
    rs = lambda i: B["resid"][:, i * TC:(i + 1) * TC]
    tok_resid = kb.load(B["resid"][:, 0:DT * TC].rearrange("p (g c) -> p g c", g=DT),
                        resid_in[:, c * TC:(c + 1) * TC].rearrange("(g p) c -> p g c", p=128),
                        waits=[tok_prev_store])
    t_xn = emit_ln(kb, B, rs, tok_resid)
    xn = B["xn_ap"]
    hA = lambda m: B["hA"][:, m * TC:(m + 1) * TC]
    hB = lambda m: B["hB"][:, m * TC:(m + 1) * TC]
    sil = lambda j: B["sil"][:, (j % 2) * TC:((j % 2) + 1) * TC]
    pb = B["pb"]
    wbuf = B["wbuf"]
    WSLOTS = B["wslots"]
    WCOLS = B["wcols"]
    slot_tok = B["wslot_tok"]

    for mg in range(HT // 4):
        m0 = mg * 4
        bank_waits = [kb.pbtok.get(j) for j in range(8)]
        sl = B["wslot_i"] % WSLOTS
        B["wslot_i"] += 1
        src1 = scr["w1t"][:, m0 * 128:(m0 + 4) * 128].rearrange("(g p) c -> p g c", p=128)
        dst1 = wbuf[:, sl * WCOLS: sl * WCOLS + DT * 512].rearrange("p (g c) -> p g c", g=DT)
        t_ld1 = kb.load(dst1, src1, waits=[slot_tok[sl]])
        src2 = scr["w2t"][:, m0 * 128:(m0 + 4) * 128].rearrange("(g p) c -> p g c", p=128)
        dst2 = wbuf[:, sl * WCOLS + DT * 512: sl * WCOLS + 2 * DT * 512].rearrange("p (g c) -> p g c", g=DT)
        t_ld2 = kb.load(dst2, src2, waits=[slot_tok[sl]], q="scalar")
        for k in range(DT):
            mmw = [t_ld1, t_ld2, t_xn] + (bank_waits if k == 0 else [])
            for j in range(8):
                off = sl * WCOLS + (512 * DT if j >= 4 else 0) + k * 512 + (j % 4) * 128
                last = j == 7
                kb.op("tensor",
                      lambda E, off=off, k=k, j=j: E.matmul(
                          pb[j][:, 0:TC], wbuf[:, off:off + 128], xn(k),
                          start=(k == 0), stop=(k == DT - 1)),
                      waits=mmw, inc="pe" if last else None)
        slot_tok[sl] = kb.tok("pe")
        t_mm = kb.tok("pe")
        for j in range(4):
            m = m0 + j
            ta = kb.op("scalar",
                       lambda E, j=j, m=m: E.activation(out=sil(j), in_=pb[j][:, 0:TC],
                                                        func=AF.Silu, scale=B["s1"][:, m:m + 1]),
                       waits=[t_mm, B["sil_free"][j % 2]], inc="act")
            tv = kb.op("vector",
                       lambda E, j=j, m=m: E.scalar_tensor_tensor(
                           out=hA(m), in0=pb[4 + j][:, 0:TC], scalar=B["s2"][:, m:m + 1],
                           in1=sil(j), op0=ALU.mult, op1=ALU.mult),
                       waits=[t_mm, ta], inc="dve")
            B["sil_free"][j % 2] = tv
            kb.pbtok[j] = tv
            kb.pbtok[4 + j] = tv
    t_hA = kb.tok("dve")

    def ev_h1(m, bank, w):
        return kb.op("scalar",
                     lambda E, m=m, bank=bank: E.activation(out=hB(m), in_=bank[:, 0:TC],
                                                            func=AF.Gelu, scale=B["sh1"][:, m:m + 1]),
                     waits=w, inc="act")
    emit_mm_stage(kb, B, scr["h1t"], HT, HT, hA, ev_h1, t_hA, kg=4)
    t_hB = kb.tok("act")

    def ev_h2(m, bank, w):
        return kb.op("scalar",
                     lambda E, m=m, bank=bank: E.activation(out=hA(m), in_=bank[:, 0:TC],
                                                            func=AF.Gelu, scale=B["sh2"][:, m:m + 1]),
                     waits=w, inc="act")
    emit_mm_stage(kb, B, scr["h2t"], HT, HT, hB, ev_h2, t_hB, kg=4, q="scalar")
    t_hA2 = kb.tok("act")

    def ev_w3(m, bank, w):
        return kb.op("vector",
                     lambda E, m=m, bank=bank: E.scalar_tensor_tensor(
                         out=rs(m), in0=bank[:, 0:TC], scalar=B["s3h"][:, m:m + 1],
                         in1=rs(m), op0=ALU.mult, op1=ALU.add),
                     waits=w + [tok_resid], inc="dve")
    emit_mm_stage(kb, B, scr["w3t"], HT, DT, hA, ev_w3, t_hA2, kg=4)
    t_done = kb.tok("dve")
    t_st = None
    for i in range(DT):
        t_st = kb.store(resid_out[i * 128:(i + 1) * 128, c * TC:(c + 1) * TC], rs(i),
                        waits=[t_done])
    return t_st


FCONSTS = [LN_EPS, -0.5, 1.0, -1.0, 0.5, 2.0, 256.0, -80.0, 0.0]


def alloc_common(kb, ctx, nc, B):
    sb = lambda n, s, d=F32: ctx.enter_context(nc.sbuf_tensor(n, s, d))
    B["resid"] = sb("resid_sb", [128, DT * TC])
    B["sq"] = sb("sq", [128, 2 * TC])
    B["t32"] = sb("t32", [128, 2 * TC])
    B["stats"] = sb("stats", [128, 4 * TC])
    B["ones_col_f"] = sb("ones_col_f", [128, 1])
    B["ones_row_f"] = sb("ones_row_f", [1, 128])
    B["pb"] = [ctx.enter_context(nc.psum_tensor(f"pb{i}", [128, 512], F32)) for i in range(B.get("npb", 8))]
    B["fconst"] = sb("fconst", [128, len(FCONSTS)])
    for j, v in enumerate(FCONSTS):
        kb.op("vector", lambda E, j=j, v=v: E.memset(B["fconst"][:, j:j + 1], v))
        nc.const_aps.register(F32, v, B["fconst"][:, j:j + 1])
    kb.op("vector", lambda E: E.memset(B["ones_col_f"][:], 1.0))
    kb.op("vector", lambda E: E.memset(B["ones_row_f"][:], 1.0), inc="dve")
    B["tok_const"] = kb.tok("dve")
    for e in ("sync", "tensor", "scalar", "gpsimd"):
        kb.op(e, lambda E: E.nop(), waits=[B["tok_const"]])
    B["wslot_i"] = 0
    B["sil_free"] = [None] * 2
    B["cb_xmm"] = None
    B["xn_free"] = None


def declare_ffn0_ins(nc):
    return {
        "w1t": nc.dram_tensor("w1t", [D, HID], F32, kind="ExternalInput").ap(),
        "w2t": nc.dram_tensor("w2t", [D, HID], F32, kind="ExternalInput").ap(),
        "h1t": nc.dram_tensor("h1t", [HID, HID], F32, kind="ExternalInput").ap(),
        "h2t": nc.dram_tensor("h2t", [HID, HID], F32, kind="ExternalInput").ap(),
        "w3t": nc.dram_tensor("w3t", [HID, D], F32, kind="ExternalInput").ap(),
        "s1": nc.dram_tensor("s1", [128, HT], F32, kind="ExternalInput").ap(),
        "s2": nc.dram_tensor("s2", [128, HT], F32, kind="ExternalInput").ap(),
        "sh1": nc.dram_tensor("sh1", [128, HT], F32, kind="ExternalInput").ap(),
        "sh2": nc.dram_tensor("sh2", [128, HT], F32, kind="ExternalInput").ap(),
        "s3": nc.dram_tensor("s3", [128, DT], F32, kind="ExternalInput").ap(),
    }


def declare_ffn0_scr(nc):
    return {
        "w1t": nc.dram_tensor("w1t_bf", [D, HID], BF16).ap(),
        "w2t": nc.dram_tensor("w2t_bf", [D, HID], BF16).ap(),
        "h1t": nc.dram_tensor("h1t_bf", [HID, HID], BF16).ap(),
        "h2t": nc.dram_tensor("h2t_bf", [HID, HID], BF16).ap(),
        "w3t": nc.dram_tensor("w3t_bf", [HID, D], BF16).ap(),
    }


def emit_ffn0_all(kb, ctx, nc, B0, A, scr, resid_in, resid_out, do_prep, tag, tok_start):
    with ExitStack() as s:
        al = lambda n, sh, d=F32: s.enter_context(nc.sbuf_tensor(n + tag, sh, d))
        pb7 = s.enter_context(nc.psum_tensor("pb7" + tag, [128, 512], F32))
        B = dict(B0)
        B["pb"] = list(B0["pb"]) + [pb7]
        B["s1"] = al("s1t", [128, HT]); B["s2"] = al("s2t", [128, HT])
        B["sh1"] = al("sh1t", [128, HT]); B["sh2"] = al("sh2t", [128, HT])
        B["s3h"] = al("s3h", [128, DT])
        s3t = al("s3t", [128, DT])

        kb.load(B["s1"][:], A["s1"], waits=[tok_start])
        kb.load(B["s2"][:], A["s2"], waits=[tok_start])
        kb.load(B["sh1"][:], A["sh1"], waits=[tok_start])
        kb.load(B["sh2"][:], A["sh2"], waits=[tok_start])
        t_s3 = kb.load(s3t[:], A["s3"], waits=[tok_start])
        kb.op("vector", lambda E, s3t=s3t, B=B: E.tensor_scalar(
            out=B["s3h"][:], in0=s3t[:], scalar1=0.5, scalar2=None, op0=ALU.mult),
            waits=[t_s3], inc="dve")

        if do_prep:
            emit_prep(kb, ctx, nc, [(A["w1t"], scr["w1t"]), (A["w2t"], scr["w2t"]),
                                    (A["h1t"], scr["h1t"]), (A["h2t"], scr["h2t"]),
                                    (A["w3t"], scr["w3t"])], tok_start=tok_start)
        t_prep = kb.tok("st")

        B["hA"] = al("hA", [128, HT * TC], BF16)
        B["hB"] = al("hB", [128, HT * TC], BF16)
        B["sil"] = al("sil", [128, 2 * TC], BF16)
        B["wbuf"] = al("wbuf", [128, 3 * 8192], BF16)
        B["wslots"] = 3
        B["wcols"] = 8192
        B["wslot_tok"] = [t_prep] * 3
        B["wslot_i"] = 0
        B["sil_free"] = [None] * 2
        B["xn_free"] = None
        B["cb_xmm"] = None
        B["xn_ap"] = lambda i, B=B: B["hB"][:, i * TC:(i + 1) * TC]

        t_st = tok_start
        for c in range(NCH):
            t_st = emit_ffn0_chunk(kb, B, c, scr, resid_in, resid_out, t_st)
        B0["stats_free"] = B.get("stats_free")
        return t_st


def build_ffn0():
    nc = bass.Bass("TRN2", target_bir_lowering=False)
    resid_in = nc.dram_tensor("resid", [D, TL], F32, kind="ExternalInput").ap()
    A = declare_ffn0_ins(nc)
    resid_out = nc.dram_tensor("resid_out", [D, TL], F32, kind="ExternalOutput").ap()
    scr = declare_ffn0_scr(nc)
    with ExitStack() as ctx:
        kb = KB(nc, ctx)
        B0 = {"nc": nc, "npb": 7}
        alloc_common(kb, ctx, nc, B0)
        t_st = emit_ffn0_all(kb, ctx, nc, B0, A, scr, resid_in, resid_out, True, "_f0", None)
        kb.op("gpsimd", lambda E: E.engine_nop(), waits=[t_st])
        block = ctx.enter_context(nc.Block())
        kb.finalize(block)
    return nc


NEXP = 8
import os
def NCH_MOE():
    return int(os.environ.get("NCH_MOE", NCH))

def emit_moe_chunk(kb, B, c, scr, resid_in, resid_out, tok_prev_store):
    rs = lambda i: B["resid"][:, i * TC:(i + 1) * TC]
    tok_resid = kb.load(B["resid"][:, 0:DT * TC].rearrange("p (g c) -> p g c", g=DT),
                        resid_in[:, c * TC:(c + 1) * TC].rearrange("(g p) c -> p g c", p=128),
                        waits=[tok_prev_store])
    xn32 = lambda i: B["xn32"][:, i * TC:(i + 1) * TC]
    xn = B["xn_ap"]

    def cb(i, t32ap, w):
        tA = kb.op("vector", lambda E, i=i, t32ap=t32ap: E.tensor_tensor(
            out=xn32(i), in0=t32ap, in1=B["pb"][3][:, 0:TC], op=ALU.mult),
            waits=w + [B.get("xn32_free")], inc="dve")
        kb.op("scalar", lambda E, i=i: E.copy(out=xn(i), in_=xn32(i)), waits=[tA], inc="act")
        return tA
    B["cb_xmm"] = cb
    emit_ln(kb, B, rs, tok_resid)
    B["cb_xmm"] = None
    t_xn32 = kb.tok("dve")
    t_xn = kb.tok("act")

    pb = B["pb"]
    lgs = lambda s: B["lg"][:, s * 8:(s + 1) * 8]
    t_lg = []
    for s in range(4):
        w = [t_xn32, kb.pbtok.get(6)]
        for k in range(DT):
            kb.op("tensor", lambda E, s=s, k=k: E.matmul(
                pb[6][:, 0:8], xn32(k)[:, s * 128:(s + 1) * 128],
                B["rwk"][:, k * NEXP:(k + 1) * NEXP],
                start=(k == 0), stop=(k == DT - 1)),
                waits=w, inc="pe" if k == DT - 1 else None)
        t_pe = kb.tok("pe")
        tl = kb.op("vector", lambda E, s=s: E.tensor_tensor(out=lgs(s), in0=pb[6][:, 0:8],
                                                            in1=B["rb_bc"][:], op=ALU.add),
                   waits=[t_pe], inc="dve")
        kb.pbtok[6] = tl
        t_lg.append(tl)
    B["xn32_free"] = kb.tok("pe")

    gt = lambda s: B["gt"][:, s * 8:(s + 1) * 8]
    m1 = B["m1"]; m2 = B["m2"]; msk1 = B["msk1"]; msk2 = B["msk2"]; g2c = B["g2c"]
    t_gates = []
    for s in range(4):
        t1 = kb.op("vector", lambda E, s=s: E.reduce_max(out=m1[:], in_=lgs(s), axis=AX.X),
                   waits=[t_lg[s], B.get("topk_free")], inc="dve")
        t2 = kb.op("vector", lambda E, s=s: E.tensor_scalar(out=msk1[:], in0=lgs(s), scalar1=m1[:, 0:1],
                                                            scalar2=None, op0=ALU.is_equal),
                   waits=[t1], inc="dve")
        t3 = kb.op("vector", lambda E, s=s: E.tensor_scalar(out=msk2[:], in0=msk1[:], scalar1=-1e30,
                                                            scalar2=None, op0=ALU.mult),
                   waits=[t2], inc="dve")
        t4 = kb.op("vector", lambda E, s=s: E.tensor_tensor(out=msk2[:], in0=msk2[:], in1=lgs(s),
                                                            op=ALU.add),
                   waits=[t3], inc="dve")
        t5 = kb.op("vector", lambda E, s=s: E.reduce_max(out=m2[:], in_=msk2[:], axis=AX.X),
                   waits=[t4], inc="dve")
        t6 = kb.op("vector", lambda E, s=s: E.tensor_scalar(out=msk2[:], in0=msk2[:], scalar1=m2[:, 0:1],
                                                            scalar2=None, op0=ALU.is_equal),
                   waits=[t5], inc="dve")
        t7 = kb.op("vector", lambda E: E.tensor_tensor(out=m2[:], in0=m2[:], in1=m1[:], op=ALU.subtract),
                   waits=[t6], inc="dve")
        t8 = kb.op("scalar", lambda E: E.activation(out=g2c[:], in_=m2[:], func=AF.Sigmoid),
                   waits=[t7], inc="act")
        t9 = kb.op("vector", lambda E: E.tensor_scalar(out=m1[:], in0=g2c[:], scalar1=-1.0,
                                                       scalar2=1.0, op0=ALU.mult, op1=ALU.add),
                   waits=[t8], inc="dve")
        tA = kb.op("vector", lambda E, s=s: E.tensor_scalar(out=gt(s), in0=msk1[:], scalar1=m1[:, 0:1],
                                                            scalar2=None, op0=ALU.mult),
                   waits=[t9], inc="dve")
        tB = kb.op("vector", lambda E, s=s: E.scalar_tensor_tensor(out=gt(s), in0=msk2[:],
                                                                   scalar=g2c[:, 0:1], in1=gt(s),
                                                                   op0=ALU.mult, op1=ALU.add),
                   waits=[tA], inc="dve")
        B["topk_free"] = tB
        t_gates.append(tB)

    if c == 0 and B.get("dbg_lg") is not None:
        kb.store(B["dbg_lg"], B["lg"][:], waits=t_gates)
    w = [kb.pbtok.get("gt")] + t_gates
    for s in range(4):
        kb.op("tensor", lambda E, s=s: E.transpose(B["pbT"][0:8, s * 128:(s + 1) * 128],
                                                   gt(s), B["ident128"][:]),
              waits=w, inc="pe" if s == 3 else None)
    t_tp = kb.tok("pe")
    t_gT = kb.op("vector", lambda E: E.tensor_copy(out=B["gatesT"][:], in_=B["pbT"][0:8, 0:512]),
                 waits=[t_tp], inc="dve")
    kb.pbtok["gt"] = t_gT
    gbc = lambda e: B["gbc"][:, e * TC:(e + 1) * TC]
    for e in range(NEXP):
        bank = e % 4
        tm = kb.op("tensor", lambda E, e=e, bank=bank: E.matmul(
            pb[bank][:, 0:TC], B["onehot8"][0:8, e * 128:(e + 1) * 128], B["gatesT"][:],
            start=True, stop=True),
            waits=[t_gT, kb.pbtok.get(bank)], inc="pe")
        te = kb.op("vector", lambda E, e=e, bank=bank: E.tensor_scalar(
            out=gbc(e), in0=pb[bank][:, 0:TC], scalar1=0.5, scalar2=None, op0=ALU.mult),
            waits=[tm], inc="dve")
        kb.pbtok[bank] = te

    h = lambda m: B["hA"][:, m * TC:(m + 1) * TC]
    t_res_upd = None
    for e in range(NEXP):
        def ev_w1(m, bank, w, e=e):
            return kb.op("scalar", lambda E, m=m, bank=bank: E.activation(
                out=h(m), in_=bank[:, 0:TC], func=AF.Gelu, bias=B["b1c"][:, e * HT + m: e * HT + m + 1]),
                waits=w + [B["h_free"][m]], inc="act")
        emit_mm_stage(kb, B, scr["w1"][e * D:(e + 1) * D, :], DT, HT, xn, ev_w1, t_xn, mgs=4, kg=4)
        t_h = kb.tok("act")
        def ev_w2(m, bank, w, e=e):
            p = m % 2
            tA = kb.op("vector", lambda E, m=m, bank=bank, e=e, p=p: E.scalar_tensor_tensor(
                out=B["t32"][:, p * TC:(p + 1) * TC], in0=bank[:, 0:TC],
                scalar=B["b2c"][:, e * DT + m: e * DT + m + 1], in1=gbc(e),
                op0=ALU.add, op1=ALU.mult),
                waits=w + [B["t32m_free"][p]], inc="dve")
            tB = kb.op("vector", lambda E, m=m, p=p: E.tensor_tensor(
                out=rs(m), in0=B["t32"][:, p * TC:(p + 1) * TC], in1=rs(m), op=ALU.add),
                waits=[tA, tok_resid], inc="dve")
            B["t32m_free"][p] = tB
            return tB
        emit_mm_stage(kb, B, scr["w2"][e * HID:(e + 1) * HID, :], HT, DT, h, ev_w2, t_h, mgs=4, kg=4, q="scalar")
        t_w2mm = kb.tok("pe")
        for m in range(HT):
            B["h_free"][m] = t_w2mm
        t_res_upd = kb.tok("dve")

    t_st = kb.store(resid_out[:, c * TC:(c + 1) * TC].rearrange("(g p) c -> p g c", p=128),
                    B["resid"][:, 0:DT * TC].rearrange("p (g c) -> p g c", g=DT),
                    waits=[t_res_upd])
    return t_st


def declare_moe_ins(nc):
    return {
        "w1": nc.dram_tensor("w1", [NEXP * D, HID], F32, kind="ExternalInput").ap(),
        "w2": nc.dram_tensor("w2", [NEXP * HID, D], F32, kind="ExternalInput").ap(),
        "rw": nc.dram_tensor("rw", [D, NEXP], F32, kind="ExternalInput").ap(),
        "rb_bc": nc.dram_tensor("rb_bc", [128, NEXP], F32, kind="ExternalInput").ap(),
        "b1c": nc.dram_tensor("b1c", [128, NEXP * HT], F32, kind="ExternalInput").ap(),
        "b2c": nc.dram_tensor("b2c", [128, NEXP * DT], F32, kind="ExternalInput").ap(),
        "ident128": nc.dram_tensor("ident128", [128, 128], F32, kind="ExternalInput").ap(),
        "onehot8": nc.dram_tensor("onehot8", [8, NEXP * 128], F32, kind="ExternalInput").ap(),
    }


def declare_moe_scr(nc):
    return {
        "w1": nc.dram_tensor("w1_bf", [NEXP * D, HID], BF16).ap(),
        "w2": nc.dram_tensor("w2_bf", [NEXP * HID, D], BF16).ap(),
    }


def emit_moe_all(kb, ctx, nc, B0, A, scr, resid_in, resid_out, do_prep, tag, tok_start):
    with ExitStack() as s:
        al = lambda n, sh, d=F32: s.enter_context(nc.sbuf_tensor(n + tag, sh, d))
        B = dict(B0)
        B["pbT"] = s.enter_context(nc.psum_tensor("pbT" + tag, [8, 512], F32))
        B["rb_bc"] = al("rbbc", [128, NEXP])
        B["b1c"] = al("b1ct", [128, NEXP * HT])
        B["b2c"] = al("b2ct", [128, NEXP * DT])
        B["ident128"] = al("ident128t", [128, 128])
        B["onehot8"] = al("onehot8t", [8, NEXP * 128])
        B["rwk"] = al("rwk", [128, DT * NEXP])

        kb.load(B["rb_bc"][:], A["rb_bc"], waits=[tok_start])
        kb.load(B["b1c"][:], A["b1c"], waits=[tok_start])
        kb.load(B["b2c"][:], A["b2c"], waits=[tok_start])
        kb.load(B["ident128"][:], A["ident128"], waits=[tok_start])
        kb.load(B["onehot8"][:], A["onehot8"], waits=[tok_start])
        for k in range(DT):
            kb.load(B["rwk"][:, k * NEXP:(k + 1) * NEXP], A["rw"][k * 128:(k + 1) * 128, :],
                    waits=[tok_start])

        if do_prep:
            emit_prep(kb, ctx, nc, [(A["w1"], scr["w1"]), (A["w2"], scr["w2"])], tok_start=tok_start)
        t_prep = kb.tok("st")

        B["hA"] = al("hA", [128, HT * TC], BF16)
        B["xn"] = al("xn", [128, DT * TC], BF16)
        B["xn32"] = al("xn32", [128, DT * TC])
        B["wbuf"] = al("wbuf", [128, 4 * 2048], BF16)
        B["wslots"] = 4
        B["wcols"] = 2048
        B["wslot_tok"] = [t_prep] * 4
        B["wslot_i"] = 0
        B["xn_ap"] = lambda i, B=B: B["xn"][:, i * TC:(i + 1) * TC]
        B["lg"] = al("lg", [128, 4 * 8])
        B["gt"] = al("gt", [128, 4 * 8])
        B["m1"] = al("m1", [128, 1]); B["m2"] = al("m2", [128, 1])
        B["msk1"] = al("msk1", [128, 8]); B["msk2"] = al("msk2", [128, 8])
        B["g2c"] = al("g2c", [128, 1])
        B["gatesT"] = al("gatesT", [8, 512])
        B["gbc"] = al("gbc", [128, NEXP * TC])
        B["h_free"] = [None] * HT
        B["t32m_free"] = [None, None]
        B["xn32_free"] = None
        B["topk_free"] = None
        B["xn_free"] = None

        t_st = tok_start
        for c in range(NCH_MOE()):
            t_st = emit_moe_chunk(kb, B, c, scr, resid_in, resid_out, t_st)
        B0["stats_free"] = B.get("stats_free")
        return t_st


def build_moe():
    nc = bass.Bass("TRN2", target_bir_lowering=False)
    resid_in = nc.dram_tensor("resid", [D, TL], F32, kind="ExternalInput").ap()
    A = declare_moe_ins(nc)
    resid_out = nc.dram_tensor("resid_out", [D, TL], F32, kind="ExternalOutput").ap()
    scr = declare_moe_scr(nc)
    with ExitStack() as ctx:
        kb = KB(nc, ctx)
        B0 = {"nc": nc, "npb": 7}
        alloc_common(kb, ctx, nc, B0)
        t_st = emit_moe_all(kb, ctx, nc, B0, A, scr, resid_in, resid_out, True, "_m0", None)
        kb.op("gpsimd", lambda E: E.engine_nop(), waits=[t_st])
        block = ctx.enter_context(nc.Block())
        kb.finalize(block)
    return nc


LA = 256
NCA = TL // LA

def emit_attn0_prelude(kb, B):
    pb = B["pb"]
    t0 = kb.op("scalar", lambda E: E.activation(out=B["dsc"][0:1, 0:1], in_=B["dlg"][0:1, 0:1],
                                                func=AF.Sigmoid), waits=[B["tok_in"]], inc="act")
    t1 = kb.op("scalar", lambda E: E.activation(out=B["dsc"][0:1, 1:2], in_=B["dsc"][0:1, 0:1],
                                                func=AF.Ln), waits=[t0], inc="act")
    t2 = kb.op("vector", lambda E: E.tensor_scalar(out=B["dsc"][0:1, 2:3], in0=B["dsc"][0:1, 1:2],
                                                   scalar1=-80.0, scalar2=None, op0=ALU.max),
               waits=[t1], inc="dve")
    lnd = B["dsc"][0:1, 2:3]
    t3 = kb.op("tensor", lambda E: E.matmul(pb[0][:, 0:1], B["ones_row_f"][0:1, :], lnd,
                                            start=True, stop=True), waits=[t2, kb.pbtok.get(0)])
    t3 = kb.op("tensor", lambda E: E.matmul(pb[0][:, 1:2], B["ones_row_f"][0:1, :],
                                            B["osc"][0:1, 0:1], start=True, stop=True),
               waits=[B["tok_in"]], inc="pe")
    t4 = kb.op("vector", lambda E: E.tensor_copy(out=B["lnd_b"][:], in_=pb[0][:, 0:1]),
               waits=[t3], inc="dve")
    t5 = kb.op("vector", lambda E: E.tensor_copy(out=B["os_b"][:], in_=pb[0][:, 1:2]),
               waits=[t3], inc="dve")
    kb.pbtok[0] = t5

    t6 = kb.op("scalar", lambda E: E.activation(out=B["dL_b"][:], in_=B["lnd_b"][:],
                                                func=AF.Exp, scale=256.0), waits=[t4], inc="act")
    t7 = kb.op("scalar", lambda E: E.activation(out=B["dLpows_b"][:], in_=B["jE"][:],
                                                func=AF.Exp, scale=B["lnd_b"][:, 0:1]),
               waits=[t4, B["tok_in"]], inc="act")
    t8 = kb.op("scalar", lambda E: E.activation(out=B["dpows_b"][:], in_=B["colE"][:],
                                                func=AF.Exp, scale=B["lnd_b"][:, 0:1]),
               waits=[t4, B["tok_in"]], inc="act")
    tm = []
    for j in range(2):
        ta = kb.op("scalar", lambda E, j=j: E.activation(
            out=B["maskT"][:, j * LA:(j + 1) * LA], in_=B["ET"][:, j * LA:(j + 1) * LA],
            func=AF.Exp, scale=B["lnd_b"][:, 0:1]), waits=[t4, B["tok_in"]], inc="act")
        tb = kb.op("vector", lambda E, j=j: E.tensor_tensor(
            out=B["maskT"][:, j * LA:(j + 1) * LA], in0=B["maskT"][:, j * LA:(j + 1) * LA],
            in1=B["C01"][:, j * LA:(j + 1) * LA], op=ALU.mult), waits=[ta], inc="dve")
        tm.append(tb)
    t9 = kb.op("scalar", lambda E: E.activation(out=B["dqrow"][0:1, :], in_=B["rowq"][0:1, :],
                                                func=AF.Exp, scale=lnd), waits=[t2], inc="act")
    ta = kb.op("tensor", lambda E: E.matmul(pb[1][:, 0:LA], B["ones_row_f"][0:1, :],
                                            B["dqrow"][0:1, :], start=True, stop=True),
               waits=[t9, kb.pbtok.get(1)], inc="pe")
    tb = kb.op("vector", lambda E: E.tensor_copy(out=B["dqrow_b"][:], in_=pb[1][:, 0:LA]),
               waits=[ta], inc="dve")
    kb.pbtok[1] = tb

    for name, coefT in (("qw", "qcT"), ("kw", "kcT"), ("vw", "vcT")):
        for m in range(DT):
            bank = m % 2
            tmm = kb.op("tensor", lambda E, m=m, coefT=coefT, bank=bank: E.matmul(
                pb[2 + bank][:, 0:256], B["basisT"][:, m * 128:(m + 1) * 128], B[coefT][:],
                start=True, stop=True), waits=[B["tok_in"], kb.pbtok.get(2 + bank)], inc="pe")
            tev = kb.op("vector", lambda E, m=m, name=name, bank=bank: E.tensor_copy(
                out=B[name][:, m * 256:(m + 1) * 256], in_=pb[2 + bank][:, 0:256]),
                waits=[tmm], inc="dve")
            kb.pbtok[2 + bank] = tev
    for m2 in range(2):
        for h in range(2):
            bank = 4 + (m2 * 2 + h) % 2
            tmm = kb.op("tensor", lambda E, m2=m2, h=h, bank=bank: E.matmul(
                pb[bank][:, 0:512], B["ocT"][:, m2 * 128:(m2 + 1) * 128],
                B["basisT"][:, h * 512:(h + 1) * 512],
                start=True, stop=True), waits=[B["tok_in"], t5, kb.pbtok.get(bank)], inc="pe")
            tev = kb.op("vector", lambda E, m2=m2, h=h, bank=bank: E.tensor_scalar(
                out=B["owT"][:, m2 * 1024 + h * 512: m2 * 1024 + (h + 1) * 512],
                in0=pb[bank][:, 0:512], scalar1=B["os_b"][:, 0:1], scalar2=None, op0=ALU.mult),
                waits=[tmm], inc="dve")
            kb.pbtok[bank] = tev
    kb.op("vector", lambda E: E.memset(B["S_after"][:], 0.0), inc="dve")
    B["tok_pre"] = kb.tok("dve")


def emit_attn0_chunk(kb, B, c, resid_in, resid_out, tok_prev):
    pb = B["pb"]
    rs = lambda i: B["resid"][:, i * LA:(i + 1) * LA]
    tok_resid = None
    for i in range(DT):
        tok_resid = kb.load(rs(i), resid_in[i * 128:(i + 1) * 128, c * LA:(c + 1) * LA],
                            waits=[tok_prev, B["tok_pre"]])
    t_xn = emit_ln(kb, B, rs, tok_resid, tcw=LA)
    xn = B["xn_ap"]

    tSbf = kb.op("scalar", lambda E: E.copy(out=B["S_bf"][:], in_=B["S_after"][:]),
                 waits=[B["tok_S"]], inc="act")

    for m2 in range(2):
        w = [t_xn, kb.pbtok.get(m2)]
        for k in range(DT):
            kb.op("tensor", lambda E, m2=m2, k=k: E.matmul(
                pb[m2][:, 0:LA], B["qw"][:, k * 256 + m2 * 128: k * 256 + (m2 + 1) * 128], xn(k),
                start=(k == 0), stop=(k == DT - 1)),
                waits=w, inc="pe" if k == DT - 1 else None)
        t_mm = kb.tok("pe")
        t1 = kb.op("vector", lambda E, m2=m2: E.tensor_copy(
            out=B["q_bf"][:, m2 * LA:(m2 + 1) * LA], in_=pb[m2][:, 0:LA]), waits=[t_mm], inc="dve")
        t2 = kb.op("vector", lambda E, m2=m2: E.tensor_tensor(
            out=B["qt_bf"][:, m2 * LA:(m2 + 1) * LA], in0=pb[m2][:, 0:LA], in1=B["dqrow_b"][:],
            op=ALU.mult), waits=[t_mm], inc="dve")
        t3 = kb.op("vector", lambda E, m2=m2, c=c: E.scalar_tensor_tensor(
            out=B["qd"][:, m2 * TL + c * LA: m2 * TL + (c + 1) * LA], in0=pb[m2][:, 0:LA],
            scalar=B["dLpows_b"][:, (NCA - 1 - c):(NCA - c)], in1=B["dqrow_b"][:],
            op0=ALU.mult, op1=ALU.mult), waits=[t_mm], inc="dve")
        kb.pbtok[m2] = t3
    for m2 in range(2):
        w = [t_xn, kb.pbtok.get(2 + m2)]
        for k in range(DT):
            kb.op("tensor", lambda E, m2=m2, k=k: E.matmul(
                pb[2 + m2][:, 0:LA], B["kw"][:, k * 256 + m2 * 128: k * 256 + (m2 + 1) * 128], xn(k),
                start=(k == 0), stop=(k == DT - 1)),
                waits=w, inc="pe" if k == DT - 1 else None)
        t_mm = kb.tok("pe")
        t1 = kb.op("vector", lambda E, m2=m2: E.tensor_copy(
            out=B["k_bf"][:, m2 * LA:(m2 + 1) * LA], in_=pb[2 + m2][:, 0:LA]), waits=[t_mm], inc="dve")
        kb.pbtok[2 + m2] = t1

    for s2 in range(2):
        w = [t_xn, kb.pbtok.get(4 + s2)]
        for k in range(DT):
            kb.op("tensor", lambda E, s2=s2, k=k: E.matmul(
                pb[4 + s2][:, 0:256], xn(k)[:, s2 * 128:(s2 + 1) * 128], B["vw"][:, k * 256:(k + 1) * 256],
                start=(k == 0), stop=(k == DT - 1)),
                waits=w, inc="pe" if k == DT - 1 else None)
        t_mm = kb.tok("pe")
        t1 = kb.op("vector", lambda E, s2=s2: E.tensor_copy(
            out=B["vT_bf"][:, s2 * 256:(s2 + 1) * 256], in_=pb[4 + s2][:, 0:256]),
            waits=[t_mm], inc="dve")
        kb.pbtok[4 + s2] = t1
    for s2 in range(2):
        w = [t_xn, kb.pbtok.get(6 + s2)]
        for k in range(DT):
            kb.op("tensor", lambda E, s2=s2, k=k: E.matmul(
                pb[6 + s2][:, 0:256], xn(k)[:, s2 * 128:(s2 + 1) * 128], B["kw"][:, k * 256:(k + 1) * 256],
                start=(k == 0), stop=(k == DT - 1)),
                waits=w, inc="pe" if k == DT - 1 else None)
        t_mm = kb.tok("pe")
        t1 = kb.op("vector", lambda E, s2=s2: E.tensor_scalar(
            out=B["kT_bf"][:, s2 * 256:(s2 + 1) * 256], in0=pb[6 + s2][:, 0:256],
            scalar1=B["dpows_b"][:, s2:s2 + 1], scalar2=None, op0=ALU.mult),
            waits=[t_mm], inc="dve")
        kb.pbtok[6 + s2] = t1
    t_kv = kb.tok("dve")

    for s2 in range(2):
        w = [t_kv, kb.pbtok.get(s2)]
        for k2 in range(2):
            kb.op("tensor", lambda E, s2=s2, k2=k2: E.matmul(
                pb[s2][:, 0:LA], B["k_bf"][:, k2 * LA + s2 * 128: k2 * LA + (s2 + 1) * 128],
                B["q_bf"][:, k2 * LA:(k2 + 1) * LA],
                start=(k2 == 0), stop=(k2 == 1)),
                waits=w, inc="pe" if k2 == 1 else None)
        t_mm = kb.tok("pe")
        t1 = kb.op("vector", lambda E, s2=s2: E.tensor_tensor(
            out=B["sc_bf"][:, s2 * LA:(s2 + 1) * LA], in0=pb[s2][:, 0:LA],
            in1=B["maskT"][:, s2 * LA:(s2 + 1) * LA], op=ALU.mult), waits=[t_mm], inc="dve")
        kb.pbtok[s2] = t1
    t_sc = kb.tok("dve")

    for m2 in range(2):
        w = [t_sc, tSbf, kb.pbtok.get(2 + m2)]
        for s2 in range(2):
            kb.op("tensor", lambda E, m2=m2, s2=s2: E.matmul(
                pb[2 + m2][:, 0:LA], B["vT_bf"][:, s2 * 256 + m2 * 128: s2 * 256 + (m2 + 1) * 128],
                B["sc_bf"][:, s2 * LA:(s2 + 1) * LA],
                start=(s2 == 0), stop=False), waits=w)
        for k2 in range(2):
            kb.op("tensor", lambda E, m2=m2, k2=k2: E.matmul(
                pb[2 + m2][:, 0:LA], B["S_bf"][:, k2 * 256 + m2 * 128: k2 * 256 + (m2 + 1) * 128],
                B["qt_bf"][:, k2 * LA:(k2 + 1) * LA],
                start=False, stop=(k2 == 1)),
                waits=w, inc="pe" if k2 == 1 else None)
        t_mm = kb.tok("pe")
        t1 = kb.op("vector", lambda E, m2=m2: E.tensor_copy(
            out=B["y_bf"][:, m2 * LA:(m2 + 1) * LA], in_=pb[2 + m2][:, 0:LA]), waits=[t_mm], inc="dve")
        kb.pbtok[2 + m2] = t1
    t_y = kb.tok("dve")

    for m2 in range(2):
        w = [t_kv, kb.pbtok.get(4 + m2), tSbf]
        for s2 in range(2):
            kb.op("tensor", lambda E, m2=m2, s2=s2: E.matmul(
                pb[4 + m2][:, 0:256], B["kT_bf"][:, s2 * 256 + m2 * 128: s2 * 256 + (m2 + 1) * 128],
                B["vT_bf"][:, s2 * 256:(s2 + 1) * 256],
                start=(s2 == 0), stop=(s2 == 1)),
                waits=w, inc="pe" if s2 == 1 else None)
        t_mm = kb.tok("pe")
        tu = kb.op("vector", lambda E, m2=m2: E.scalar_tensor_tensor(
            out=B["S_after"][:, m2 * 256:(m2 + 1) * 256], in0=B["S_after"][:, m2 * 256:(m2 + 1) * 256],
            scalar=B["dL_b"][:, 0:1], in1=pb[4 + m2][:, 0:256], op0=ALU.mult, op1=ALU.add),
            waits=[t_mm, tSbf], inc="dve")
        kb.pbtok[4 + m2] = tu
    B["tok_S"] = kb.tok("dve")

    for m in range(DT):
        bank = 4 + m % 4
        half = (m // 4) * 256
        w = [t_y, kb.pbtok.get(bank)]
        for k2 in range(2):
            kb.op("tensor", lambda E, m=m, k2=k2, bank=bank, half=half: E.matmul(
                pb[bank][:, half:half + LA],
                B["owT"][:, k2 * 1024 + m * 128: k2 * 1024 + (m + 1) * 128],
                B["y_bf"][:, k2 * LA:(k2 + 1) * LA],
                start=(k2 == 0), stop=(k2 == 1)),
                waits=w, inc="pe" if k2 == 1 else None)
        t_mm = kb.tok("pe")
        tr = kb.op("vector", lambda E, m=m, bank=bank, half=half: E.tensor_tensor(
            out=rs(m), in0=pb[bank][:, half:half + LA], in1=rs(m), op=ALU.add),
            waits=[t_mm, tok_resid], inc="dve")
        kb.pbtok[bank] = tr
    t_done = kb.tok("dve")
    t_st = None
    for i in range(DT):
        t_st = kb.store(resid_out[i * 128:(i + 1) * 128, c * LA:(c + 1) * LA], rs(i),
                        waits=[t_done])
    return t_st


def emit_attn0_corr(kb, B, c, resid_io, tok_prev, t_geff):
    pb = B["pb"]
    rs = lambda i: B["resid"][:, i * LA:(i + 1) * LA]
    tok_resid = None
    for i in range(DT):
        tok_resid = kb.load(rs(i), resid_io[i * 128:(i + 1) * 128, c * LA:(c + 1) * LA],
                            waits=[tok_prev, t_geff])
    for m2 in range(2):
        w = [t_geff, kb.pbtok.get(m2)]
        for k2 in range(2):
            kb.op("tensor", lambda E, m2=m2, k2=k2, c=c: E.matmul(
                pb[m2][:, 0:LA], B["geff"][:, k2 * 256 + m2 * 128: k2 * 256 + (m2 + 1) * 128],
                B["qd"][:, k2 * TL + c * LA: k2 * TL + (c + 1) * LA],
                start=(k2 == 0), stop=(k2 == 1)),
                waits=w, inc="pe" if k2 == 1 else None)
        t_mm = kb.tok("pe")
        t1 = kb.op("vector", lambda E, m2=m2: E.tensor_copy(
            out=B["y_bf"][:, m2 * LA:(m2 + 1) * LA], in_=pb[m2][:, 0:LA]), waits=[t_mm], inc="dve")
        kb.pbtok[m2] = t1
    t_y = kb.tok("dve")
    for m in range(DT):
        bank = 4 + m % 4
        half = (m // 4) * 256
        w = [t_y, kb.pbtok.get(bank)]
        for k2 in range(2):
            kb.op("tensor", lambda E, m=m, k2=k2, bank=bank, half=half: E.matmul(
                pb[bank][:, half:half + LA],
                B["owT"][:, k2 * 1024 + m * 128: k2 * 1024 + (m + 1) * 128],
                B["y_bf"][:, k2 * LA:(k2 + 1) * LA],
                start=(k2 == 0), stop=(k2 == 1)),
                waits=w, inc="pe" if k2 == 1 else None)
        t_mm = kb.tok("pe")
        tr = kb.op("vector", lambda E, m=m, bank=bank, half=half: E.tensor_tensor(
            out=rs(m), in0=pb[bank][:, half:half + LA], in1=rs(m), op=ALU.add),
            waits=[t_mm, tok_resid], inc="dve")
        kb.pbtok[bank] = tr
    t_done = kb.tok("dve")
    t_st = None
    for i in range(DT):
        t_st = kb.store(resid_io[i * 128:(i + 1) * 128, c * LA:(c + 1) * LA], rs(i),
                        waits=[t_done])
    return t_st


def declare_attn0_ins(nc):
    return {
        "basisT": nc.dram_tensor("basisT", [128, D], F32, kind="ExternalInput").ap(),
        "qcT": nc.dram_tensor("qcT", [128, 256], F32, kind="ExternalInput").ap(),
        "kcT": nc.dram_tensor("kcT", [128, 256], F32, kind="ExternalInput").ap(),
        "vcT": nc.dram_tensor("vcT", [128, 256], F32, kind="ExternalInput").ap(),
        "ocT": nc.dram_tensor("ocT", [128, 256], F32, kind="ExternalInput").ap(),
        "dlg": nc.dram_tensor("dlg", [1, 1], F32, kind="ExternalInput").ap(),
        "osc": nc.dram_tensor("osc", [1, 1], F32, kind="ExternalInput").ap(),
        "ET": nc.dram_tensor("ET", [256, 256], F32, kind="ExternalInput").ap(),
        "C01": nc.dram_tensor("C01", [256, 256], F32, kind="ExternalInput").ap(),
        "rowq": nc.dram_tensor("rowq", [1, 256], F32, kind="ExternalInput").ap(),
        "colE": nc.dram_tensor("colE", [128, 2], F32, kind="ExternalInput").ap(),
        "jE": nc.dram_tensor("jE", [128, 16], F32, kind="ExternalInput").ap(),
        "isfirst": nc.dram_tensor("isfirst", [128, 1], F32, kind="ExternalInput").ap(),
    }


def emit_attn0_all(kb, ctx, nc, B0, A, resid_in, resid_out, tok_start, tag="_a0"):
    g_in = nc.dram_tensor("g_in" + tag, [256, 256], F32)
    g_out = nc.dram_tensor("g_out" + tag, [256, 256], F32)
    with ExitStack() as s:
        al = lambda n, sh, d=F32: s.enter_context(nc.sbuf_tensor(n + tag, sh, d))
        pb7 = s.enter_context(nc.psum_tensor("pb7" + tag, [128, 512], F32))
        B = dict(B0)
        B["pb"] = list(B0["pb"]) + [pb7]
        B["basisT"] = al("basisT_t", [128, D])
        B["qcT"] = al("qcT_t", [128, 256]); B["kcT"] = al("kcT_t", [128, 256])
        B["vcT"] = al("vcT_t", [128, 256]); B["ocT"] = al("ocT_t", [128, 256])
        B["dlg"] = al("dlg_t", [1, 1]); B["osc"] = al("osc_t", [1, 1])
        B["ET"] = al("ET_t", [128, 2 * LA]); B["C01"] = al("C01_t", [128, 2 * LA])
        B["rowq"] = al("rowq_t", [1, LA]); B["colE"] = al("colE_t", [128, 2])
        B["jE"] = al("jE_t", [128, NCA]); B["isf"] = al("isf_t", [128, 1])
        B["dsc"] = al("dsc_t", [1, 4])
        B["lnd_b"] = al("lnd_b", [128, 1]); B["os_b"] = al("os_b", [128, 1])
        B["dL_b"] = al("dL_b", [128, 1]); B["dLpows_b"] = al("dLpows_b", [128, NCA])
        B["dpows_b"] = al("dpows_b", [128, 2]); B["dqrow"] = al("dqrow", [1, LA])
        B["dqrow_b"] = al("dqrow_b", [128, LA]); B["maskT"] = al("maskT", [128, 2 * LA])
        B["qw"] = al("qw", [128, DT * 256], BF16); B["kw"] = al("kw", [128, DT * 256], BF16)
        B["vw"] = al("vw", [128, DT * 256], BF16); B["owT"] = al("owT", [128, 2 * 1024], BF16)
        B["S_after"] = al("S_after", [128, 2 * 256])
        B["S_bf"] = al("S_bf", [128, 2 * 256], BF16)
        B["geff"] = al("geff", [128, 2 * 256], BF16)
        B["q_bf"] = al("q_bf", [128, 2 * LA], BF16); B["qt_bf"] = al("qt_bf", [128, 2 * LA], BF16)
        B["k_bf"] = al("k_bf", [128, 2 * LA], BF16)
        B["kT_bf"] = al("kT_bf", [128, 2 * 256], BF16); B["vT_bf"] = al("vT_bf", [128, 2 * 256], BF16)
        B["sc_bf"] = al("sc_bf", [128, 2 * LA], BF16); B["y_bf"] = al("y_bf", [128, 2 * LA], BF16)
        B["qd"] = al("qd", [128, 2 * TL], BF16)
        B["xn"] = al("xn_attn", [128, DT * LA], BF16)
        B["xn_ap"] = lambda i, B=B: B["xn"][:, i * LA:(i + 1) * LA]
        B["xn_free"] = None
        B["cb_xmm"] = None

        for name, ap in (("basisT", A["basisT"]), ("qcT", A["qcT"]), ("kcT", A["kcT"]),
                         ("vcT", A["vcT"]), ("ocT", A["ocT"]), ("dlg", A["dlg"]),
                         ("osc", A["osc"]), ("rowq", A["rowq"]), ("colE", A["colE"]),
                         ("jE", A["jE"]), ("isf", A["isfirst"])):
            kb.load(B[name][0:ap.shape[0], 0:ap.shape[1]], ap, waits=[tok_start])
        for j in range(2):
            kb.load(B["ET"][:, j * LA:(j + 1) * LA], A["ET"][j * 128:(j + 1) * 128, :],
                    waits=[tok_start])
            kb.load(B["C01"][:, j * LA:(j + 1) * LA], A["C01"][j * 128:(j + 1) * 128, :],
                    waits=[tok_start])
        B["tok_in"] = kb.tok("ld")

        emit_attn0_prelude(kb, B)
        B["tok_S"] = B["tok_pre"]

        t_st = tok_start
        for c in range(NCA - 1, -1, -1):
            t_st = emit_attn0_chunk(kb, B, c, resid_in, resid_out, t_st)

        tS = B["tok_S"]
        t_g1 = kb.store(g_in[0:128, :], B["S_after"][:, 0:256], waits=[tS])
        t_g2 = kb.store(g_in[128:256, :], B["S_after"][:, 256:512], waits=[tS])
        t_cc = kb.op("gpsimd", lambda E, g_in=g_in, g_out=g_out: E.collective_compute(
            "AllReduce", ALU.add, replica_groups=[[0, 1], [2, 3], [4, 5], [6, 7]],
            ins=[g_in.ap()], outs=[g_out.ap()]), waits=[t_g1, t_g2], inc="cc", inc_n=1)
        t_l1 = kb.load(B["ET"][:, 0:256], g_out[0:128, :], waits=[t_cc])
        t_l2 = kb.load(B["ET"][:, 256:512], g_out[128:256, :], waits=[t_cc])
        for m2 in range(2):
            ta = kb.op("vector", lambda E, m2=m2, B=B: E.tensor_tensor(
                out=B["ET"][:, m2 * 256:(m2 + 1) * 256], in0=B["ET"][:, m2 * 256:(m2 + 1) * 256],
                in1=B["S_after"][:, m2 * 256:(m2 + 1) * 256], op=ALU.subtract),
                waits=[t_l1, t_l2], inc="dve")
            tb = kb.op("vector", lambda E, m2=m2, B=B: E.tensor_scalar(
                out=B["geff"][:, m2 * 256:(m2 + 1) * 256], in0=B["ET"][:, m2 * 256:(m2 + 1) * 256],
                scalar1=B["isf"][:, 0:1], scalar2=None, op0=ALU.mult),
                waits=[ta], inc="dve")
        t_geff = kb.tok("dve")

        for c in range(NCA):
            t_st = emit_attn0_corr(kb, B, c, resid_out, t_st, t_geff)
        B0["stats_free"] = B.get("stats_free")
        return t_st


def build_attn0():
    nc = bass.Bass("TRN2", target_bir_lowering=False)
    resid_in = nc.dram_tensor("resid", [D, TL], F32, kind="ExternalInput").ap()
    A = declare_attn0_ins(nc)
    resid_out = nc.dram_tensor("resid_out", [D, TL], F32, kind="ExternalOutput").ap()
    with ExitStack() as ctx:
        kb = KB(nc, ctx)
        B0 = {"nc": nc, "npb": 7}
        alloc_common(kb, ctx, nc, B0)
        t_st = emit_attn0_all(kb, ctx, nc, B0, A, resid_in, resid_out, None)
        kb.op("gpsimd", lambda E: E.engine_nop(), waits=[t_st])
        block = ctx.enter_context(nc.Block())
        kb.finalize(block)
    return nc


def run_attn0(resids, w, is_first_flags=None):
    L = 256
    sidx = np.arange(L)[:, None]
    tidx = np.arange(L)[None, :]
    ET = np.maximum(sidx - tidx - 1, 0).astype(np.float32)
    C01 = (sidx > tidx).astype(np.float32)
    rowq = (L - 1 - np.arange(L, dtype=np.float32))[None, :]
    colE = (np.arange(128, dtype=np.float32)[:, None] + 128.0 * np.arange(2, dtype=np.float32)[None, :])
    jE = 256.0 * np.broadcast_to(np.arange(NCA, dtype=np.float32)[None, :], (128, NCA))
    common = {
        "basisT": np.ascontiguousarray(np.asarray(w["basis"]).T),
        "qcT": np.ascontiguousarray(np.asarray(w["q_coeffs"]).T),
        "kcT": np.ascontiguousarray(np.asarray(w["k_coeffs"]).T),
        "vcT": np.ascontiguousarray(np.asarray(w["v_coeffs"]).T),
        "ocT": np.ascontiguousarray(np.asarray(w["o_coeffs"]).T),
        "dlg": np.asarray(w["decay_logit"], np.float32).reshape(1, 1),
        "osc": np.asarray(w["out_scale"], np.float32).reshape(1, 1),
        "ET": ET, "C01": C01, "rowq": np.ascontiguousarray(rowq),
        "colE": np.ascontiguousarray(colE), "jE": np.ascontiguousarray(jE),
    }
    nc = _get_nc("attn0", build_attn0)
    in_maps = []
    for i, r in enumerate(resids):
        isf = np.full((128, 1), 1.0 if (i % 2 == 0) else 0.0, np.float32)
        in_maps.append(dict(common, resid=np.ascontiguousarray(r), isfirst=isf))
    res = run_bass_kernel_spmd(nc, in_maps, core_ids=list(range(NCORES)))
    return [res.results[i]["resid_out"] for i in range(NCORES)]


DI = 1024
DIT = DI // 128
TFULL = 2 * TL
NCT = TFULL // TC
NST = 16

def emit_mamba(kb, ctx, nc, B, scr, resid_in, resid_out, tok_start=None):
    pb = B["pb"]
    PAIRS = [[0, 1], [2, 3], [4, 5], [6, 7]]
    sb = lambda n, s, d=F32: None

    with ExitStack() as s1:
        al = lambda n, s, d=F32: s1.enter_context(nc.sbuf_tensor(n, s, d))
        B["xn"] = al("xn_m", [128, DT * TC], BF16)
        B["xn_ap"] = lambda i: B["xn"][:, i * TC:(i + 1) * TC]
        B["xnf"] = al("xnf", [128, DT * TC], BF16)
        B["xpre"] = al("xpre", [128, 2 * DIT * 515])
        B["xpre_free"] = [None] * (2 * DIT)
        B["xpre_halo"] = [None] * DIT
        B["resb"] = al("resb", [128, DIT * TC])
        B["xcb"] = al("xcb", [128, DIT * TC])
        B["xcb_bf"] = al("xcb_bf", [128, DIT * TC], BF16)
        B["ct1"] = al("ct1", [128, TC]); B["ct2"] = al("ct2", [128, TC]); B["ct3"] = al("ct3", [128, TC])
        B["xdb"] = al("xdb", [128, TC])
        B["xpw"] = al("xpwt", [128, DIT * 96], BF16)
        B["wbuf"] = al("wbuf_m", [128, 3 * 4096], BF16)
        B["wslots"] = 3
        B["wcols"] = 4096
        B["wslot_tok"] = [B["tok_prep"]] * 6

        with ExitStack() as s2:
            tmp = s2.enter_context(nc.sbuf_tensor("preptmp2", [128, 96], F32))
            tok = B.get("tok_start")
            for k in range(DIT):
                tl = kb.load(tmp[:, 0:96], B["xpw_dram"][k * 128:(k + 1) * 128, :], waits=[tok])
                tok = kb.op("scalar", lambda E, k=k, tmp=tmp: E.copy(out=B["xpw"][:, k * 96:(k + 1) * 96],
                                                            in_=tmp[:, 0:96]), waits=[tl], inc="act")
            B["tok_xpw"] = tok

        rs = lambda i: B["resid"][:, i * TC:(i + 1) * TC]
        xn = B["xn_ap"]
        t_st = tok_start
        for c in range(NCH):
            tok_resid = None
            for i in range(DT):
                tok_resid = kb.load(rs(i), resid_in[i * 128:(i + 1) * 128, c * TC:(c + 1) * TC],
                                    waits=[t_st])
            t_xn = emit_ln(kb, B, rs, tok_resid)
            for i in range(DT):
                t_st = kb.store(scr["xn_stage"][i * 128:(i + 1) * 128, c * TC:(c + 1) * TC], xn(i),
                                waits=[t_xn])
            B["xn_free"] = t_st
        t_ag = kb.op("gpsimd", lambda E: E.collective_compute(
            "AllGather", ALU.bypass, replica_groups=PAIRS,
            ins=[scr["xn_stage"]], outs=[scr["ag_out"]]), waits=[t_st], inc="cc", inc_n=1)

        xpre = lambda p, i: B["xpre"][:, (p * DIT + i) * 515: (p * DIT + i) * 515 + 515]
        for i in range(DIT):
            kb.op("vector", lambda E, i=i: E.memset(B["xpre"][:, i * 515: i * 515 + 3], 0.0),
                  inc="dve" if i == DIT - 1 else None)
        t_xc_st = None
        t_res_st = None
        for ct in range(NCT):
            p = ct % 2
            blk = 0 if ct < NCH else 1024
            cc0 = (ct % NCH) * TC
            t_ld = None
            for i in range(DT):
                t_ld = kb.load(B["xnf"][:, i * TC:(i + 1) * TC],
                               scr["ag_out"][blk + i * 128: blk + (i + 1) * 128, cc0:cc0 + TC],
                               waits=[t_ag, B.get("xnf_free")])
            def ev_ip(m, bank, w, p=p):
                if m < DIT:
                    t = kb.op("vector", lambda E, m=m, bank=bank, p=p: E.tensor_copy(
                        out=xpre(p, m)[:, 3:515], in_=bank[:, 0:TC]),
                        waits=w + [B["xpre_free"][p * DIT + m]], inc="dve")
                else:
                    t = kb.op("vector", lambda E, m=m, bank=bank: E.tensor_copy(
                        out=B["resb"][:, (m - DIT) * TC:(m - DIT + 1) * TC], in_=bank[:, 0:TC]),
                        waits=w + [B.get("resb_free")], inc="dve")
                return t
            emit_mm_stage(kb, B, scr["ipw"], DT, 2 * DIT, lambda k: B["xnf"][:, k * TC:(k + 1) * TC],
                          ev_ip, t_ld, mgs=8, kg=4)
            B["xnf_free"] = kb.tok("pe")
            t_ip = kb.tok("dve")
            for i in range(DIT):
                t_res_st = kb.store(scr["resg"][i * 128:(i + 1) * 128, ct * TC:(ct + 1) * TC],
                                    B["resb"][:, i * TC:(i + 1) * TC], waits=[t_ip])
            B["resb_free"] = t_res_st
            t_xca = None
            for i in range(DIT):
                cw = lambda tap, i=i: B["cwc"][:, tap * DIT + i: tap * DIT + i + 1]
                t1 = kb.op("vector", lambda E, i=i, p=p, cw=cw: E.tensor_scalar(
                    out=B["ct1"][:, 0:TC], in0=xpre(p, i)[:, 0:TC], scalar1=cw(0), scalar2=None,
                    op0=ALU.mult), waits=[t_ip, B.get("ct_free")], inc="dve")
                t2 = kb.op("vector", lambda E, i=i, p=p, cw=cw: E.tensor_scalar(
                    out=B["ct2"][:, 0:TC], in0=xpre(p, i)[:, 1:1 + TC], scalar1=cw(1), scalar2=None,
                    op0=ALU.mult), waits=[t_ip], inc="dve")
                t3 = kb.op("vector", lambda E, i=i, p=p, cw=cw: E.tensor_scalar(
                    out=B["ct3"][:, 0:TC], in0=xpre(p, i)[:, 2:2 + TC], scalar1=cw(2), scalar2=None,
                    op0=ALU.mult), waits=[t_ip], inc="dve")
                t4 = kb.op("vector", lambda E, i=i, p=p, cw=cw: E.scalar_tensor_tensor(
                    out=B["ct1"][:, 0:TC], in0=xpre(p, i)[:, 3:3 + TC], scalar=cw(3),
                    in1=B["ct1"][:, 0:TC], op0=ALU.mult, op1=ALU.add), waits=[t1], inc="dve")
                t5 = kb.op("vector", lambda E: E.tensor_tensor(
                    out=B["ct2"][:, 0:TC], in0=B["ct2"][:, 0:TC], in1=B["ct3"][:, 0:TC], op=ALU.add),
                    waits=[t2, t3], inc="dve")
                t6 = kb.op("vector", lambda E: E.tensor_tensor(
                    out=B["ct1"][:, 0:TC], in0=B["ct1"][:, 0:TC], in1=B["ct2"][:, 0:TC], op=ALU.add),
                    waits=[t4, t5], inc="dve")
                t7 = kb.op("scalar", lambda E, i=i: E.activation(
                    out=B["xcb"][:, i * TC:(i + 1) * TC], in_=B["ct1"][:, 0:TC], func=AF.Silu,
                    bias=B["cbc"][:, i:i + 1]), waits=[t6, B.get("xcb_free")], inc="act")
                t8 = kb.op("scalar", lambda E, i=i: E.copy(
                    out=B["xcb_bf"][:, i * TC:(i + 1) * TC], in_=B["xcb"][:, i * TC:(i + 1) * TC]),
                    waits=[t7, B.get("xcb_bf_free")], inc="act")
                B["ct_free"] = t7
                t_xca = t8
                th = kb.op("vector", lambda E, i=i, p=p: E.tensor_copy(
                    out=xpre(1 - p, i)[:, 0:3], in_=xpre(p, i)[:, 512:515]),
                    waits=[t_ip, B["xpre_free"][(1 - p) * DIT + i]], inc="dve")
                B["xpre_halo"][i] = th
            for i in range(DIT):
                t_xc_st = kb.store(scr["xconv"][i * 128:(i + 1) * 128, ct * TC:(ct + 1) * TC],
                                   B["xcb"][:, i * TC:(i + 1) * TC], waits=[t_xca])
            B["xcb_free"] = t_xc_st
            w96 = [t_xca, kb.pbtok.get(0), B["tok_xpw"]]
            for k in range(DIT):
                kb.op("tensor", lambda E, k=k: E.matmul(
                    pb[0][0:96, 0:TC], B["xpw"][:, k * 96:(k + 1) * 96],
                    B["xcb_bf"][:, k * TC:(k + 1) * TC],
                    start=(k == 0), stop=(k == DIT - 1)), waits=w96, inc="pe" if k == DIT - 1 else None)
            t_xp = kb.tok("pe")
            B["xcb_bf_free"] = t_xp
            t_ev = kb.op("vector", lambda E: E.tensor_copy(out=B["xdb"][0:96, 0:TC],
                                                           in_=pb[0][0:96, 0:TC]),
                         waits=[t_xp, B.get("xdb_free")], inc="dve")
            kb.pbtok[0] = t_ev
            B["xdb_free"] = kb.store(scr["xdbl_part"][0:96, ct * TC:(ct + 1) * TC],
                                     B["xdb"][0:96, 0:TC], waits=[t_ev])
            for i in range(DIT):
                B["xpre_free"][p * DIT + i] = B["xpre_halo"][i]
        t_xd_st = kb.tok("st")
        t_xc_st_last = kb.tok("st")

    t_ar1 = kb.op("gpsimd", lambda E: E.collective_compute(
        "AllReduce", ALU.add, replica_groups=PAIRS,
        ins=[scr["xdbl_part"]], outs=[scr["xdbl_sum"]]), waits=[t_xd_st], inc="cc", inc_n=1)

    with ExitStack() as s3:
        al = lambda n, s, d=F32: s3.enter_context(nc.sbuf_tensor(n, s, d))
        B["dpw"] = al("dpwt", [64, DI], BF16)
        B["opw"] = al("opwt", [128, DIT * 1024], BF16)
        B["dtb"] = al("dtbt", [64, TC], BF16)
        B["dt32"] = al("dt32t", [64, TC])
        B["BCc"] = al("BCct", [32, TC])
        B["delta"] = al("deltat", [128, DIT * TC])
        B["du"] = al("dut", [128, DIT * TC])
        B["u32"] = al("u32t", [128, DIT * TC])
        B["res32"] = al("res32t", [128, DIT * TC])
        B["yb"] = al("ybt", [128, DIT * TC])
        B["ygb"] = al("ygbt", [128, DIT * TC], BF16)
        B["pob"] = al("pobt", [128, DT * TC])
        B["Bb"] = al("Bbt", [128, TC]); B["Cb"] = al("Cbt", [128, TC])
        B["dA"] = al("dAt", [128, TC]); B["dBu"] = al("dBut", [128, TC])
        B["hseq"] = al("hseqt", [128, TC])
        B["ct1"] = al("ct1b", [128, TC]); B["ct2"] = al("ct2b", [128, TC]); B["ct3"] = al("ct3b", [128, TC])

        with ExitStack() as s4:
            tmp = s4.enter_context(nc.sbuf_tensor("preptmp3", [128, 1024], F32))
            tl = kb.load(tmp[0:64, 0:DI], B["dpw_dram"], waits=[t_xc_st_last])
            tok = kb.op("scalar", lambda E, tmp=tmp: E.copy(out=B["dpw"][:], in_=tmp[0:64, 0:DI]),
                        waits=[tl], inc="act")
            for k in range(DIT):
                tl = kb.load(tmp[:, 0:1024], B["opw_dram"][k * 128:(k + 1) * 128, :], waits=[tok])
                tok = kb.op("scalar", lambda E, k=k, tmp=tmp: E.copy(
                    out=B["opw"][:, k * 1024:(k + 1) * 1024], in_=tmp[:, 0:1024]),
                    waits=[tl], inc="act")
            B["tok_w6"] = tok

        ta = kb.op("scalar", lambda E: E.activation(out=B["aneg"][:], in_=B["alogc"][:], func=AF.Exp),
                   waits=[B["tok_in"]], inc="act")
        t_aneg = kb.op("vector", lambda E: E.tensor_scalar(out=B["aneg"][:], in0=B["aneg"][:],
                                                           scalar1=-1.0, scalar2=None, op0=ALU.mult),
                       waits=[ta], inc="dve")
        kb.op("vector", lambda E: E.memset(B["carries"][:], 0.0), inc="dve")
        t_car = kb.tok("dve")

        t_po_st = None
        for ct in range(NCT):
            tl = kb.load(B["dt32"][0:64, 0:TC], scr["xdbl_sum"][0:64, ct * TC:(ct + 1) * TC],
                         waits=[t_ar1, B.get("dt32_free")])
            t_dt = kb.op("scalar", lambda E: E.copy(out=B["dtb"][0:64, 0:TC], in_=B["dt32"][0:64, 0:TC]),
                         waits=[tl, B.get("dtb_free")], inc="act")
            B["dt32_free"] = t_dt
            t_bcl = kb.load(B["BCc"][0:32, 0:TC], scr["xdbl_sum"][64:96, ct * TC:(ct + 1) * TC],
                            waits=[t_ar1, B.get("BCc_free")])
            bank_waits = [kb.pbtok.get(j) for j in range(8)]
            for m in range(DIT):
                kb.op("tensor", lambda E, m=m: E.matmul(
                    pb[m][:, 0:TC], B["dpw"][0:64, m * 128:(m + 1) * 128], B["dtb"][0:64, 0:TC],
                    start=True, stop=True),
                    waits=[t_dt, B["tok_w6"]] + bank_waits, inc="pe" if m == DIT - 1 else None)
            t_mm = kb.tok("pe")
            B["dtb_free"] = t_mm
            for m in range(DIT):
                dsl = B["delta"][:, m * TC:(m + 1) * TC]
                q0 = kb.op("vector", lambda E, m=m, dsl=dsl: E.tensor_scalar(
                    out=dsl, in0=pb[m][:, 0:TC], scalar1=B["dpbc"][:, m:m + 1], scalar2=None,
                    op0=ALU.add), waits=[t_mm, B.get("delta_free"), B["tok_in"]], inc="dve")
                q1 = kb.op("scalar", lambda E, dsl=dsl: E.activation(
                    out=B["ct1"][:, 0:TC], in_=dsl, func=AF.Abs),
                    waits=[q0, B.get("ct_free6")], inc="act")
                q2 = kb.op("scalar", lambda E: E.activation(out=B["ct2"][:, 0:TC], in_=B["ct1"][:, 0:TC],
                                                            func=AF.Exp, scale=-1.0), waits=[q1], inc="act")
                q3 = kb.op("scalar", lambda E: E.activation(out=B["ct1"][:, 0:TC], in_=B["ct2"][:, 0:TC],
                                                            func=AF.Ln, bias=1.0), waits=[q2], inc="act")
                q4 = kb.op("vector", lambda E, dsl=dsl: E.tensor_scalar(
                    out=B["ct2"][:, 0:TC], in0=dsl, scalar1=0.0, scalar2=None, op0=ALU.max),
                    waits=[q3], inc="dve")
                q5 = kb.op("vector", lambda E, dsl=dsl: E.tensor_tensor(
                    out=dsl, in0=B["ct1"][:, 0:TC], in1=B["ct2"][:, 0:TC], op=ALU.add),
                    waits=[q4], inc="dve")
                B["ct_free6"] = q5
                kb.pbtok[m] = q5
            t_delta = kb.tok("dve")
            t_u = None
            for i in range(DIT):
                t_u = kb.load(B["u32"][:, i * TC:(i + 1) * TC],
                              scr["xconv"][i * 128:(i + 1) * 128, ct * TC:(ct + 1) * TC],
                              waits=[t_xc_st_last, B.get("u32_free")])
            t_r = None
            for i in range(DIT):
                t_r = kb.load(B["res32"][:, i * TC:(i + 1) * TC],
                              scr["resg"][i * 128:(i + 1) * 128, ct * TC:(ct + 1) * TC],
                              waits=[t_xc_st_last, B.get("res32_free")])
            t_du = None
            for i in range(DIT):
                t_du = kb.op("vector", lambda E, i=i: E.tensor_tensor(
                    out=B["du"][:, i * TC:(i + 1) * TC], in0=B["delta"][:, i * TC:(i + 1) * TC],
                    in1=B["u32"][:, i * TC:(i + 1) * TC], op=ALU.mult),
                    waits=[t_delta, t_u, B.get("du_free")], inc="dve")
            for n in range(NST):
                bb = n % 2
                tb1 = kb.op("tensor", lambda E, n=n, bb=bb: E.matmul(
                    pb[bb][:, 0:TC], B["oh32"][0:32, n * 128:(n + 1) * 128], B["BCc"][0:32, 0:TC],
                    start=True, stop=True),
                    waits=[t_bcl, B["tok_in"], kb.pbtok.get(bb)], inc="pe")
                tb2 = kb.op("vector", lambda E, bb=bb: E.tensor_copy(out=B["Bb"][:, 0:TC],
                                                                     in_=pb[bb][:, 0:TC]),
                            waits=[tb1, B.get("Bb_free")], inc="dve")
                tb3 = kb.op("tensor", lambda E, n=n, bb=bb: E.matmul(
                    pb[bb][:, 0:TC], B["oh32"][0:32, (16 + n) * 128:(16 + n + 1) * 128],
                    B["BCc"][0:32, 0:TC], start=True, stop=True),
                    waits=[tb2], inc="pe")
                tb4 = kb.op("vector", lambda E, bb=bb: E.tensor_copy(out=B["Cb"][:, 0:TC],
                                                                     in_=pb[bb][:, 0:TC]),
                            waits=[tb3, B.get("Cb_free")], inc="dve")
                kb.pbtok[bb] = tb4
                for i in range(DIT):
                    idx = i * NST + n
                    tda = kb.op("scalar", lambda E, i=i, idx=idx: E.activation(
                        out=B["dA"][:, 0:TC], in_=B["delta"][:, i * TC:(i + 1) * TC], func=AF.Exp,
                        scale=B["aneg"][:, idx:idx + 1]),
                        waits=[t_delta, t_aneg, B.get("dA_free")], inc="act")
                    tdb = kb.op("vector", lambda E, i=i: E.tensor_tensor(
                        out=B["dBu"][:, 0:TC], in0=B["du"][:, i * TC:(i + 1) * TC],
                        in1=B["Bb"][:, 0:TC], op=ALU.mult),
                        waits=[t_du, tb2, B.get("dBu_free")], inc="dve")
                    tsc = kb.op("vector", lambda E, idx=idx: E.tensor_tensor_scan(
                        out=B["hseq"][:, 0:TC], data0=B["dA"][:, 0:TC], data1=B["dBu"][:, 0:TC],
                        initial=B["carries"][:, idx:idx + 1], op0=ALU.mult, op1=ALU.add),
                        waits=[tda, tdb, t_car, B.get("hseq_free")], inc="dve")
                    tcar = kb.op("vector", lambda E, idx=idx: E.tensor_copy(
                        out=B["carries"][:, idx:idx + 1], in_=B["hseq"][:, TC - 1:TC]),
                        waits=[tsc], inc="dve")
                    B["dA_free"] = tsc
                    B["dBu_free"] = tsc
                    if n == 0:
                        ty = kb.op("vector", lambda E, i=i: E.tensor_tensor(
                            out=B["yb"][:, i * TC:(i + 1) * TC], in0=B["hseq"][:, 0:TC],
                            in1=B["Cb"][:, 0:TC], op=ALU.mult),
                            waits=[tsc, tb4, B.get("yb_free")], inc="dve")
                    else:
                        tt = kb.op("vector", lambda E: E.tensor_tensor(
                            out=B["ct3"][:, 0:TC], in0=B["hseq"][:, 0:TC], in1=B["Cb"][:, 0:TC],
                            op=ALU.mult), waits=[tsc, tb4], inc="dve")
                        ty = kb.op("vector", lambda E, i=i: E.tensor_tensor(
                            out=B["yb"][:, i * TC:(i + 1) * TC], in0=B["yb"][:, i * TC:(i + 1) * TC],
                            in1=B["ct3"][:, 0:TC], op=ALU.add), waits=[tt], inc="dve")
                    B["hseq_free"] = ty
                B["Bb_free"] = kb.tok("dve")
                B["Cb_free"] = B["Bb_free"]
            t_car = kb.tok("dve")
            B["delta_free"] = t_car
            B["du_free"] = t_car
            B["BCc_free"] = t_car
            t_yg = None
            for i in range(DIT):
                ta2 = kb.op("vector", lambda E, i=i: E.scalar_tensor_tensor(
                    out=B["yb"][:, i * TC:(i + 1) * TC], in0=B["u32"][:, i * TC:(i + 1) * TC],
                    scalar=B["dskc"][:, i:i + 1], in1=B["yb"][:, i * TC:(i + 1) * TC],
                    op0=ALU.mult, op1=ALU.add), waits=[t_car, B["tok_in"]], inc="dve")
                tsr = kb.op("scalar", lambda E, i=i: E.activation(
                    out=B["ct1"][:, 0:TC] if i % 2 == 0 else B["ct2"][:, 0:TC],
                    in_=B["res32"][:, i * TC:(i + 1) * TC], func=AF.Silu),
                    waits=[t_r, B.get("srp_free_%d" % (i % 2))], inc="act")
                t_yg = kb.op("vector", lambda E, i=i: E.tensor_tensor(
                    out=B["ygb"][:, i * TC:(i + 1) * TC], in0=B["yb"][:, i * TC:(i + 1) * TC],
                    in1=B["ct1"][:, 0:TC] if i % 2 == 0 else B["ct2"][:, 0:TC], op=ALU.mult),
                    waits=[ta2, tsr, B.get("ygb_free")], inc="dve")
                B["srp_free_%d" % (i % 2)] = t_yg
            B["u32_free"] = t_yg
            B["res32_free"] = t_yg
            B["yb_free"] = t_yg
            B["ct_free6"] = t_yg
            bank_waits = [kb.pbtok.get(j) for j in range(8)]
            for m in range(DT):
                for k in range(DIT):
                    kb.op("tensor", lambda E, m=m, k=k: E.matmul(
                        pb[m][:, 0:TC], B["opw"][:, k * 1024 + m * 128: k * 1024 + (m + 1) * 128],
                        B["ygb"][:, k * TC:(k + 1) * TC],
                        start=(k == 0), stop=(k == DIT - 1)),
                        waits=[t_yg, B["tok_w6"]] + (bank_waits if k == 0 else []),
                        inc="pe" if k == DIT - 1 else None)
            t_pmm = kb.tok("pe")
            t_poev = None
            for m in range(DT):
                t_poev = kb.op("vector", lambda E, m=m: E.tensor_copy(
                    out=B["pob"][:, m * TC:(m + 1) * TC], in_=pb[m][:, 0:TC]),
                    waits=[t_pmm, B.get("pob_free")], inc="dve")
                kb.pbtok[m] = t_poev
            B["ygb_free"] = kb.tok("pe")
            t_po = None
            for m in range(DT):
                t_po = kb.store(scr["pout"][m * 128:(m + 1) * 128, ct * TC:(ct + 1) * TC],
                                B["pob"][:, m * TC:(m + 1) * TC], waits=[t_poev])
            B["pob_free"] = t_po
            t_po_st = t_po

    t_ar2 = kb.op("gpsimd", lambda E: E.collective_compute(
        "AllReduce", ALU.add, replica_groups=PAIRS,
        ins=[scr["pout"]], outs=[scr["pout_sum"]]), waits=[t_po_st], inc="cc", inc_n=1)

    with ExitStack() as s5:
        al = lambda n, s, d=F32: s5.enter_context(nc.sbuf_tensor(n, s, d))
        pa = al("pa", [128, DT * TC])
        pb2 = al("pob2", [128, DT * TC])
        rs = lambda i: B["resid"][:, i * TC:(i + 1) * TC]
        t_st2 = None
        pa_free = None
        for c in range(NCH):
            tok_resid = None
            for i in range(DT):
                tok_resid = kb.load(rs(i), resid_in[i * 128:(i + 1) * 128, c * TC:(c + 1) * TC],
                                    waits=[t_st2, t_ar2])
            tpa = None
            for i in range(DT):
                tpa = kb.load(pa[:, i * TC:(i + 1) * TC],
                              scr["pout_sum"][i * 128:(i + 1) * 128, c * TC:(c + 1) * TC],
                              waits=[t_ar2, pa_free])
            tpb = None
            for i in range(DT):
                tpb = kb.load(pb2[:, i * TC:(i + 1) * TC],
                              scr["pout_sum"][i * 128:(i + 1) * 128, TL + c * TC: TL + (c + 1) * TC],
                              waits=[t_ar2, pa_free])
            t_up = None
            for i in range(DT):
                t1 = kb.op("vector", lambda E, i=i: E.scalar_tensor_tensor(
                    out=rs(i), in0=pa[:, i * TC:(i + 1) * TC], scalar=B["isf"][:, 0:1],
                    in1=rs(i), op0=ALU.mult, op1=ALU.add),
                    waits=[tok_resid, tpa, B["tok_in"]], inc="dve")
                t_up = kb.op("vector", lambda E, i=i: E.scalar_tensor_tensor(
                    out=rs(i), in0=pb2[:, i * TC:(i + 1) * TC], scalar=B["iss"][:, 0:1],
                    in1=rs(i), op0=ALU.mult, op1=ALU.add), waits=[t1, tpb], inc="dve")
            pa_free = t_up
            for i in range(DT):
                t_st2 = kb.store(resid_out[i * 128:(i + 1) * 128, c * TC:(c + 1) * TC], rs(i),
                                 waits=[t_up])
    return t_st2


def declare_mamba_ins(nc):
    return {
        "ipw": nc.dram_tensor("ipw", [D, 2 * DI], F32, kind="ExternalInput").ap(),
        "cwc": nc.dram_tensor("cwc", [128, 4 * DIT], F32, kind="ExternalInput").ap(),
        "cbc": nc.dram_tensor("cbc", [128, DIT], F32, kind="ExternalInput").ap(),
        "xpw": nc.dram_tensor("xpw", [DI, 96], F32, kind="ExternalInput").ap(),
        "dpw": nc.dram_tensor("dpw", [64, DI], F32, kind="ExternalInput").ap(),
        "dpbc": nc.dram_tensor("dpbc", [128, DIT], F32, kind="ExternalInput").ap(),
        "alogc": nc.dram_tensor("alogc", [128, DIT * NST], F32, kind="ExternalInput").ap(),
        "dskc": nc.dram_tensor("dskc", [128, DIT], F32, kind="ExternalInput").ap(),
        "opw": nc.dram_tensor("opw", [DI, D], F32, kind="ExternalInput").ap(),
        "oh32": nc.dram_tensor("oh32", [32, 32 * 128], F32, kind="ExternalInput").ap(),
        "issecond": nc.dram_tensor("issecond", [128, 1], F32, kind="ExternalInput").ap(),
    }


def declare_mamba_scr(nc):
    return {
        "xn_stage": nc.dram_tensor("xn_stage", [D, TL], BF16).ap(),
        "ag_out": nc.dram_tensor("ag_out", [2 * D, TL], BF16).ap(),
        "xdbl_part": nc.dram_tensor("xdbl_part", [96, TFULL], F32).ap(),
        "xdbl_sum": nc.dram_tensor("xdbl_sum", [96, TFULL], F32).ap(),
        "xconv": nc.dram_tensor("xconv", [DI, TFULL], F32).ap(),
        "resg": nc.dram_tensor("resg", [DI, TFULL], F32).ap(),
        "pout": nc.dram_tensor("pout", [D, TFULL], F32).ap(),
        "pout_sum": nc.dram_tensor("pout_sum", [D, TFULL], F32).ap(),
        "ipw": nc.dram_tensor("ipw_bf", [D, 2 * DI], BF16).ap(),
    }


def emit_mamba_all(kb, ctx, nc, B0, A, A_isfirst, scr, resid_in, resid_out, tok_start, tag="_mb"):
    with ExitStack() as s:
        al = lambda n, sh, d=F32: s.enter_context(nc.sbuf_tensor(n + tag, sh, d))
        pb7 = s.enter_context(nc.psum_tensor("pb7" + tag, [128, 512], F32))
        B = dict(B0)
        B["pb"] = list(B0["pb"]) + [pb7]
        B["cwc"] = al("cwct", [128, 4 * DIT]); B["cbc"] = al("cbct", [128, DIT])
        B["dpbc"] = al("dpbct", [128, DIT]); B["alogc"] = al("alogct", [128, DIT * NST])
        B["dskc"] = al("dskct", [128, DIT]); B["isf"] = al("isft", [128, 1])
        B["iss"] = al("isst", [128, 1])
        B["oh32"] = al("oh32t", [32, 32 * 128])
        B["aneg"] = al("anegt", [128, DIT * NST])
        B["carries"] = al("carriest", [128, DIT * NST])
        B["xpw_dram"] = A["xpw"]
        B["dpw_dram"] = A["dpw"]
        B["opw_dram"] = A["opw"]
        B["xn_free"] = None
        B["cb_xmm"] = None

        B["tok_start"] = tok_start
        for name, ap in (("cwc", A["cwc"]), ("cbc", A["cbc"]), ("dpbc", A["dpbc"]),
                         ("alogc", A["alogc"]), ("dskc", A["dskc"]), ("isf", A_isfirst),
                         ("iss", A["issecond"]), ("oh32", A["oh32"])):
            kb.load(B[name][0:ap.shape[0], 0:ap.shape[1]], ap, waits=[tok_start])
        B["tok_in"] = kb.tok("ld")

        emit_prep(kb, ctx, nc, [(A["ipw"], scr["ipw"])], tok_start=tok_start)
        B["tok_prep"] = kb.tok("st")

        t_end = emit_mamba(kb, ctx, nc, B, scr, resid_in, resid_out, tok_start)
        B0["stats_free"] = B.get("stats_free")
        return t_end


def build_mamba():
    nc = bass.Bass("TRN2", target_bir_lowering=False)
    resid_in = nc.dram_tensor("resid", [D, TL], F32, kind="ExternalInput").ap()
    A = declare_mamba_ins(nc)
    isfirst = nc.dram_tensor("isfirst", [128, 1], F32, kind="ExternalInput").ap()
    resid_out = nc.dram_tensor("resid_out", [D, TL], F32, kind="ExternalOutput").ap()
    scr = declare_mamba_scr(nc)
    with ExitStack() as ctx:
        kb = KB(nc, ctx)
        B0 = {"nc": nc, "npb": 7}
        alloc_common(kb, ctx, nc, B0)
        t_end = emit_mamba_all(kb, ctx, nc, B0, A, isfirst, scr, resid_in, resid_out, None)
        kb.op("gpsimd", lambda E: E.engine_nop(), waits=[t_end])
        block = ctx.enter_context(nc.Block())
        kb.finalize(block)
    return nc


def run_mamba(resids, w):
    w = {k: np.asarray(v) for k, v in w.items()}
    nc = _get_nc("mamba", build_mamba)
    oh32 = np.zeros((32, 32 * 128), np.float32)
    for r in range(32):
        oh32[r, r * 128:(r + 1) * 128] = 1.0
    in_maps = []
    for core in range(NCORES):
        dh = core % 2
        sl = slice(dh * DI, (dh + 1) * DI)
        ipw_x = w["in_proj_w"][:, sl]
        ipw_r = w["in_proj_w"][:, 2048 + dh * DI: 2048 + (dh + 1) * DI]
        cw = w["conv_w"][sl, 0, :]
        cwc = np.concatenate([_col128(np.ascontiguousarray(cw[:, tap])) for tap in range(4)], axis=1)
        alog = w["A_log"][sl]
        alogc = np.ascontiguousarray(alog.reshape(DIT, 128, NST).transpose(1, 0, 2).reshape(128, DIT * NST))
        m = {
            "resid": np.ascontiguousarray(resids[core]),
            "ipw": np.ascontiguousarray(np.concatenate([ipw_x, ipw_r], axis=1)),
            "cwc": cwc,
            "cbc": _col128(w["conv_b"][sl]),
            "xpw": np.ascontiguousarray(w["x_proj_w"][sl]),
            "dpw": np.ascontiguousarray(w["dt_proj_w"][:, sl]),
            "dpbc": _col128(w["dt_proj_b"][sl]),
            "alogc": alogc,
            "dskc": _col128(w["D_skip"][sl]),
            "opw": np.ascontiguousarray(w["out_proj_w"][sl]),
            "oh32": oh32,
            "isfirst": np.full((128, 1), 1.0 if dh == 0 else 0.0, np.float32),
            "issecond": np.full((128, 1), 1.0 if dh == 1 else 0.0, np.float32),
        }
        in_maps.append(m)
    res = run_bass_kernel_spmd(nc, in_maps, core_ids=list(range(NCORES)))
    return [res.results[i]["resid_out"] for i in range(NCORES)]


def build_fused():
    nph = int(os.environ.get("N_PHASES", "6"))
    nc = bass.Bass("TRN2", target_bir_lowering=False)
    resid0 = nc.dram_tensor("resid", [D, TL], F32, kind="ExternalInput").ap()
    A_f = declare_ffn0_ins(nc)
    A_a = declare_attn0_ins(nc)
    A_m = declare_moe_ins(nc)
    A_b = declare_mamba_ins(nc)
    out = nc.dram_tensor("resid_out", [D, TL], F32, kind="ExternalOutput").ap()
    r = [nc.dram_tensor(f"rscr{i}", [D, TL], F32).ap() for i in range(5)]
    scr_f = declare_ffn0_scr(nc)
    scr_m = declare_moe_scr(nc)
    scr_b = declare_mamba_scr(nc)

    with ExitStack() as ctx:
        kb = KB(nc, ctx)
        B0 = {"nc": nc, "npb": 7}
        alloc_common(kb, ctx, nc, B0)
        dsts = [r[0], r[1], r[2], r[3], r[4], out]
        dsts[nph - 1] = out
        t = emit_ffn0_all(kb, ctx, nc, B0, A_f, scr_f, resid0, dsts[0], True, "_f0", None)
        if nph >= 2:
            t = emit_attn0_all(kb, ctx, nc, B0, A_a, dsts[0], dsts[1], t)
        if nph >= 3:
            t = emit_ffn0_all(kb, ctx, nc, B0, A_f, scr_f, dsts[1], dsts[2], False, "_f1", t)
        if nph >= 4:
            t = emit_moe_all(kb, ctx, nc, B0, A_m, scr_m, dsts[2], dsts[3], True, "_m0", t)
        if nph >= 5:
            t = emit_mamba_all(kb, ctx, nc, B0, A_b, A_a["isfirst"], scr_b, dsts[3], dsts[4], t)
        if nph >= 6:
            t = emit_moe_all(kb, ctx, nc, B0, A_m, scr_m, dsts[4], out, False, "_m1", t)
        kb.op("gpsimd", lambda E: E.engine_nop(), waits=[t])
        block = ctx.enter_context(nc.Block())
        kb.finalize(block)
    return nc


def _ffn0_maps(w):
    return {
        "w1t": np.ascontiguousarray(np.asarray(w["w1_w"]).T),
        "w2t": np.ascontiguousarray(np.asarray(w["w2_w"]).T),
        "h1t": np.ascontiguousarray(np.asarray(w["h1_w"]).T),
        "h2t": np.ascontiguousarray(np.asarray(w["h2_w"]).T),
        "w3t": np.ascontiguousarray(np.asarray(w["w3_w"]).T),
        "s1": _col128(w["w1_s"]), "s2": _col128(w["w2_s"]),
        "sh1": _col128(w["h1_s"]), "sh2": _col128(w["h2_s"]),
        "s3": _col128(w["w3_s"]),
    }


def _attn0_maps(w):
    L = 256
    sidx = np.arange(L)[:, None]
    tidx = np.arange(L)[None, :]
    return {
        "basisT": np.ascontiguousarray(np.asarray(w["basis"]).T),
        "qcT": np.ascontiguousarray(np.asarray(w["q_coeffs"]).T),
        "kcT": np.ascontiguousarray(np.asarray(w["k_coeffs"]).T),
        "vcT": np.ascontiguousarray(np.asarray(w["v_coeffs"]).T),
        "ocT": np.ascontiguousarray(np.asarray(w["o_coeffs"]).T),
        "dlg": np.asarray(w["decay_logit"], np.float32).reshape(1, 1),
        "osc": np.asarray(w["out_scale"], np.float32).reshape(1, 1),
        "ET": np.maximum(sidx - tidx - 1, 0).astype(np.float32),
        "C01": (sidx > tidx).astype(np.float32),
        "rowq": np.ascontiguousarray((L - 1 - np.arange(L, dtype=np.float32))[None, :]),
        "colE": np.ascontiguousarray(np.arange(128, dtype=np.float32)[:, None]
                                     + 128.0 * np.arange(2, dtype=np.float32)[None, :]),
        "jE": np.ascontiguousarray(256.0 * np.broadcast_to(
            np.arange(NCA, dtype=np.float32)[None, :], (128, NCA))),
    }


def _moe_maps(w):
    b1 = np.asarray(w["b1"]); b2 = np.asarray(w["b2"])
    return {
        "w1": np.ascontiguousarray(np.asarray(w["w1"]).reshape(NEXP * D, HID)),
        "w2": np.ascontiguousarray(np.asarray(w["w2"]).reshape(NEXP * HID, D)),
        "rw": np.ascontiguousarray(np.asarray(w["router_w"])),
        "rb_bc": np.ascontiguousarray(np.broadcast_to(np.asarray(w["router_b"])[None, :], (128, NEXP))),
        "b1c": np.concatenate([_col128(b1[e]) for e in range(NEXP)], axis=1),
        "b2c": np.concatenate([_col128(b2[e]) for e in range(NEXP)], axis=1),
        "ident128": np.eye(128, dtype=np.float32),
        "onehot8": np.concatenate([np.eye(8, dtype=np.float32)[:, e:e + 1]
                                   * np.ones((1, 128), np.float32) for e in range(NEXP)], axis=1),
    }


def _mamba_maps(w, dh):
    w = {k: np.asarray(v) for k, v in w.items()}
    oh32 = np.zeros((32, 32 * 128), np.float32)
    for rr in range(32):
        oh32[rr, rr * 128:(rr + 1) * 128] = 1.0
    sl = slice(dh * DI, (dh + 1) * DI)
    ipw_x = w["in_proj_w"][:, sl]
    ipw_r = w["in_proj_w"][:, 2048 + dh * DI: 2048 + (dh + 1) * DI]
    cw = w["conv_w"][sl, 0, :]
    alog = w["A_log"][sl]
    return {
        "ipw": np.ascontiguousarray(np.concatenate([ipw_x, ipw_r], axis=1)),
        "cwc": np.concatenate([_col128(np.ascontiguousarray(cw[:, tap])) for tap in range(4)], axis=1),
        "cbc": _col128(w["conv_b"][sl]),
        "xpw": np.ascontiguousarray(w["x_proj_w"][sl]),
        "dpw": np.ascontiguousarray(w["dt_proj_w"][:, sl]),
        "dpbc": _col128(w["dt_proj_b"][sl]),
        "alogc": np.ascontiguousarray(alog.reshape(DIT, 128, NST).transpose(1, 0, 2).reshape(128, DIT * NST)),
        "dskc": _col128(w["D_skip"][sl]),
        "opw": np.ascontiguousarray(w["out_proj_w"][sl]),
        "oh32": oh32,
        "issecond": np.full((128, 1), 1.0 if dh == 1 else 0.0, np.float32),
    }


def _col128(v):
    return np.ascontiguousarray(np.asarray(v).reshape(-1, 128).T)


def _shard_x(x):
    shards = []
    for b in range(4):
        for h in range(2):
            xs = x[b, h * TL:(h + 1) * TL, :]
            shards.append(np.ascontiguousarray(xs.T))
    return shards


_NC_CACHE = {}

def _get_nc(name, builder):
    if name not in _NC_CACHE:
        _NC_CACHE[name] = builder()
    return _NC_CACHE[name]


def run_ffn0(resids, w):
    common = {
        "w1t": np.ascontiguousarray(np.asarray(w["w1_w"]).T),
        "w2t": np.ascontiguousarray(np.asarray(w["w2_w"]).T),
        "h1t": np.ascontiguousarray(np.asarray(w["h1_w"]).T),
        "h2t": np.ascontiguousarray(np.asarray(w["h2_w"]).T),
        "w3t": np.ascontiguousarray(np.asarray(w["w3_w"]).T),
        "s1": _col128(w["w1_s"]), "s2": _col128(w["w2_s"]),
        "sh1": _col128(w["h1_s"]), "sh2": _col128(w["h2_s"]),
        "s3": _col128(w["w3_s"]),
    }
    nc = _get_nc("ffn0", build_ffn0)
    in_maps = [dict(common, resid=np.ascontiguousarray(r)) for r in resids]
    res = run_bass_kernel_spmd(nc, in_maps, core_ids=list(range(NCORES)))
    return [res.results[i]["resid_out"] for i in range(NCORES)]


def run_moe(resids, w):
    w1 = np.ascontiguousarray(np.asarray(w["w1"]).reshape(NEXP * D, HID))
    w2 = np.ascontiguousarray(np.asarray(w["w2"]).reshape(NEXP * HID, D))
    b1 = np.asarray(w["b1"])
    b2 = np.asarray(w["b2"])
    common = {
        "w1": w1, "w2": w2,
        "rw": np.ascontiguousarray(np.asarray(w["router_w"])),
        "rb_bc": np.ascontiguousarray(np.broadcast_to(np.asarray(w["router_b"])[None, :], (128, NEXP))),
        "b1c": np.concatenate([_col128(b1[e]) for e in range(NEXP)], axis=1),
        "b2c": np.concatenate([_col128(b2[e]) for e in range(NEXP)], axis=1),
        "ident128": np.eye(128, dtype=np.float32),
        "onehot8": np.concatenate([np.eye(8, dtype=np.float32)[:, e:e + 1] * np.ones((1, 128), np.float32)
                                   for e in range(NEXP)], axis=1),
    }
    nc = _get_nc("moe", build_moe)
    in_maps = [dict(common, resid=np.ascontiguousarray(r)) for r in resids]
    res = run_bass_kernel_spmd(nc, in_maps, core_ids=list(range(NCORES)))
    return [res.results[i]["resid_out"] for i in range(NCORES)]


def kernel(**inputs):
    x = np.asarray(inputs["L0_attn_x"])
    resids = _shard_x(x)
    w_ffn0 = {k[len("L0_ffn_"):]: np.asarray(v) for k, v in inputs.items() if k.startswith("L0_ffn_")}
    w_attn0 = {k[len("L0_attn_"):]: np.asarray(v) for k, v in inputs.items()
               if k.startswith("L0_attn_") and k != "L0_attn_x"}
    w_moe = {k[len("L1_ffn_"):]: np.asarray(v) for k, v in inputs.items() if k.startswith("L1_ffn_")}
    w_mamba = {k[len("L1_attn_"):]: np.asarray(v) for k, v in inputs.items() if k.startswith("L1_attn_")}

    common = {}
    common.update(_ffn0_maps(w_ffn0))
    common.update(_attn0_maps(w_attn0))
    common.update(_moe_maps(w_moe))
    mam = [_mamba_maps(w_mamba, dh) for dh in range(2)]

    nc = _get_nc("fused", build_fused)
    in_maps = []
    for core in range(NCORES):
        dh = core % 2
        m = dict(common)
        m.update(mam[dh])
        m["resid"] = np.ascontiguousarray(resids[core])
        m["isfirst"] = np.full((128, 1), 1.0 if dh == 0 else 0.0, np.float32)
        in_maps.append(m)
    res = run_bass_kernel_spmd(nc, in_maps, core_ids=list(range(NCORES)))
    outs = [res.results[i]["resid_out"] for i in range(NCORES)]

    out = np.empty((4, 8192, D), np.float32)
    for b in range(4):
        for h in range(2):
            out[b, h * TL:(h + 1) * TL, :] = outs[2 * b + h].T
    return out
